# version 05
# baseline (speedup 1.0000x reference)
import numpy as np
import ml_dtypes
import concourse.bass as bass
import concourse.mybir as mybir
from concourse.bass_utils import run_bass_kernel_spmd

F32 = mybir.dt.float32
BF16 = mybir.dt.bfloat16
AF = mybir.ActivationFunctionType
ALU = mybir.AluOpType
bf16 = ml_dtypes.bfloat16

B, S, D, H, DH = 4, 4096, 1024, 16, 64
DFF, RANK = 4096, 256
NT = S // 128
NQ = 16
P = [8, 16, 24, 32]
SM = [p // 2 for p in P]
NPAIR = sum(SM)
NSTEP = 16 * NPAIR
EXPBIAS = -20.0
KILL = -1.0e30
GQ_ROLE = [[0, 8, 20, 28], [4, 12, 16, 24]]
KB = 1024


def slot2phys(m, l):
    return P[m] - 4 + l if l < 4 else l - 4


def phys_order(role):
    gq = GQ_ROLE[role]
    N = [g + 4 for g in gq]
    phys = [-1] * NT
    placed = set()
    for m in range(4):
        for t in range(4):
            phys[P[m] - 4 + t] = N[m] - 4 + t
            placed.add(N[m] - 4 + t)
    rest = [g for g in range(max(N)) if g not in placed]
    for i in range(NT):
        if phys[i] < 0:
            phys[i] = rest.pop(0) if rest else -1
    return phys, N


def kill_pairs(role):
    phys, N = phys_order(role)
    live = np.zeros(NPAIR, dtype=bool)
    idx = 0
    for m in range(4):
        for sp in range(SM[m]):
            l = 2 * sp
            if l < 4:
                live[idx] = True
            else:
                g = phys[slot2phys(m, l)]
                live[idx] = (g >= 0) and (g <= N[m] - 5)
            idx += 1
    return live


DBG = False
ENGINES = ("sync", "gpsimd", "tensor", "vector", "scalar")


class Sched:
    def __init__(self):
        self.ops = {e: [] for e in ENGINES}
        self.cnt = {}

    def add(self, eng, kind, waits=(), inc=None, incv=1, **kw):
        self.ops[eng].append(dict(kind=kind, waits=[w for w in waits if w],
                                  inc=inc, incv=incv, kw=kw))
        if inc is not None:
            self.cnt[inc] = self.cnt.get(inc, 0) + incv
            return (inc, self.cnt[inc])
        return None

    def now(self, name):
        return (name, self.cnt.get(name, 0))


def build_program():
    nc = bass.Bass("TRN2", target_bir_lowering=False,
                   detect_race_conditions=False)

    xkv = nc.dram_tensor("xkv", [NT * 128, D], F32, kind="ExternalInput").ap()
    xq = nc.dram_tensor("xq", [NQ * 128, D], F32, kind="ExternalInput").ap()
    wts = {n: nc.dram_tensor(n, [D, D], F32, kind="ExternalInput").ap()
           for n in ("wq", "wk", "wv", "wo")}
    u1 = nc.dram_tensor("u1", [D, RANK], F32, kind="ExternalInput").ap()
    v1 = nc.dram_tensor("v1", [RANK, DFF], F32, kind="ExternalInput").ap()
    b1 = nc.dram_tensor("b1", [DFF], F32, kind="ExternalInput").ap()
    u2 = nc.dram_tensor("u2", [DFF, RANK], F32, kind="ExternalInput").ap()
    v2 = nc.dram_tensor("v2", [RANK, D], F32, kind="ExternalInput").ap()
    b2 = nc.dram_tensor("b2", [D], F32, kind="ExternalInput").ap()
    cst = nc.dram_tensor("cst", [128, 8], F32, kind="ExternalInput").ap()
    bkd = nc.dram_tensor("bk", [128, NPAIR], F32, kind="ExternalInput").ap()
    atl = nc.dram_tensor("atl", [128, 7 * 128], BF16, kind="ExternalInput").ap()
    idf_d = nc.dram_tensor("idf", [128, 128], F32, kind="ExternalInput").ap()
    idb_d = nc.dram_tensor("idb", [128, 128], BF16, kind="ExternalInput").ap()
    out_d = nc.dram_tensor("out", [NQ * 128, D], F32, kind="ExternalOutput").ap()
    if DBG:
        dbga = nc.dram_tensor("dbga", [128, 103 * 1024], BF16, kind="ExternalOutput").ap()
        dbgc = nc.dram_tensor("dbgc", [128, 103 * 1024], BF16, kind="ExternalOutput").ap()
        dbgl = nc.dram_tensor("dbgl", [128, 2 * 3 * 1024 + 32], F32, kind="ExternalOutput").ap()

    AR = 206 * KB
    arena = nc.alloc_sbuf_tensor("arena", [128, AR // 2], BF16)
    arA = arena.ap()

    def bfv(off_b, cols):
        assert off_b % 2 == 0 and off_b + 2 * cols <= AR, (off_b, cols)
        return arA[:, off_b // 2: off_b // 2 + cols]

    PSTEP = AR // 2

    def sbap(off_el, dims):
        return bass.AP(arena, off_el, [[PSTEP, 128]] + dims)

    def f32v(off_b, cols):
        assert off_b % 4 == 0 and off_b + 4 * cols <= AR, (off_b, cols)
        return arA[:, off_b // 2: off_b // 2 + 2 * cols].bitcast(F32)

    o_cst, o_bk, o_atl = 0, 64, 256
    o_idb, o_idf = 2048, 2304
    o_b1c, o_b2c, o_rec, o_ln = 2816, 2944, 2976, 3072
    oA = 3200
    oB = oA + 64 * KB
    oC = oB + 65 * KB
    oD = oC + 32 * KB
    TOT = oD + 36 * KB
    assert TOT <= AR, (TOT, AR)

    cstS = f32v(o_cst, 8)
    bkS = f32v(o_bk, NPAIR)
    atlS = bfv(o_atl, 7 * 128)
    idbS = bfv(o_idb, 128)
    idfS = f32v(o_idf, 128)
    b1cS = f32v(o_b1c, 32)
    b2cS = f32v(o_b2c, 8)
    recS = f32v(o_rec, 8)
    lnS = f32v(o_ln, 16)

    KTv = bfv(oA, 32 * KB)
    Vsb = bfv(oB, 33280)
    QTv = bfv(oC, 16 * KB)
    SCR = [bfv(oD + i * 2 * KB, KB) for i in range(2)]
    XTG = [bfv(oD + 4 * KB, 8 * KB), bfv(oD + 20 * KB, 8 * KB)]
    ESB = bfv(oD, 2 * KB)
    ATOK = bfv(oD + 4 * KB, 16 * KB)
    XTOK = [f32v(oD, KB), f32v(oD + 4 * KB, KB)]
    YTOK = [f32v(oD + 8 * KB, KB), f32v(oD + 12 * KB, KB)]
    SQ = f32v(oD + 16 * KB, KB)
    Y1B = [bfv(oD + 20 * KB, KB), bfv(oD + 22 * KB, KB)]
    OTOK = [f32v(oD + 24 * KB, KB), f32v(oD + 28 * KB, KB)]
    PT = bfv(oD + 32 * KB, KB)
    Q2T = bfv(oD + 34 * KB, KB)
    ATT = bfv(oA, 16 * KB)
    WoT = bfv(oA + 32 * KB, 8 * KB)
    YAT = f32v(oA + 48 * KB, 4 * KB)
    U1B = bfv(oA, 2 * KB)
    V1B = bfv(oA + 4 * KB, 8 * KB)
    V2B = bfv(oA + 20 * KB, 2 * KB)
    U2Bb = bfv(oA + 24 * KB, 4 * KB)
    FFT = f32v(oA + 32 * KB, 4 * KB)
    HTr = bfv(oA + 48 * KB, 2 * KB)
    U2Ba = bfv(oD, 4 * KB)
    XTG = bfv(oD + 4 * KB, 8 * KB)
    WT = bfv(oD + 20 * KB, 8 * KB)
    ATOK = bfv(oD + 4 * KB, 16 * KB)
    ESB = bfv(oD, 2 * KB)
    Y1 = f32v(oB, 16 * KB)
    Y1T = bfv(oC, 16 * KB)

    ps = nc.alloc_psum_tensor("ps", [128, 4096], F32)
    psA = ps.ap()
    psB = psA.bitcast(BF16)

    sch = Sched()
    add, now = sch.add, sch.now

    for (dst, src) in ((cstS, cst[:, :]), (bkS, bkd[:, :]), (atlS, atl[:, :]),
                       (idbS, idb_d[:, :]), (idfS, idf_d[:, :]),
                       (b1cS[:, 0:32], b1.rearrange("(c p) -> p c", p=128)),
                       (b2cS[:, 0:8], b2.rearrange("(c p) -> p c", p=128))):
        add("sync", "dma", inc="c", incv=16, out=dst, in_=src, slow=True)
    t_c = now("c")
    t_ones = add("vector", "memset", inc="ones", ap=Vsb, value=1.0)

    scr_rel = [None, None]

    def wt_build(wsrc, gate_gp=None, gate_tr=None, tag="w"):
        lastt = None
        for j in range(8):
            w = [scr_rel[j % 2]] if scr_rel[j % 2] else []
            if j == 0 and gate_gp:
                w.append(gate_gp)
            ld = add("gpsimd", "dma", waits=w, inc=tag + "l", incv=16,
                     out=SCR[j % 2][:, :], in_=wsrc[j * 128:(j + 1) * 128, :])
            w2 = [ld]
            if j == 0 and gate_tr:
                w2.append(gate_tr)
            for i in range(8):
                lastt = add("sync", "dmaT", waits=w2 if i == 0 else [],
                            inc=tag + "t", incv=16,
                            out=WT[:, i * 1024 + j * 128: i * 1024 + (j + 1) * 128],
                            in_=SCR[j % 2][:, i * 128:(i + 1) * 128])
            scr_rel[j % 2] = lastt
        return now(tag + "t")

    xtg_rel = None

    def x_pass(xsrc, ntile, consume, gate0=None, tag="x"):
        nonlocal xtg_rel
        for grp in range(ntile // 8):
            for t in range(8):
                gt = grp * 8 + t
                w = [scr_rel[gt % 2]] if scr_rel[gt % 2] else []
                if gt == 0 and gate0:
                    w.append(gate0)
                ld = add("gpsimd", "dma", waits=w, inc=tag + "l", incv=16,
                         out=SCR[gt % 2][:, :],
                         in_=xsrc[gt * 128:(gt + 1) * 128, :])
                w2 = [ld]
                if t == 0 and xtg_rel:
                    w2.append(xtg_rel)
                lastt = None
                for i in range(8):
                    lastt = add("sync", "dmaT", waits=w2 if i == 0 else [],
                                inc=tag + "t", incv=16,
                                out=XTG[:, i * 1024 + t * 128: i * 1024 + (t + 1) * 128],
                                in_=SCR[gt % 2][:, i * 128:(i + 1) * 128])
                scr_rel[gt % 2] = lastt
            consume(grp)

    t_wk = wt_build(wts["wk"], gate_gp=None, gate_tr=t_c, tag="wk")
    kbank_rel = [None, None]

    def k_consume(grp):
        nonlocal xtg_rel
        wx = [now("xkt")]
        for j in range(8):
            kset = j % 2
            w = list(wx) + ([t_wk] if (grp == 0 and j == 0) else [])
            if kbank_rel[kset]:
                w.append(kbank_rel[kset])
            last = None
            for i in range(8):
                for hf in range(2):
                    last = add("tensor", "mm",
                               waits=w if (i == 0 and hf == 0) else [],
                               inc="kmm" if (i == 7 and hf == 1) else None,
                               out=psA[:, kset * 1024 + hf * 512:
                                       kset * 1024 + hf * 512 + 512],
                               lhsT=WT[:, i * 1024 + j * 128: i * 1024 + (j + 1) * 128],
                               rhs=XTG[:, i * 1024 + hf * 512: i * 1024 + hf * 512 + 512],
                               start=(i == 0), stop=(i == 7))
            ev = add("scalar", "copy", waits=[now("kmm")], inc="kev",
                     out=KTv[:, j * 4096 + grp * 1024: j * 4096 + (grp + 1) * 1024],
                     in_=psA[:, kset * 1024: kset * 1024 + 1024])
            kbank_rel[kset] = ev
            if j == 7:
                xtg_rel = last

    x_pass(xkv, NT, k_consume, gate0=None, tag="xk")
    t_kmm_all = now("kmm")
    t_kev_all = now("kev")

    t_wv = wt_build(wts["wv"], gate_gp=None, gate_tr=(
        "kmm", sch.cnt["kmm"]), tag="wv")
    vbank_rel = [None, None]

    def v_consume(grp):
        nonlocal xtg_rel
        wx = [now("xvt")]
        for t in range(8):
            gt = grp * 8 + t
            vset = t % 2
            w = list(wx) + ([t_wv] if (grp == 0 and t == 0) else [])
            if vbank_rel[vset]:
                w.append(vbank_rel[vset])
            last = None
            for i in range(8):
                for hf in range(2):
                    last = add("tensor", "mm",
                               waits=w if (i == 0 and hf == 0) else [],
                               inc="vmm" if (i == 7 and hf == 1) else None,
                               out=psA[:, 2048 + vset * 1024 + hf * 512:
                                       2048 + vset * 1024 + hf * 512 + 512],
                               lhsT=XTG[:, i * 1024 + t * 128: i * 1024 + (t + 1) * 128],
                               rhs=WT[:, i * 1024 + hf * 512: i * 1024 + hf * 512 + 512],
                               start=(i == 0), stop=(i == 7))
            dst = sbap(oB // 2 + gt * 1040, [[65, 16], [1, 64]])
            src = psA[:, 2048 + vset * 1024: 2048 + vset * 1024 + 1024].rearrange(
                "p (h d) -> p h d", h=16)
            ev = add("scalar", "copy",
                     waits=[now("vmm")] + ([t_ones] if (grp == 0 and t == 0) else []),
                     inc="vev", out=dst, in_=src)
            vbank_rel[vset] = ev
            if t == 7:
                xtg_rel = last

    x_pass(xkv, NT, v_consume, gate0=None, tag="xv")
    t_vev_all = now("vev")

    t_wq = wt_build(wts["wq"], gate_gp=None, gate_tr=(
        "vmm", sch.cnt["vmm"]), tag="wq")
    qbank_rel = [kbank_rel[0], kbank_rel[1]]

    def q_consume(grp):
        nonlocal xtg_rel
        wx = [now("xqt")]
        for j in range(8):
            qset = j % 2
            w = list(wx) + ([t_wq] if (grp == 0 and j == 0) else [])
            if qbank_rel[qset]:
                w.append(qbank_rel[qset])
            last = None
            for i in range(8):
                for hf in range(2):
                    last = add("tensor", "mm",
                               waits=w if (i == 0 and hf == 0) else [],
                               inc="qmm" if (i == 7 and hf == 1) else None,
                               out=psA[:, qset * 1024 + hf * 512:
                                       qset * 1024 + hf * 512 + 512],
                               lhsT=WT[:, i * 1024 + j * 128: i * 1024 + (j + 1) * 128],
                               rhs=XTG[:, i * 1024 + hf * 512: i * 1024 + hf * 512 + 512],
                               start=(i == 0), stop=(i == 7))
            ev = add("scalar", "copy", waits=[now("qmm")], inc="qev",
                     out=QTv[:, j * 2048 + grp * 1024: j * 2048 + (grp + 1) * 1024],
                     in_=psA[:, qset * 1024: qset * 1024 + 1024])
            qbank_rel[qset] = ev
            if j == 7:
                xtg_rel = last

    x_pass(xq, NQ, q_consume, gate0=None, tag="xq")
    t_qev_all = now("qev")

    exp_tok = [None] * NSTEP
    pvd_tok = [None] * NSTEP
    evac_rel = None
    gstep = 0
    first_S = True
    for m in range(4):
        for h in range(16):
            j = h // 2
            hp = 64 * (h % 2)
            pend = None
            for s in range(SM[m]):
                par = gstep % 2
                for li in range(2):
                    l = 2 * s + li
                    pk = slot2phys(m, l)
                    w = []
                    if first_S:
                        w = [t_qev_all, t_kev_all, qbank_rel[0], qbank_rel[1]]
                        first_S = False
                    if gstep >= 2 and li == 0:
                        w.append(("e", gstep - 1))
                    add("tensor", "mm", waits=w, inc="smm" if li == 1 else None,
                        out=psA[:, par * 1024 + li * 512: par * 1024 + li * 512 + 512],
                        lhsT=KTv[hp:hp + 64, j * 4096 + pk * 128: j * 4096 + (pk + 1) * 128],
                        rhs=QTv[hp:hp + 64, j * 2048 + m * 512: j * 2048 + (m + 1) * 512],
                        start=True, stop=True)
                wexp = [now("smm")]
                if gstep >= 2:
                    wexp.append(("pvd", gstep - 1))
                pr = sum(SM[:mm2] for mm2 in []) if False else (sum(SM[:m]) + s)
                exp_tok[gstep] = add("scalar", "act", waits=wexp, inc="e",
                                     out=ESB[:, par * 1024: par * 1024 + 1024],
                                     in_=psA[:, par * 1024: par * 1024 + 1024],
                                     func=AF.Exp, bias=bkS[:, pr:pr + 1],
                                     scale=cstS[:, 0:1])
                if s < 2:
                    for li in range(2):
                        tau = 2 * s + li
                        add("vector", "tt",
                            waits=[exp_tok[gstep]] if li == 0 else [],
                            inc="mk" if li == 1 else None,
                            out=ESB[:, par * 1024 + li * 512: par * 1024 + li * 512 + 512],
                            in0=ESB[:, par * 1024 + li * 512: par * 1024 + li * 512 + 512],
                            in1=atlS[:, (3 - tau) * 128: (3 - tau) * 128 + 512],
                            op=ALU.mult)
                    exp_tok[gstep] = now("mk")
                if pend is not None:
                    emit_pv(m, h, pend, exp_tok, add, now, psA, ESB, Vsb,
                            evac_rel if pend[2] == 0 else None,
                            (t_vev_all, t_ones) if (m == 0 and h == 0 and pend[2] == 0) else None)
                    pvd_tok[pend[3]] = now("pvd")
                pend = (m, h, s, gstep)
                gstep += 1
            emit_pv(m, h, pend, exp_tok, add, now, psA, ESB, Vsb,
                    evac_rel if pend[2] == 0 else None,
                    (t_vev_all, t_ones) if (m == 0 and h == 0 and pend[2] == 0) else None)
            pvd_tok[pend[3]] = now("pvd")
            sums_ap = psA[:, 2048:4096].rearrange("p (t q) -> p t q", t=4)[:, :, 64:65]
            rec_ap = recS[:, 0:4].rearrange("p (t o) -> p t o", o=1)
            rc = add("vector", "recip", waits=[pvd_tok[pend[3]]], inc="dve",
                     out=rec_ap, in_=sums_ap)
            add("vector", "wait", waits=[rc])
            for tau in range(4):
                add("vector", "ts", inc="aev" if tau == 3 else None,
                    out=ATOK[:, (4 * m + tau) * 1024 + 64 * h:
                             (4 * m + tau) * 1024 + 64 * h + 64],
                    in0=psA[:, 2048 + tau * 512: 2048 + tau * 512 + 64],
                    s1=recS[:, tau:tau + 1], s2=None, op0=ALU.mult, op1=None)
            evac_rel = now("aev")
    t_exp_all = now("e")
    t_aev_all = now("aev")
    t_smm_all = now("smm")
    t_dbga = None
    if DBG:
        t_dbga = add("sync", "dma", waits=[t_aev_all], inc="dba", incv=16,
                     out=dbga[:, :], in_=arA[:, 0:103 * 1024])

    tr4_rel = [None, None]
    for t in range(NQ):
        st_ = t % 2
        w = [t_aev_all, t_exp_all] if t == 0 else []
        if tr4_rel[st_]:
            w = w + [tr4_rel[st_]]
        trl = None
        for cch in range(8):
            trl = add("tensor", "tr", waits=w if cch == 0 else [], inc="tr4",
                      out=psB[:, st_ * 2048 + cch * 128: st_ * 2048 + (cch + 1) * 128],
                      in_=ATOK[:, t * 1024 + cch * 128: t * 1024 + (cch + 1) * 128],
                      identity=idbS[:, :])
        dst = sbap(oA // 2 + t * 128, [[2048, 8], [1, 128]])
        srcap = psB[:, st_ * 2048: st_ * 2048 + 1024].rearrange(
            "p (c q) -> p c q", c=8)
        ev = add("scalar", "copy",
                 waits=[trl, t_smm_all if t == 0 else None,
                        t_dbga if (DBG and t == 0) else None],
                 inc="atev", out=dst, in_=srcap)
        tr4_rel[st_] = ev
    t_atev_all = now("atev")

    WT_PH4 = WoT
    wo_last = None
    for j in range(8):
        w = [scr_rel[j % 2]] if scr_rel[j % 2] else []
        if j < 2:
            w.append(("pvd", NSTEP))
        ld = add("gpsimd", "dma", waits=w, inc="wol", incv=16,
                 out=SCR[j % 2][:, :], in_=wts["wo"][j * 128:(j + 1) * 128, :])
        w2 = [ld]
        if j == 0:
            w2.append(t_atev_all)
        for i in range(8):
            wo_last = add("sync", "dmaT", waits=w2 if i == 0 else [],
                          inc="wot", incv=16,
                          out=WT_PH4[:, i * 1024 + j * 128: i * 1024 + (j + 1) * 128],
                          in_=SCR[j % 2][:, i * 128:(i + 1) * 128])
        scr_rel[j % 2] = wo_last
    t_wo = now("wot")

    ffn_gate = None
    ffn_loads = []

    ln_ysum, ln_qsum = lnS[:, 0:1], lnS[:, 1:2]
    ln_nm, ln_ex2 = lnS[:, 2:3], lnS[:, 3:4]
    ln_m2, ln_vpe = lnS[:, 4:5], lnS[:, 5:6]
    ln_rv, ln_inv = lnS[:, 6:7], lnS[:, 7:8]
    ln_nmi = lnS[:, 8:9]

    def ln_chain(par, resid_in1, apply_out, extra_w=()):
        t1 = add("vector", "stt", waits=list(extra_w), inc="dve",
                 out=YTOK[par][:, :],
                 in0=psA[:, 2048 + par * 1024: 2048 + par * 1024 + 1024],
                 s1=cstS[:, 3:4], in1=resid_in1, op0=ALU.mult, op1=ALU.add,
                 accum_out=ln_ysum)
        t2 = add("scalar", "act", waits=[t1], inc="sq", out=SQ[:, :],
                 in_=YTOK[par][:, :], func=AF.Square, accum_out=ln_qsum)
        e = add("vector", "ts", waits=[t2], inc="dve", out=ln_nm, in0=ln_ysum,
                s1=cstS[:, 4:5], s2=None, op0=ALU.mult, op1=None)
        e2 = add("vector", "ts", inc="dve", out=ln_ex2, in0=ln_qsum,
                 s1=cstS[:, 1:2], s2=None, op0=ALU.mult, op1=None)
        add("vector", "wait", waits=[e2])
        e = add("vector", "tt", inc="dve", out=ln_m2, in0=ln_nm, in1=ln_nm,
                op=ALU.mult)
        add("vector", "wait", waits=[e])
        e = add("vector", "tt", inc="dve", out=ln_vpe, in0=ln_ex2, in1=ln_m2,
                op=ALU.subtract)
        add("vector", "wait", waits=[e])
        e = add("vector", "ts", inc="dve", out=ln_vpe, in0=ln_vpe,
                s1=cstS[:, 2:3], s2=None, op0=ALU.add, op1=None)
        add("vector", "wait", waits=[e])
        t3 = add("vector", "recip", inc="dve", out=ln_rv, in_=ln_vpe)
        t4 = add("scalar", "act", waits=[t3], inc="sq", out=ln_inv, in_=ln_rv,
                 func=AF.Sqrt)
        t5 = add("vector", "tt", waits=[t4], inc="dve", out=ln_nmi, in0=ln_nm,
                 in1=ln_inv, op=ALU.mult)
        add("vector", "wait", waits=[t5])
        ap1 = add("vector", "ts", inc="dve", out=apply_out, in0=YTOK[par][:, :],
                  s1=ln_inv, s2=ln_nmi, op0=ALU.mult, op1=ALU.add)
        return t1, ap1

    yev_rel = [None, None]
    t2_rel = [None, None]
    xtok_rel = [None, None]
    ytok_rel = [None, None]
    y1b_rel = [None, None]
    xr_tok = [None] * NQ
    last_womm = None
    for g4 in range(4):
        for jj in range(8):
            wset = jj % 2
            w = [t_wo] if (g4 == 0 and jj == 0) else []
            if g4 == 0 and jj == 0:
                w.append(tr4_rel[0])
                w.append(tr4_rel[1])
            if yev_rel[wset]:
                w.append(yev_rel[wset])
            for i in range(8):
                last_womm = add("tensor", "mm", waits=w if i == 0 else [],
                                inc="womm" if i == 7 else None,
                                out=psA[:, wset * 512: wset * 512 + 512],
                                lhsT=WT_PH4[:, i * 1024 + jj * 128: i * 1024 + (jj + 1) * 128],
                                rhs=ATT[:, i * 2048 + g4 * 512: i * 2048 + (g4 + 1) * 512],
                                start=(i == 0), stop=(i == 7))
            ev = add("scalar", "copy",
                     waits=[now("womm")] + ([("t2", 32 * g4)] if g4 > 0 else []),
                     inc="yev",
                     out=YAT[:, jj * 512:(jj + 1) * 512],
                     in_=psA[:, wset * 512: wset * 512 + 512])
            yev_rel[wset] = ev
        t_ya = now("yev")
        for tau in range(4):
            tloc = g4 * 4 + tau
            par = tloc % 2
            w = [t_atev_all, t_wo] if tloc < 2 else []
            if xtok_rel[par]:
                w.append(xtok_rel[par])
            xr_tok[tloc] = add("sync", "dma", waits=w, inc="xr", incv=16,
                               out=XTOK[par][:, :],
                               in_=xq[tloc * 128:(tloc + 1) * 128, :])
            w = [t_ya]
            if t2_rel[par]:
                w.append(t2_rel[par])
            if tloc == 0:
                w.append(evac_rel)
            trl = None
            for cch in range(8):
                trl = add("tensor", "tr", waits=w if cch == 0 else [], inc="t2",
                          out=psA[:, 2048 + par * 1024 + cch * 128:
                                  2048 + par * 1024 + (cch + 1) * 128],
                          in_=YAT[:, cch * 512 + tau * 128: cch * 512 + (tau + 1) * 128],
                          identity=idfS[:, :])
            extra = [trl, xr_tok[tloc]]
            if ytok_rel[par]:
                extra.append(ytok_rel[par])
            t1, ap1 = ln_chain(par, XTOK[par][:, :],
                               Y1[:, tloc * 1024:(tloc + 1) * 1024], extra)
            t2_rel[par] = t1
            xtok_rel[par] = t1
            if DBG and tloc < 2:
                off = tloc * (3 * 1024 + 16)
                dsn = add("sync", "dma", waits=[ap1], inc="dbl", incv=16,
                          out=dbgl[:, off:off + 16], in_=lnS[:, 0:16])
                dsn = add("sync", "dma", inc="dbl", incv=16,
                          out=dbgl[:, off + 16:off + 1040], in_=XTOK[par][:, :])
                dsn = add("sync", "dma", inc="dbl", incv=16,
                          out=dbgl[:, off + 1040:off + 2064], in_=YTOK[par][:, :])
                dsn = add("sync", "dma", inc="dbl", incv=16,
                          out=dbgl[:, off + 2064:off + 3088],
                          in_=Y1[:, tloc * 1024:(tloc + 1) * 1024])
                sch.dbg_gate = dsn
            ap2 = add("scalar", "copy",
                      waits=[ap1] + ([sch.dbg_gate] if (DBG and tloc < 2) else []),
                      inc="y1c",
                      out=Y1B[par][:, :],
                      in_=Y1[:, tloc * 1024:(tloc + 1) * 1024])
            ytok_rel[par] = ap2
            w = [ap2] + ([y1b_rel[par]] if y1b_rel[par] else [])
            lastd = None
            for cch in range(8):
                lastd = add("scalar", "dmaT", waits=w if cch == 0 else [],
                            inc="y1t", incv=16,
                            out=Y1T[:, cch * 2048 + tloc * 128: cch * 2048 + (tloc + 1) * 128],
                            in_=Y1B[par][:, cch * 128:(cch + 1) * 128])
            y1b_rel[par] = lastd
    t_y1t = now("y1t")
    t_womm_all = now("womm")
    t_t2_all = now("t2")

    t_dbgc = None
    if DBG:
        t_dbgc = add("sync", "dma", waits=[t_womm_all, t_t2_all, ("y1c", 16),
                                           now("y1t")],
                     inc="dbc", incv=16, out=dbgc[:, :], in_=arA[:, 0:103 * 1024])
    w = [t_womm_all, t_t2_all, ("y1c", 16)]
    if DBG:
        w.append(t_dbgc)
    f_tok = None
    for i in range(8):
        f_tok = add("gpsimd", "dma", waits=w if i == 0 else [], inc="fw",
                    incv=16, out=U1B[:, i * 256:(i + 1) * 256],
                    in_=u1[i * 128:(i + 1) * 128, :])
    for r in range(2):
        f_tok = add("gpsimd", "dma", inc="fw", incv=16,
                    out=V1B[:, r * 4096:(r + 1) * 4096],
                    in_=v1[r * 128:(r + 1) * 128, :])
    for r in range(2):
        f_tok = add("gpsimd", "dma", inc="fw", incv=16,
                    out=V2B[:, r * 1024:(r + 1) * 1024],
                    in_=v2[r * 128:(r + 1) * 128, :])
    for i in range(32):
        dstw = U2Ba if i < 16 else U2Bb
        f_tok = add("gpsimd", "dma", inc="fw", incv=16,
                    out=dstw[:, (i % 16) * 256:(i % 16 + 1) * 256],
                    in_=u2[i * 128:(i + 1) * 128, :])
    t_fw = now("fw")

    def u2slice(fc, r):
        t_ = U2Ba if fc < 16 else U2Bb
        return t_[:, (fc % 16) * 256 + r * 128: (fc % 16) * 256 + (r + 1) * 128]

    pbank_rel = [None, None]
    hbank_rel = [None, None]
    t3_rel = [None, None]
    str_rel = [None, None]
    for g4 in range(4):
        for r in range(2):
            pset = r % 2
            w = [t_fw, t_y1t] if (g4 == 0 and r == 0) else []
            if pbank_rel[pset]:
                w.append(pbank_rel[pset])
            elif g4 == 0:
                w.append(yev_rel[pset])
            for i in range(8):
                add("tensor", "mm", waits=w if i == 0 else [],
                    inc="pmm" if i == 7 else None,
                    out=psA[:, pset * 512: pset * 512 + 512],
                    lhsT=U1B[:, i * 256 + r * 128: i * 256 + (r + 1) * 128],
                    rhs=Y1T[:, i * 2048 + g4 * 512: i * 2048 + (g4 + 1) * 512],
                    start=(i == 0), stop=(i == 7))
            ev = add("scalar", "copy", waits=[now("pmm")], inc="pev",
                     out=PT[:, r * 512:(r + 1) * 512],
                     in_=psA[:, pset * 512: pset * 512 + 512])
            pbank_rel[pset] = ev
        t_pt = now("pev")
        ht_rel = [None] * 4
        for fc in range(32):
            hset = fc % 2
            w = [t_pt] if fc < 2 else []
            if hbank_rel[hset]:
                w.append(hbank_rel[hset])
            for r in range(2):
                add("tensor", "mm", waits=w if r == 0 else [],
                    inc="hmm" if r == 1 else None,
                    out=psA[:, (2 + hset) * 512: (2 + hset) * 512 + 512],
                    lhsT=V1B[:, r * 4096 + fc * 128: r * 4096 + (fc + 1) * 128],
                    rhs=PT[:, r * 512:(r + 1) * 512],
                    start=(r == 0), stop=(r == 1))
            wg = [now("hmm")]
            if ht_rel[fc % 4]:
                wg.append(ht_rel[fc % 4])
            ev = add("scalar", "act", waits=wg, inc="hev",
                     out=HTr[:, (fc % 4) * 512: (fc % 4 + 1) * 512],
                     in_=psA[:, (2 + hset) * 512: (2 + hset) * 512 + 512],
                     func=AF.Gelu_apprx_tanh, bias=b1cS[:, fc:fc + 1],
                     scale=cstS[:, 3:4])
            hbank_rel[hset] = ev
            wq2 = [ev]
            if fc == 0:
                if pbank_rel[0]:
                    wq2.append(pbank_rel[0])
                if pbank_rel[1]:
                    wq2.append(pbank_rel[1])
            last_q2 = None
            for r in range(2):
                last_q2 = add("tensor", "mm", waits=wq2 if r == 0 else [],
                              inc="q2p" if r == 1 else None,
                              out=psA[:, r * 512: r * 512 + 512],
                              lhsT=u2slice(fc, r),
                              rhs=HTr[:, (fc % 4) * 512: (fc % 4 + 1) * 512],
                              start=(fc == 0), stop=(fc == 31))
            ht_rel[fc % 4] = last_q2
        for r in range(2):
            ev = add("scalar", "copy", waits=[now("q2p")], inc="q2ev",
                     out=Q2T[:, r * 512:(r + 1) * 512],
                     in_=psA[:, r * 512: r * 512 + 512])
            pbank_rel[r] = ev
        t_q2 = now("q2ev")
        for jj in range(8):
            oset = jj % 2
            w = [t_q2]
            if hbank_rel[oset]:
                w.append(hbank_rel[oset])
            for r in range(2):
                add("tensor", "mm", waits=w if r == 0 else [],
                    inc="omm" if r == 1 else None,
                    out=psA[:, (2 + oset) * 512: (2 + oset) * 512 + 512],
                    lhsT=V2B[:, r * 1024 + jj * 128: r * 1024 + (jj + 1) * 128],
                    rhs=Q2T[:, r * 512:(r + 1) * 512],
                    start=(r == 0), stop=(r == 1))
            ev = add("vector", "ts",
                     waits=[now("omm")] + ([("t3", 32 * g4)] if g4 > 0 else []),
                     inc="oev",
                     out=FFT[:, jj * 512:(jj + 1) * 512],
                     in0=psA[:, (2 + oset) * 512: (2 + oset) * 512 + 512],
                     s1=b2cS[:, jj:jj + 1], s2=None, op0=ALU.add, op1=None)
            hbank_rel[oset] = ev
        t_ff = now("oev")
        for tau in range(4):
            tloc = g4 * 4 + tau
            par = tloc % 2
            w = [t_ff]
            if t3_rel[par]:
                w.append(t3_rel[par])
            elif g4 == 0:
                w.append(t2_rel[par])
            trl = None
            for cch in range(8):
                trl = add("tensor", "tr", waits=w if cch == 0 else [], inc="t3",
                          out=psA[:, 2048 + par * 1024 + cch * 128:
                                  2048 + par * 1024 + (cch + 1) * 128],
                          in_=FFT[:, cch * 512 + tau * 128: cch * 512 + (tau + 1) * 128],
                          identity=idfS[:, :])
            extra = [trl]
            if ytok_rel[par]:
                extra.append(ytok_rel[par])
            t1, ap1 = ln_chain(par, Y1[:, tloc * 1024:(tloc + 1) * 1024],
                               OTOK[par][:, :],
                               extra + ([str_rel[par]] if str_rel[par] else []))
            t3_rel[par] = t1
            ytok_rel[par] = ap1
            st_tok = add("sync", "dma", waits=[ap1], inc="st", incv=16,
                         out=out_d[tloc * 128:(tloc + 1) * 128, :],
                         in_=OTOK[par][:, :])
            str_rel[par] = st_tok
    add("sync", "wait", waits=[now("st")])

    semnames = sorted(sch.cnt.keys())
    sems = {}
    for n in semnames:
        g = nc.semaphore(n)
        sems[n] = g.__enter__()
        nc._sem_guards = getattr(nc, "_sem_guards", [])
        nc._sem_guards.append(g)

    def run_ops(eng, name):
        waited = {}
        for op in sch.ops[name]:
            for (sname, v) in op["waits"]:
                if v > waited.get(sname, 0):
                    eng.wait_ge(sems[sname], v)
                    waited[sname] = v
            k, kw = op["kind"], op["kw"]
            if k == "wait":
                continue
            if k == "dma":
                ins = eng.dma_start(out=kw["out"], in_=kw["in_"],
                                    allow_slow_non_contiguous=kw.get("slow", False))
            elif k == "dmaT":
                ins = eng.dma_start(out=kw["out"], in_=kw["in_"], transpose=True)
            elif k == "mm":
                ins = eng.matmul(kw["out"], kw["lhsT"], kw["rhs"],
                                 start=kw["start"], stop=kw["stop"])
            elif k == "tr":
                ins = eng.transpose(kw["out"], kw["in_"], kw["identity"])
            elif k == "act":
                a = {}
                for f in ("bias", "scale", "accum_out"):
                    if f in kw and kw[f] is not None:
                        a[f] = kw[f]
                ins = eng.activation(kw["out"], kw["in_"], kw["func"], **a)
            elif k == "copy":
                ins = (eng.copy(kw["out"], kw["in_"]) if name == "scalar"
                       else eng.tensor_copy(kw["out"], kw["in_"]))
            elif k == "tt":
                ins = eng.tensor_tensor(kw["out"], kw["in0"], kw["in1"], kw["op"])
            elif k == "ts":
                if kw.get("s2") is None:
                    ins = eng.tensor_scalar(kw["out"], kw["in0"], kw["s1"],
                                            None, kw["op0"])
                else:
                    ins = eng.tensor_scalar(kw["out"], kw["in0"], kw["s1"],
                                            kw["s2"], kw["op0"], kw["op1"])
            elif k == "stt":
                ins = eng.scalar_tensor_tensor(kw["out"], kw["in0"], kw["s1"],
                                               kw["in1"], kw["op0"], kw["op1"],
                                               accum_out=kw.get("accum_out"))
            elif k == "recip":
                ins = eng.reciprocal(kw["out"], kw["in_"])
            elif k == "memset":
                ins = eng.memset(kw["ap"], kw["value"])
            else:
                raise ValueError(k)
            if op["inc"] is not None:
                ins.then_inc(sems[op["inc"]], op["incv"])

    with nc.Block() as block:
        @block.sync
        def _(e):
            run_ops(e, "sync")

        @block.gpsimd
        def _(e):
            run_ops(e, "gpsimd")

        @block.tensor
        def _(e):
            run_ops(e, "tensor")

        @block.vector
        def _(e):
            run_ops(e, "vector")

        @block.scalar
        def _(e):
            run_ops(e, "scalar")

    return nc


def emit_pv(m, h, pend, exp_tok, add, now, psA, ESB, Vsb, evac_rel, v_gate):
    _, _, s, gs = pend
    par = gs % 2
    gate = exp_tok[gs]
    for li in range(2):
        l = 2 * s + li
        pk = slot2phys(m, l)
        for tau in range(4):
            w = []
            if li == 0 and tau == 0:
                w.append(gate)
                if evac_rel:
                    w.append(evac_rel)
                if v_gate:
                    w.extend(v_gate)
            add("tensor", "mm", waits=w,
                inc="pvd" if (li == 1 and tau == 3) else None,
                out=psA[:, 2048 + tau * 512: 2048 + tau * 512 + 65],
                lhsT=ESB[:, par * 1024 + li * 512 + tau * 128:
                         par * 1024 + li * 512 + (tau + 1) * 128],
                rhs=Vsb[:, pk * 1040 + 65 * h: pk * 1040 + 65 * h + 65],
                start=(l == 0), stop=(l == P[m] - 1))


_CACHE = {}


def kernel(L0_attn_in_features, L0_attn_Wq, L0_attn_Wk, L0_attn_Wv, L0_attn_Wo,
           L0_ffn_U1, L0_ffn_V1, L0_ffn_b1, L0_ffn_U2, L0_ffn_V2, L0_ffn_b2):
    x = np.ascontiguousarray(np.asarray(L0_attn_in_features, dtype=np.float32))
    Ws = {k: np.ascontiguousarray(np.asarray(v, dtype=np.float32))
          for k, v in (("wq", L0_attn_Wq), ("wk", L0_attn_Wk),
                       ("wv", L0_attn_Wv), ("wo", L0_attn_Wo),
                       ("u1", L0_ffn_U1), ("v1", L0_ffn_V1),
                       ("u2", L0_ffn_U2), ("v2", L0_ffn_V2))}
    b1v = np.ascontiguousarray(np.asarray(L0_ffn_b1, dtype=np.float32))
    b2v = np.ascontiguousarray(np.asarray(L0_ffn_b2, dtype=np.float32))

    cstv = np.zeros((128, 8), np.float32)
    cstv[:, 0] = 0.125
    cstv[:, 1] = 1.0 / 1024.0
    cstv[:, 2] = 1.0e-5
    cstv[:, 3] = 1.0
    cstv[:, 4] = -1.0 / 1024.0
    atlv = np.zeros((128, 7 * 128), np.float32)
    kk, qq = np.meshgrid(np.arange(128), np.arange(128), indexing="ij")
    atlv[:, 3 * 128:4 * 128] = (kk <= qq).astype(np.float32)
    atlv[:, 4 * 128:] = 1.0
    atlv = atlv.astype(bf16)
    idfv = np.eye(128, dtype=np.float32)
    idbv = idfv.astype(bf16)

    in_maps, meta = [], []
    for c in range(8):
        seq, role = c // 2, c % 2
        phys, _N = phys_order(role)
        qtiles = [g0 + t for g0 in GQ_ROLE[role] for t in range(4)]
        xs = x[seq]
        xkv_rows = np.concatenate(
            [xs[max(g, 0) * 128: max(g, 0) * 128 + 128] for g in phys], axis=0)
        xq_rows = np.concatenate(
            [xs[g * 128:(g + 1) * 128] for g in qtiles], axis=0)
        live = kill_pairs(role)
        bkv = np.full((128, NPAIR), EXPBIAS, np.float32)
        bkv[:, ~live] = KILL
        im = {"xkv": np.ascontiguousarray(xkv_rows),
              "xq": np.ascontiguousarray(xq_rows),
              "b1": b1v, "b2": b2v, "cst": cstv, "bk": bkv, "atl": atlv,
              "idf": idfv, "idb": idbv}
        im.update(Ws)
        in_maps.append(im)
        meta.append((seq, qtiles))

    if "nc" not in _CACHE:
        _CACHE["nc"] = build_program()
    res = run_bass_kernel_spmd(_CACHE["nc"], in_maps, core_ids=list(range(8)))

    out = np.empty((B, S, D), np.float32)
    for c in range(8):
        seq, qtiles = meta[c]
        oc = np.asarray(res.results[c]["out"])
        for li, g in enumerate(qtiles):
            out[seq, g * 128:(g + 1) * 128] = oc[li * 128:(li + 1) * 128]
    return out


if __name__ == "__main__":
    nc = build_program()
    n = sum(len(v) for v in [])
    print("build ok")


# version 06
# speedup vs baseline: 1.3557x; 1.3557x over previous
import numpy as np
import ml_dtypes
import concourse.bass as bass
import concourse.mybir as mybir
from concourse.bass_utils import run_bass_kernel_spmd

F32 = mybir.dt.float32
BF16 = mybir.dt.bfloat16
AF = mybir.ActivationFunctionType
ALU = mybir.AluOpType
bf16 = ml_dtypes.bfloat16

B, S, D, H, DH = 4, 4096, 1024, 16, 64
DFF, RANK = 4096, 256
NT = S // 128
NQ = 16
P = [8, 16, 24, 32]
SM = [p // 2 for p in P]
NPAIR = sum(SM)
NSTEP = 16 * NPAIR
EXPBIAS = -20.0
KILL = -1.0e30
GQ_ROLE = [[0, 8, 20, 28], [4, 12, 16, 24]]
KB = 1024


def slot2phys(m, l):
    return P[m] - 4 + l if l < 4 else l - 4


def phys_order(role):
    gq = GQ_ROLE[role]
    N = [g + 4 for g in gq]
    phys = [-1] * NT
    placed = set()
    for m in range(4):
        for t in range(4):
            phys[P[m] - 4 + t] = N[m] - 4 + t
            placed.add(N[m] - 4 + t)
    rest = [g for g in range(max(N)) if g not in placed]
    for i in range(NT):
        if phys[i] < 0:
            phys[i] = rest.pop(0) if rest else -1
    return phys, N


def kill_pairs(role):
    phys, N = phys_order(role)
    live = np.zeros(NPAIR, dtype=bool)
    idx = 0
    for m in range(4):
        for sp in range(SM[m]):
            l = 2 * sp
            if l < 4:
                live[idx] = True
            else:
                g = phys[slot2phys(m, l)]
                live[idx] = (g >= 0) and (g <= N[m] - 5)
            idx += 1
    return live


DBG = False
ENGINES = ("sync", "gpsimd", "tensor", "vector", "scalar")


class Sched:
    def __init__(self):
        self.ops = {e: [] for e in ENGINES}
        self.cnt = {}

    def add(self, eng, kind, waits=(), inc=None, incv=1, **kw):
        self.ops[eng].append(dict(kind=kind, waits=[w for w in waits if w],
                                  inc=inc, incv=incv, kw=kw))
        if inc is not None:
            self.cnt[inc] = self.cnt.get(inc, 0) + incv
            return (inc, self.cnt[inc])
        return None

    def now(self, name):
        return (name, self.cnt.get(name, 0))


def build_program():
    nc = bass.Bass("TRN2", target_bir_lowering=False,
                   detect_race_conditions=False)

    xkv = nc.dram_tensor("xkv", [NT * 128, D], F32, kind="ExternalInput").ap()
    xq = nc.dram_tensor("xq", [NQ * 128, D], F32, kind="ExternalInput").ap()
    wts = {n: nc.dram_tensor(n, [D, D], F32, kind="ExternalInput").ap()
           for n in ("wq", "wk", "wv", "wo")}
    u1 = nc.dram_tensor("u1", [D, RANK], F32, kind="ExternalInput").ap()
    v1 = nc.dram_tensor("v1", [RANK, DFF], F32, kind="ExternalInput").ap()
    b1 = nc.dram_tensor("b1", [DFF], F32, kind="ExternalInput").ap()
    u2 = nc.dram_tensor("u2", [DFF, RANK], F32, kind="ExternalInput").ap()
    v2 = nc.dram_tensor("v2", [RANK, D], F32, kind="ExternalInput").ap()
    b2 = nc.dram_tensor("b2", [D], F32, kind="ExternalInput").ap()
    cst = nc.dram_tensor("cst", [128, 8], F32, kind="ExternalInput").ap()
    bkd = nc.dram_tensor("bk", [128, NPAIR], F32, kind="ExternalInput").ap()
    atl = nc.dram_tensor("atl", [128, 7 * 128], BF16, kind="ExternalInput").ap()
    idf_d = nc.dram_tensor("idf", [128, 128], F32, kind="ExternalInput").ap()
    idb_d = nc.dram_tensor("idb", [128, 128], BF16, kind="ExternalInput").ap()
    out_d = nc.dram_tensor("out", [NQ * 128, D], F32, kind="ExternalOutput").ap()
    if DBG:
        dbga = nc.dram_tensor("dbga", [128, 103 * 1024], BF16, kind="ExternalOutput").ap()
        dbgc = nc.dram_tensor("dbgc", [128, 103 * 1024], BF16, kind="ExternalOutput").ap()
        dbgl = nc.dram_tensor("dbgl", [128, 2 * 3 * 1024 + 32], F32, kind="ExternalOutput").ap()

    AR = 206 * KB
    arena = nc.alloc_sbuf_tensor("arena", [128, AR // 2], BF16)
    arA = arena.ap()

    def bfv(off_b, cols):
        assert off_b % 2 == 0 and off_b + 2 * cols <= AR, (off_b, cols)
        return arA[:, off_b // 2: off_b // 2 + cols]

    PSTEP = AR // 2

    def sbap(off_el, dims):
        return bass.AP(arena, off_el, [[PSTEP, 128]] + dims)

    def f32v(off_b, cols):
        assert off_b % 4 == 0 and off_b + 4 * cols <= AR, (off_b, cols)
        return arA[:, off_b // 2: off_b // 2 + 2 * cols].bitcast(F32)

    o_cst, o_bk, o_atl = 0, 64, 256
    o_idb, o_idf = 2048, 2304
    o_b1c, o_b2c, o_rec, o_ln = 2816, 2944, 2976, 3072
    oA = 3200
    oB = oA + 64 * KB
    oC = oB + 65 * KB
    oD = oC + 32 * KB
    TOT = oD + 36 * KB
    assert TOT <= AR, (TOT, AR)

    cstS = f32v(o_cst, 8)
    bkS = f32v(o_bk, NPAIR)
    atlS = bfv(o_atl, 7 * 128)
    idbS = bfv(o_idb, 128)
    idfS = f32v(o_idf, 128)
    b1cS = f32v(o_b1c, 32)
    b2cS = f32v(o_b2c, 8)
    recS = f32v(o_rec, 8)
    lnS = f32v(o_ln, 16)

    KTv = bfv(oA, 32 * KB)
    Vsb = bfv(oB, 33280)
    QTv = bfv(oC, 16 * KB)
    SCR = [bfv(oD + i * 2 * KB, KB) for i in range(2)]
    XTG = [bfv(oD + 4 * KB, 8 * KB), bfv(oD + 20 * KB, 8 * KB)]
    ESB = bfv(oD, 2 * KB)
    ATOK = bfv(oD + 4 * KB, 16 * KB)
    XTOK = [f32v(oD, KB), f32v(oD + 4 * KB, KB)]
    YTOK = [f32v(oD + 8 * KB, KB), f32v(oD + 12 * KB, KB)]
    SQ = f32v(oD + 16 * KB, KB)
    Y1B = [bfv(oD + 20 * KB, KB), bfv(oD + 22 * KB, KB)]
    OTOK = [f32v(oD + 24 * KB, KB), f32v(oD + 28 * KB, KB)]
    PT = bfv(oD + 32 * KB, KB)
    Q2T = bfv(oD + 34 * KB, KB)
    ATT = bfv(oA, 16 * KB)
    WoT = bfv(oA + 32 * KB, 8 * KB)
    YAT = f32v(oA + 48 * KB, 4 * KB)
    U1B = bfv(oA, 2 * KB)
    V1B = bfv(oA + 4 * KB, 8 * KB)
    V2B = bfv(oA + 20 * KB, 2 * KB)
    U2Bb = bfv(oA + 24 * KB, 4 * KB)
    FFT = f32v(oA + 32 * KB, 4 * KB)
    HTr = bfv(oA + 48 * KB, 2 * KB)
    U2Ba = bfv(oD, 4 * KB)
    XTG = bfv(oD + 4 * KB, 8 * KB)
    WT = bfv(oD + 20 * KB, 8 * KB)
    ATOK = bfv(oD + 4 * KB, 16 * KB)
    ESB = bfv(oD, 2 * KB)
    Y1 = f32v(oB, 16 * KB)
    Y1T = bfv(oC, 16 * KB)

    ps = nc.alloc_psum_tensor("ps", [128, 4096], F32)
    psA = ps.ap()
    psB = psA.bitcast(BF16)

    sch = Sched()
    add, now = sch.add, sch.now

    for (dst, src) in ((cstS, cst[:, :]), (bkS, bkd[:, :]), (atlS, atl[:, :]),
                       (idbS, idb_d[:, :]), (idfS, idf_d[:, :]),
                       (b1cS[:, 0:32], b1.rearrange("(c p) -> p c", p=128)),
                       (b2cS[:, 0:8], b2.rearrange("(c p) -> p c", p=128))):
        add("sync", "dma", inc="c", incv=16, out=dst, in_=src, slow=True)
    t_c = now("c")
    t_ones = add("vector", "memset", inc="ones", ap=Vsb, value=1.0)

    SCRF = [f32v(oD + i * 4 * KB, KB) for i in range(2)]
    XTGn = bfv(oD + 8 * KB, 4 * KB)
    WKT = bfv(oD + 16 * KB, 8 * KB)
    WVT = bfv(oC, 8 * KB)
    WOSCR = [f32v(oD + 24 * KB, KB), f32v(oD + 28 * KB, KB)]

    scrf_rel = [None, None]
    trset_rel = [None, None]
    mmset_rel = [None, None]
    st_tr = [0]
    st_mm = [0]

    def tile_load_transpose(xsrc_row_ap, slot, gates_ld=(), gates_tr=(),
                            evac_dst=None, evac_gates=()):
        w = [scrf_rel[slot]] if scrf_rel[slot] else []
        w += list(gates_ld)
        ld = add("sync", "dma", waits=w, inc="xld", incv=16,
                 out=SCRF[slot][:, :], in_=xsrc_row_ap)
        par = st_tr[0]; st_tr[0] ^= 1
        base = par * 1024
        trl = None
        for i in range(8):
            w2 = [ld] + (list(gates_tr) if i == 0 else [])
            if i == 0 and trset_rel[par]:
                w2.append(trset_rel[par])
            trl = add("tensor", "tr", waits=w2, inc="xtr",
                      out=psA[:, base + i * 128: base + (i + 1) * 128],
                      in_=SCRF[slot][:, i * 128:(i + 1) * 128],
                      identity=idfS[:, :])
        scrf_rel[slot] = trl
        ev = add("scalar", "copy", waits=[trl] + list(evac_gates), inc="xev",
                 out=evac_dst,
                 in_=psA[:, base: base + 1024].rearrange("p (i c) -> p i c", i=8))
        trset_rel[par] = ev
        return ev

    def wt_prep(wsrc, dst_off_el, tag, gates_ld=(), gates_tr=(), evac_gates=()):
        ev = None
        for j in range(8):
            ev = tile_load_transpose(
                wsrc[j * 128:(j + 1) * 128, :], j % 2,
                gates_ld=gates_ld if j < 2 else (),
                gates_tr=gates_tr if j == 0 else (),
                evac_dst=sbap(dst_off_el + j * 128, [[1024, 8], [1, 128]]),
                evac_gates=evac_gates if j == 0 else ())
        return ev

    t_wk = wt_prep(wts["wk"], (oD + 16 * KB) // 2, "wk", gates_ld=[t_c])
    t_wv = wt_prep(wts["wv"], oC // 2, "wv")

    kbank = [None, None]
    xtg_rel = None
    for g in range(8):
        evs = []
        for t in range(4):
            gt = g * 4 + t
            dst = sbap((oD + 8 * KB) // 2 + t * 128, [[512, 8], [1, 128]])
            egates = [xtg_rel] if (t == 0 and xtg_rel) else []
            evs.append(tile_load_transpose(
                xkv[gt * 128:(gt + 1) * 128, :], gt % 2,
                gates_tr=[t_wk, t_wv] if gt == 0 else (),
                evac_dst=dst, evac_gates=egates))
        t_xt = evs[-1]
        last = None
        for j in range(8):
            par = st_mm[0]; st_mm[0] ^= 1
            base = 2048 + par * 1024
            w = [t_xt]
            if kbank[par]:
                w.append(kbank[par])
            for i in range(8):
                last = add("tensor", "mm", waits=w if i == 0 else [],
                           inc="kmm" if i == 7 else None,
                           out=psA[:, base: base + 512],
                           lhsT=WKT[:, i * 1024 + j * 128: i * 1024 + (j + 1) * 128],
                           rhs=XTGn[:, i * 512: (i + 1) * 512],
                           start=(i == 0), stop=(i == 7))
            ev = add("scalar", "copy", waits=[now("kmm")], inc="kev",
                     out=KTv[:, j * 4096 + g * 512: j * 4096 + (g + 1) * 512],
                     in_=psA[:, base: base + 512])
            kbank[par] = ev
        for t in range(4):
            gt = g * 4 + t
            par = st_mm[0]; st_mm[0] ^= 1
            base = 2048 + par * 1024
            w = [t_xt, t_ones] if gt == 0 else [t_xt]
            if kbank[par]:
                w.append(kbank[par])
            for i in range(8):
                for hf in range(2):
                    last = add("tensor", "mm",
                               waits=w if (i == 0 and hf == 0) else [],
                               inc="vmm" if (i == 7 and hf == 1) else None,
                               out=psA[:, base + hf * 512: base + hf * 512 + 512],
                               lhsT=XTGn[:, i * 512 + t * 128: i * 512 + (t + 1) * 128],
                               rhs=WVT[:, i * 1024 + hf * 512: i * 1024 + hf * 512 + 512],
                               start=(i == 0), stop=(i == 7))
            dstv = sbap(oB // 2 + gt * 1040, [[65, 16], [1, 64]])
            srcv = psA[:, base: base + 1024].rearrange("p (h d) -> p h d", h=16)
            ev = add("vector", "ts", waits=[now("vmm")], inc="vev",
                     out=dstv, in0=srcv, s1=cstS[:, 3:4], s2=None,
                     op0=ALU.mult, op1=None)
            kbank[par] = ev
        xtg_rel = last
    t_kmm_all = now("kmm")
    t_kev_all = now("kev")
    t_vmm_all = now("vmm")
    t_vev_all = now("vev")

    t_wq = wt_prep(wts["wq"], (oD + 16 * KB) // 2, "wq",
                   gates_tr=[t_kmm_all])
    qbank = kbank
    for g in range(4):
        evs = []
        for t in range(4):
            gt = g * 4 + t
            dst = sbap((oD + 8 * KB) // 2 + t * 128, [[512, 8], [1, 128]])
            egates = [xtg_rel] if (t == 0 and xtg_rel) else []
            evs.append(tile_load_transpose(
                xq[gt * 128:(gt + 1) * 128, :], gt % 2,
                gates_tr=[t_wq] if gt == 0 else (),
                evac_dst=dst, evac_gates=egates))
        t_xt = evs[-1]
        last = None
        for j in range(8):
            par = st_mm[0]; st_mm[0] ^= 1
            base = 2048 + par * 1024
            w = [t_xt]
            if qbank[par]:
                w.append(qbank[par])
            for i in range(8):
                last = add("tensor", "mm", waits=w if i == 0 else [],
                           inc="qmm" if i == 7 else None,
                           out=psA[:, base: base + 512],
                           lhsT=WKT[:, i * 1024 + j * 128: i * 1024 + (j + 1) * 128],
                           rhs=XTGn[:, i * 512: (i + 1) * 512],
                           start=(i == 0), stop=(i == 7))
            ev = add("scalar", "copy",
                     waits=[now("qmm")] + ([t_vmm_all] if (g == 0 and j == 0) else []),
                     inc="qev",
                     out=QTv[:, j * 2048 + g * 512: j * 2048 + (g + 1) * 512],
                     in_=psA[:, base: base + 512])
            qbank[par] = ev
        xtg_rel = last
    t_qev_all = now("qev")
    qbank_rel = qbank


    exp_tok = [None] * NSTEP
    pvd_tok = [None] * NSTEP
    evac_rel = None
    gstep = 0
    first_S = True
    for m in range(4):
        for h in range(16):
            j = h // 2
            hp = 64 * (h % 2)
            pend = None
            for s in range(SM[m]):
                par = gstep % 2
                for li in range(2):
                    l = 2 * s + li
                    pk = slot2phys(m, l)
                    w = []
                    if first_S:
                        w = [t_qev_all, t_kev_all, qbank_rel[0], qbank_rel[1],
                             trset_rel[0], trset_rel[1]]
                        first_S = False
                    if gstep >= 2 and li == 0:
                        w.append(("e", gstep - 1))
                    add("tensor", "mm", waits=w, inc="smm" if li == 1 else None,
                        out=psA[:, par * 1024 + li * 512: par * 1024 + li * 512 + 512],
                        lhsT=KTv[hp:hp + 64, j * 4096 + pk * 128: j * 4096 + (pk + 1) * 128],
                        rhs=QTv[hp:hp + 64, j * 2048 + m * 512: j * 2048 + (m + 1) * 512],
                        start=True, stop=True)
                wexp = [now("smm")]
                if gstep >= 2:
                    wexp.append(("pvd", gstep - 1))
                pr = sum(SM[:mm2] for mm2 in []) if False else (sum(SM[:m]) + s)
                exp_tok[gstep] = add("scalar", "act", waits=wexp, inc="e",
                                     out=ESB[:, par * 1024: par * 1024 + 1024],
                                     in_=psA[:, par * 1024: par * 1024 + 1024],
                                     func=AF.Exp, bias=bkS[:, pr:pr + 1],
                                     scale=cstS[:, 0:1])
                if s < 2:
                    for li in range(2):
                        tau = 2 * s + li
                        add("vector", "tt",
                            waits=[exp_tok[gstep]] if li == 0 else [],
                            inc="mk" if li == 1 else None,
                            out=ESB[:, par * 1024 + li * 512: par * 1024 + li * 512 + 512],
                            in0=ESB[:, par * 1024 + li * 512: par * 1024 + li * 512 + 512],
                            in1=atlS[:, (3 - tau) * 128: (3 - tau) * 128 + 512],
                            op=ALU.mult)
                    exp_tok[gstep] = now("mk")
                if pend is not None:
                    emit_pv(m, h, pend, exp_tok, add, now, psA, ESB, Vsb,
                            evac_rel if pend[2] == 0 else None,
                            (t_vev_all, t_ones) if (m == 0 and h == 0 and pend[2] == 0) else None)
                    pvd_tok[pend[3]] = now("pvd")
                pend = (m, h, s, gstep)
                gstep += 1
            emit_pv(m, h, pend, exp_tok, add, now, psA, ESB, Vsb,
                    evac_rel if pend[2] == 0 else None,
                    (t_vev_all, t_ones) if (m == 0 and h == 0 and pend[2] == 0) else None)
            pvd_tok[pend[3]] = now("pvd")
            sums_ap = psA[:, 2048:4096].rearrange("p (t q) -> p t q", t=4)[:, :, 64:65]
            rec_ap = recS[:, 0:4].rearrange("p (t o) -> p t o", o=1)
            rc = add("vector", "recip", waits=[pvd_tok[pend[3]]], inc="dve",
                     out=rec_ap, in_=sums_ap)
            add("vector", "wait", waits=[rc])
            for tau in range(4):
                add("vector", "ts", inc="aev" if tau == 3 else None,
                    out=ATOK[:, (4 * m + tau) * 1024 + 64 * h:
                             (4 * m + tau) * 1024 + 64 * h + 64],
                    in0=psA[:, 2048 + tau * 512: 2048 + tau * 512 + 64],
                    s1=recS[:, tau:tau + 1], s2=None, op0=ALU.mult, op1=None)
            evac_rel = now("aev")
    t_exp_all = now("e")
    t_aev_all = now("aev")
    t_smm_all = now("smm")
    t_dbga = None
    if DBG:
        t_dbga = add("sync", "dma", waits=[t_aev_all], inc="dba", incv=16,
                     out=dbga[:, :], in_=arA[:, 0:103 * 1024])

    tr4_rel = [None, None]
    for t in range(NQ):
        st_ = t % 2
        w = [t_aev_all, t_exp_all] if t == 0 else []
        if tr4_rel[st_]:
            w = w + [tr4_rel[st_]]
        trl = None
        for cch in range(8):
            trl = add("tensor", "tr", waits=w if cch == 0 else [], inc="tr4",
                      out=psB[:, st_ * 2048 + cch * 128: st_ * 2048 + (cch + 1) * 128],
                      in_=ATOK[:, t * 1024 + cch * 128: t * 1024 + (cch + 1) * 128],
                      identity=idbS[:, :])
        dst = sbap(oA // 2 + t * 128, [[2048, 8], [1, 128]])
        srcap = psB[:, st_ * 2048: st_ * 2048 + 1024].rearrange(
            "p (c q) -> p c q", c=8)
        ev = add("scalar", "copy",
                 waits=[trl, t_smm_all if t == 0 else None,
                        t_dbga if (DBG and t == 0) else None],
                 inc="atev", out=dst, in_=srcap)
        tr4_rel[st_] = ev
    t_atev_all = now("atev")

    WT_PH4 = WoT
    woscr_rel = [None, None]
    for j in range(8):
        w = [woscr_rel[j % 2]] if woscr_rel[j % 2] else [t_atev_all]
        ld = add("sync", "dma", waits=w, inc="wol", incv=16,
                 out=WOSCR[j % 2][:, :], in_=wts["wo"][j * 128:(j + 1) * 128, :])
        par = st_tr[0]; st_tr[0] ^= 1
        base = par * 1024
        trl = None
        for i in range(8):
            w2 = [ld]
            if i == 0 and j < 2:
                w2 += [tr4_rel[0], tr4_rel[1]]
            if i == 0 and trset_rel[par]:
                w2.append(trset_rel[par])
            trl = add("tensor", "tr", waits=w2 if i == 0 else [ld], inc="wotr",
                      out=psA[:, base + i * 128: base + (i + 1) * 128],
                      in_=WOSCR[j % 2][:, i * 128:(i + 1) * 128],
                      identity=idfS[:, :])
        woscr_rel[j % 2] = trl
        ev = add("scalar", "copy", waits=[trl], inc="wot",
                 out=sbap((oA + 32 * KB) // 2 + j * 128, [[1024, 8], [1, 128]]),
                 in_=psA[:, base: base + 1024].rearrange("p (i c) -> p i c", i=8))
        trset_rel[par] = ev
    t_wo = now("wot")

    ffn_gate = None
    ffn_loads = []

    ln_ysum, ln_qsum = lnS[:, 0:1], lnS[:, 1:2]
    ln_nm, ln_ex2 = lnS[:, 2:3], lnS[:, 3:4]
    ln_m2, ln_vpe = lnS[:, 4:5], lnS[:, 5:6]
    ln_rv, ln_inv = lnS[:, 6:7], lnS[:, 7:8]
    ln_nmi = lnS[:, 8:9]

    def ln_chain(par, resid_in1, apply_out, extra_w=()):
        t1 = add("vector", "stt", waits=list(extra_w), inc="dve",
                 out=YTOK[par][:, :],
                 in0=psA[:, 2048 + par * 1024: 2048 + par * 1024 + 1024],
                 s1=cstS[:, 3:4], in1=resid_in1, op0=ALU.mult, op1=ALU.add,
                 accum_out=ln_ysum)
        t2 = add("scalar", "act", waits=[t1], inc="sq", out=SQ[:, :],
                 in_=YTOK[par][:, :], func=AF.Square, accum_out=ln_qsum)
        e = add("vector", "ts", waits=[t2], inc="dve", out=ln_nm, in0=ln_ysum,
                s1=cstS[:, 4:5], s2=None, op0=ALU.mult, op1=None)
        e2 = add("vector", "ts", inc="dve", out=ln_ex2, in0=ln_qsum,
                 s1=cstS[:, 1:2], s2=None, op0=ALU.mult, op1=None)
        add("vector", "wait", waits=[e2])
        e = add("vector", "tt", inc="dve", out=ln_m2, in0=ln_nm, in1=ln_nm,
                op=ALU.mult)
        add("vector", "wait", waits=[e])
        e = add("vector", "tt", inc="dve", out=ln_vpe, in0=ln_ex2, in1=ln_m2,
                op=ALU.subtract)
        add("vector", "wait", waits=[e])
        e = add("vector", "ts", inc="dve", out=ln_vpe, in0=ln_vpe,
                s1=cstS[:, 2:3], s2=None, op0=ALU.add, op1=None)
        add("vector", "wait", waits=[e])
        t3 = add("vector", "recip", inc="dve", out=ln_rv, in_=ln_vpe)
        t4 = add("scalar", "act", waits=[t3], inc="sq", out=ln_inv, in_=ln_rv,
                 func=AF.Sqrt)
        t5 = add("vector", "tt", waits=[t4], inc="dve", out=ln_nmi, in0=ln_nm,
                 in1=ln_inv, op=ALU.mult)
        add("vector", "wait", waits=[t5])
        ap1 = add("vector", "ts", inc="dve", out=apply_out, in0=YTOK[par][:, :],
                  s1=ln_inv, s2=ln_nmi, op0=ALU.mult, op1=ALU.add)
        return t1, ap1

    yev_rel = [None, None]
    t2_rel = [None, None]
    xtok_rel = [None, None]
    ytok_rel = [None, None]
    y1b_rel = [None, None]
    xr_tok = [None] * NQ
    last_womm = None
    for g4 in range(4):
        for jj in range(8):
            wset = jj % 2
            w = [t_wo] if (g4 == 0 and jj == 0) else []
            if g4 == 0 and jj == 0:
                w.append(tr4_rel[0])
                w.append(tr4_rel[1])
            if yev_rel[wset]:
                w.append(yev_rel[wset])
            for i in range(8):
                last_womm = add("tensor", "mm", waits=w if i == 0 else [],
                                inc="womm" if i == 7 else None,
                                out=psA[:, wset * 512: wset * 512 + 512],
                                lhsT=WT_PH4[:, i * 1024 + jj * 128: i * 1024 + (jj + 1) * 128],
                                rhs=ATT[:, i * 2048 + g4 * 512: i * 2048 + (g4 + 1) * 512],
                                start=(i == 0), stop=(i == 7))
            ev = add("scalar", "copy",
                     waits=[now("womm")] + ([("t2", 32 * g4)] if g4 > 0 else []),
                     inc="yev",
                     out=YAT[:, jj * 512:(jj + 1) * 512],
                     in_=psA[:, wset * 512: wset * 512 + 512])
            yev_rel[wset] = ev
        t_ya = now("yev")
        for tau in range(4):
            tloc = g4 * 4 + tau
            par = tloc % 2
            w = [t_atev_all] if tloc < 2 else []
            if xtok_rel[par]:
                w.append(xtok_rel[par])
            xr_tok[tloc] = add("sync", "dma", waits=w, inc="xr", incv=16,
                               out=XTOK[par][:, :],
                               in_=xq[tloc * 128:(tloc + 1) * 128, :])
            w = [t_ya]
            if t2_rel[par]:
                w.append(t2_rel[par])
            if tloc == 0:
                w.append(evac_rel)
            trl = None
            for cch in range(8):
                trl = add("tensor", "tr", waits=w if cch == 0 else [], inc="t2",
                          out=psA[:, 2048 + par * 1024 + cch * 128:
                                  2048 + par * 1024 + (cch + 1) * 128],
                          in_=YAT[:, cch * 512 + tau * 128: cch * 512 + (tau + 1) * 128],
                          identity=idfS[:, :])
            extra = [trl, xr_tok[tloc]]
            if ytok_rel[par]:
                extra.append(ytok_rel[par])
            t1, ap1 = ln_chain(par, XTOK[par][:, :],
                               Y1[:, tloc * 1024:(tloc + 1) * 1024], extra)
            t2_rel[par] = t1
            xtok_rel[par] = t1
            if DBG and tloc < 2:
                off = tloc * (3 * 1024 + 16)
                dsn = add("sync", "dma", waits=[ap1], inc="dbl", incv=16,
                          out=dbgl[:, off:off + 16], in_=lnS[:, 0:16])
                dsn = add("sync", "dma", inc="dbl", incv=16,
                          out=dbgl[:, off + 16:off + 1040], in_=XTOK[par][:, :])
                dsn = add("sync", "dma", inc="dbl", incv=16,
                          out=dbgl[:, off + 1040:off + 2064], in_=YTOK[par][:, :])
                dsn = add("sync", "dma", inc="dbl", incv=16,
                          out=dbgl[:, off + 2064:off + 3088],
                          in_=Y1[:, tloc * 1024:(tloc + 1) * 1024])
                sch.dbg_gate = dsn
            ap2 = add("scalar", "copy",
                      waits=[ap1] + ([sch.dbg_gate] if (DBG and tloc < 2) else []),
                      inc="y1c",
                      out=Y1B[par][:, :],
                      in_=Y1[:, tloc * 1024:(tloc + 1) * 1024])
            ytok_rel[par] = ap2
            w = [ap2] + ([y1b_rel[par]] if y1b_rel[par] else [])
            lastd = None
            for cch in range(8):
                lastd = add("scalar", "dmaT", waits=w if cch == 0 else [],
                            inc="y1t", incv=16,
                            out=Y1T[:, cch * 2048 + tloc * 128: cch * 2048 + (tloc + 1) * 128],
                            in_=Y1B[par][:, cch * 128:(cch + 1) * 128])
            y1b_rel[par] = lastd
    t_y1t = now("y1t")
    t_womm_all = now("womm")
    t_t2_all = now("t2")

    t_dbgc = None
    if DBG:
        t_dbgc = add("sync", "dma", waits=[t_womm_all, t_t2_all, ("y1c", 16),
                                           now("y1t")],
                     inc="dbc", incv=16, out=dbgc[:, :], in_=arA[:, 0:103 * 1024])
    w = [t_womm_all, t_t2_all, ("y1c", 16)]
    if DBG:
        w.append(t_dbgc)
    f_tok = None
    for i in range(8):
        f_tok = add("gpsimd", "dma", waits=w if i == 0 else [], inc="fw",
                    incv=16, out=U1B[:, i * 256:(i + 1) * 256],
                    in_=u1[i * 128:(i + 1) * 128, :])
    for r in range(2):
        f_tok = add("gpsimd", "dma", inc="fw", incv=16,
                    out=V1B[:, r * 4096:(r + 1) * 4096],
                    in_=v1[r * 128:(r + 1) * 128, :])
    for r in range(2):
        f_tok = add("gpsimd", "dma", inc="fw", incv=16,
                    out=V2B[:, r * 1024:(r + 1) * 1024],
                    in_=v2[r * 128:(r + 1) * 128, :])
    for i in range(32):
        dstw = U2Ba if i < 16 else U2Bb
        f_tok = add("gpsimd", "dma", inc="fw", incv=16,
                    out=dstw[:, (i % 16) * 256:(i % 16 + 1) * 256],
                    in_=u2[i * 128:(i + 1) * 128, :])
    t_fw = now("fw")

    def u2slice(fc, r):
        t_ = U2Ba if fc < 16 else U2Bb
        return t_[:, (fc % 16) * 256 + r * 128: (fc % 16) * 256 + (r + 1) * 128]

    pbank_rel = [None, None]
    hbank_rel = [None, None]
    t3_rel = [None, None]
    str_rel = [None, None]
    for g4 in range(4):
        for r in range(2):
            pset = r % 2
            w = [t_fw, t_y1t] if (g4 == 0 and r == 0) else []
            if pbank_rel[pset]:
                w.append(pbank_rel[pset])
            elif g4 == 0:
                w.append(yev_rel[pset])
            for i in range(8):
                add("tensor", "mm", waits=w if i == 0 else [],
                    inc="pmm" if i == 7 else None,
                    out=psA[:, pset * 512: pset * 512 + 512],
                    lhsT=U1B[:, i * 256 + r * 128: i * 256 + (r + 1) * 128],
                    rhs=Y1T[:, i * 2048 + g4 * 512: i * 2048 + (g4 + 1) * 512],
                    start=(i == 0), stop=(i == 7))
            ev = add("scalar", "copy", waits=[now("pmm")], inc="pev",
                     out=PT[:, r * 512:(r + 1) * 512],
                     in_=psA[:, pset * 512: pset * 512 + 512])
            pbank_rel[pset] = ev
        t_pt = now("pev")
        ht_rel = [None] * 4
        for fc in range(32):
            hset = fc % 2
            w = [t_pt] if fc < 2 else []
            if hbank_rel[hset]:
                w.append(hbank_rel[hset])
            for r in range(2):
                add("tensor", "mm", waits=w if r == 0 else [],
                    inc="hmm" if r == 1 else None,
                    out=psA[:, (2 + hset) * 512: (2 + hset) * 512 + 512],
                    lhsT=V1B[:, r * 4096 + fc * 128: r * 4096 + (fc + 1) * 128],
                    rhs=PT[:, r * 512:(r + 1) * 512],
                    start=(r == 0), stop=(r == 1))
            wg = [now("hmm")]
            if ht_rel[fc % 4]:
                wg.append(ht_rel[fc % 4])
            ev = add("scalar", "act", waits=wg, inc="hev",
                     out=HTr[:, (fc % 4) * 512: (fc % 4 + 1) * 512],
                     in_=psA[:, (2 + hset) * 512: (2 + hset) * 512 + 512],
                     func=AF.Gelu_apprx_tanh, bias=b1cS[:, fc:fc + 1],
                     scale=cstS[:, 3:4])
            hbank_rel[hset] = ev
            wq2 = [ev]
            if fc == 0:
                if pbank_rel[0]:
                    wq2.append(pbank_rel[0])
                if pbank_rel[1]:
                    wq2.append(pbank_rel[1])
            last_q2 = None
            for r in range(2):
                last_q2 = add("tensor", "mm", waits=wq2 if r == 0 else [],
                              inc="q2p" if r == 1 else None,
                              out=psA[:, r * 512: r * 512 + 512],
                              lhsT=u2slice(fc, r),
                              rhs=HTr[:, (fc % 4) * 512: (fc % 4 + 1) * 512],
                              start=(fc == 0), stop=(fc == 31))
            ht_rel[fc % 4] = last_q2
        for r in range(2):
            ev = add("scalar", "copy", waits=[now("q2p")], inc="q2ev",
                     out=Q2T[:, r * 512:(r + 1) * 512],
                     in_=psA[:, r * 512: r * 512 + 512])
            pbank_rel[r] = ev
        t_q2 = now("q2ev")
        for jj in range(8):
            oset = jj % 2
            w = [t_q2]
            if hbank_rel[oset]:
                w.append(hbank_rel[oset])
            for r in range(2):
                add("tensor", "mm", waits=w if r == 0 else [],
                    inc="omm" if r == 1 else None,
                    out=psA[:, (2 + oset) * 512: (2 + oset) * 512 + 512],
                    lhsT=V2B[:, r * 1024 + jj * 128: r * 1024 + (jj + 1) * 128],
                    rhs=Q2T[:, r * 512:(r + 1) * 512],
                    start=(r == 0), stop=(r == 1))
            ev = add("vector", "ts",
                     waits=[now("omm")] + ([("t3", 32 * g4)] if g4 > 0 else []),
                     inc="oev",
                     out=FFT[:, jj * 512:(jj + 1) * 512],
                     in0=psA[:, (2 + oset) * 512: (2 + oset) * 512 + 512],
                     s1=b2cS[:, jj:jj + 1], s2=None, op0=ALU.add, op1=None)
            hbank_rel[oset] = ev
        t_ff = now("oev")
        for tau in range(4):
            tloc = g4 * 4 + tau
            par = tloc % 2
            w = [t_ff]
            if t3_rel[par]:
                w.append(t3_rel[par])
            elif g4 == 0:
                w.append(t2_rel[par])
            trl = None
            for cch in range(8):
                trl = add("tensor", "tr", waits=w if cch == 0 else [], inc="t3",
                          out=psA[:, 2048 + par * 1024 + cch * 128:
                                  2048 + par * 1024 + (cch + 1) * 128],
                          in_=FFT[:, cch * 512 + tau * 128: cch * 512 + (tau + 1) * 128],
                          identity=idfS[:, :])
            extra = [trl]
            if ytok_rel[par]:
                extra.append(ytok_rel[par])
            t1, ap1 = ln_chain(par, Y1[:, tloc * 1024:(tloc + 1) * 1024],
                               OTOK[par][:, :],
                               extra + ([str_rel[par]] if str_rel[par] else []))
            t3_rel[par] = t1
            ytok_rel[par] = ap1
            st_tok = add("sync", "dma", waits=[ap1], inc="st", incv=16,
                         out=out_d[tloc * 128:(tloc + 1) * 128, :],
                         in_=OTOK[par][:, :])
            str_rel[par] = st_tok
    add("sync", "wait", waits=[now("st")])

    semnames = sorted(sch.cnt.keys())
    sems = {}
    for n in semnames:
        g = nc.semaphore(n)
        sems[n] = g.__enter__()
        nc._sem_guards = getattr(nc, "_sem_guards", [])
        nc._sem_guards.append(g)

    def run_ops(eng, name):
        waited = {}
        for op in sch.ops[name]:
            for (sname, v) in op["waits"]:
                if v > waited.get(sname, 0):
                    eng.wait_ge(sems[sname], v)
                    waited[sname] = v
            k, kw = op["kind"], op["kw"]
            if k == "wait":
                continue
            if k == "dma":
                ins = eng.dma_start(out=kw["out"], in_=kw["in_"],
                                    allow_slow_non_contiguous=kw.get("slow", False))
            elif k == "dmaT":
                ins = eng.dma_start(out=kw["out"], in_=kw["in_"], transpose=True)
            elif k == "mm":
                ins = eng.matmul(kw["out"], kw["lhsT"], kw["rhs"],
                                 start=kw["start"], stop=kw["stop"])
            elif k == "tr":
                ins = eng.transpose(kw["out"], kw["in_"], kw["identity"])
            elif k == "act":
                a = {}
                for f in ("bias", "scale", "accum_out"):
                    if f in kw and kw[f] is not None:
                        a[f] = kw[f]
                ins = eng.activation(kw["out"], kw["in_"], kw["func"], **a)
            elif k == "copy":
                ins = (eng.copy(kw["out"], kw["in_"]) if name == "scalar"
                       else eng.tensor_copy(kw["out"], kw["in_"]))
            elif k == "tt":
                ins = eng.tensor_tensor(kw["out"], kw["in0"], kw["in1"], kw["op"])
            elif k == "ts":
                if kw.get("s2") is None:
                    ins = eng.tensor_scalar(kw["out"], kw["in0"], kw["s1"],
                                            None, kw["op0"])
                else:
                    ins = eng.tensor_scalar(kw["out"], kw["in0"], kw["s1"],
                                            kw["s2"], kw["op0"], kw["op1"])
            elif k == "stt":
                ins = eng.scalar_tensor_tensor(kw["out"], kw["in0"], kw["s1"],
                                               kw["in1"], kw["op0"], kw["op1"],
                                               accum_out=kw.get("accum_out"))
            elif k == "recip":
                ins = eng.reciprocal(kw["out"], kw["in_"])
            elif k == "memset":
                ins = eng.memset(kw["ap"], kw["value"])
            else:
                raise ValueError(k)
            if op["inc"] is not None:
                ins.then_inc(sems[op["inc"]], op["incv"])

    with nc.Block() as block:
        @block.sync
        def _(e):
            run_ops(e, "sync")

        @block.gpsimd
        def _(e):
            run_ops(e, "gpsimd")

        @block.tensor
        def _(e):
            run_ops(e, "tensor")

        @block.vector
        def _(e):
            run_ops(e, "vector")

        @block.scalar
        def _(e):
            run_ops(e, "scalar")

    return nc


def emit_pv(m, h, pend, exp_tok, add, now, psA, ESB, Vsb, evac_rel, v_gate):
    _, _, s, gs = pend
    par = gs % 2
    gate = exp_tok[gs]
    for li in range(2):
        l = 2 * s + li
        pk = slot2phys(m, l)
        for tau in range(4):
            w = []
            if li == 0 and tau == 0:
                w.append(gate)
                if evac_rel:
                    w.append(evac_rel)
                if v_gate:
                    w.extend(v_gate)
            add("tensor", "mm", waits=w,
                inc="pvd" if (li == 1 and tau == 3) else None,
                out=psA[:, 2048 + tau * 512: 2048 + tau * 512 + 65],
                lhsT=ESB[:, par * 1024 + li * 512 + tau * 128:
                         par * 1024 + li * 512 + (tau + 1) * 128],
                rhs=Vsb[:, pk * 1040 + 65 * h: pk * 1040 + 65 * h + 65],
                start=(l == 0), stop=(l == P[m] - 1))


_CACHE = {}


def kernel(L0_attn_in_features, L0_attn_Wq, L0_attn_Wk, L0_attn_Wv, L0_attn_Wo,
           L0_ffn_U1, L0_ffn_V1, L0_ffn_b1, L0_ffn_U2, L0_ffn_V2, L0_ffn_b2):
    x = np.ascontiguousarray(np.asarray(L0_attn_in_features, dtype=np.float32))
    Ws = {k: np.ascontiguousarray(np.asarray(v, dtype=np.float32))
          for k, v in (("wq", L0_attn_Wq), ("wk", L0_attn_Wk),
                       ("wv", L0_attn_Wv), ("wo", L0_attn_Wo),
                       ("u1", L0_ffn_U1), ("v1", L0_ffn_V1),
                       ("u2", L0_ffn_U2), ("v2", L0_ffn_V2))}
    b1v = np.ascontiguousarray(np.asarray(L0_ffn_b1, dtype=np.float32))
    b2v = np.ascontiguousarray(np.asarray(L0_ffn_b2, dtype=np.float32))

    cstv = np.zeros((128, 8), np.float32)
    cstv[:, 0] = 0.125
    cstv[:, 1] = 1.0 / 1024.0
    cstv[:, 2] = 1.0e-5
    cstv[:, 3] = 1.0
    cstv[:, 4] = -1.0 / 1024.0
    atlv = np.zeros((128, 7 * 128), np.float32)
    kk, qq = np.meshgrid(np.arange(128), np.arange(128), indexing="ij")
    atlv[:, 3 * 128:4 * 128] = (kk <= qq).astype(np.float32)
    atlv[:, 4 * 128:] = 1.0
    atlv = atlv.astype(bf16)
    idfv = np.eye(128, dtype=np.float32)
    idbv = idfv.astype(bf16)

    in_maps, meta = [], []
    for c in range(8):
        seq, role = c // 2, c % 2
        phys, _N = phys_order(role)
        qtiles = [g0 + t for g0 in GQ_ROLE[role] for t in range(4)]
        xs = x[seq]
        xkv_rows = np.concatenate(
            [xs[max(g, 0) * 128: max(g, 0) * 128 + 128] for g in phys], axis=0)
        xq_rows = np.concatenate(
            [xs[g * 128:(g + 1) * 128] for g in qtiles], axis=0)
        live = kill_pairs(role)
        bkv = np.full((128, NPAIR), EXPBIAS, np.float32)
        bkv[:, ~live] = KILL
        im = {"xkv": np.ascontiguousarray(xkv_rows),
              "xq": np.ascontiguousarray(xq_rows),
              "b1": b1v, "b2": b2v, "cst": cstv, "bk": bkv, "atl": atlv,
              "idf": idfv, "idb": idbv}
        im.update(Ws)
        in_maps.append(im)
        meta.append((seq, qtiles))

    if "nc" not in _CACHE:
        _CACHE["nc"] = build_program()
    res = run_bass_kernel_spmd(_CACHE["nc"], in_maps, core_ids=list(range(8)))

    out = np.empty((B, S, D), np.float32)
    for c in range(8):
        seq, qtiles = meta[c]
        oc = np.asarray(res.results[c]["out"])
        for li, g in enumerate(qtiles):
            out[seq, g * 128:(g + 1) * 128] = oc[li * 128:(li + 1) * 128]
    return out


if __name__ == "__main__":
    nc = build_program()
    n = sum(len(v) for v in [])
    print("build ok")


# version 10
# speedup vs baseline: 1.4186x; 1.0463x over previous
import numpy as np
import ml_dtypes
import concourse.bass as bass
import concourse.mybir as mybir
from concourse.bass_utils import run_bass_kernel_spmd

F32 = mybir.dt.float32
BF16 = mybir.dt.bfloat16
AF = mybir.ActivationFunctionType
ALU = mybir.AluOpType
bf16 = ml_dtypes.bfloat16

B, S, D, H, DH = 4, 4096, 1024, 16, 64
DFF, RANK = 4096, 256
NT = S // 128
NQ = 16
P = [8, 16, 24, 32]
SM = [p // 2 for p in P]
NPAIR = sum(SM)
NSTEP = 16 * NPAIR
EXPBIAS = -20.0
KILL = -1.0e30
GQ_ROLE = [[0, 8, 20, 28], [4, 12, 16, 24]]
KB = 1024


def slot2phys(m, l):
    return P[m] - 4 + l if l < 4 else l - 4


def phys_order(role):
    gq = GQ_ROLE[role]
    N = [g + 4 for g in gq]
    phys = [-1] * NT
    placed = set()
    for m in range(4):
        for t in range(4):
            phys[P[m] - 4 + t] = N[m] - 4 + t
            placed.add(N[m] - 4 + t)
    rest = [g for g in range(max(N)) if g not in placed]
    for i in range(NT):
        if phys[i] < 0:
            phys[i] = rest.pop(0) if rest else -1
    return phys, N


def kill_pairs(role):
    phys, N = phys_order(role)
    live = np.zeros(NPAIR, dtype=bool)
    idx = 0
    for m in range(4):
        for sp in range(SM[m]):
            l = 2 * sp
            if l < 4:
                live[idx] = True
            else:
                g = phys[slot2phys(m, l)]
                live[idx] = (g >= 0) and (g <= N[m] - 5)
            idx += 1
    return live


DBG = False
ENGINES = ("sync", "gpsimd", "tensor", "vector", "scalar")


class Sched:
    def __init__(self):
        self.ops = {e: [] for e in ENGINES}
        self.cnt = {}

    def add(self, eng, kind, waits=(), inc=None, incv=1, **kw):
        self.ops[eng].append(dict(kind=kind, waits=[w for w in waits if w],
                                  inc=inc, incv=incv, kw=kw))
        if inc is not None:
            self.cnt[inc] = self.cnt.get(inc, 0) + incv
            return (inc, self.cnt[inc])
        return None

    def now(self, name):
        return (name, self.cnt.get(name, 0))


def build_program():
    nc = bass.Bass("TRN2", target_bir_lowering=False,
                   detect_race_conditions=False)

    xkv = nc.dram_tensor("xkv", [NT * 128, D], F32, kind="ExternalInput").ap()
    xq = nc.dram_tensor("xq", [NQ * 128, D], F32, kind="ExternalInput").ap()
    wts = {n: nc.dram_tensor(n, [D, D], F32, kind="ExternalInput").ap()
           for n in ("wq", "wk", "wv", "wo")}
    u1 = nc.dram_tensor("u1", [D, RANK], F32, kind="ExternalInput").ap()
    v1 = nc.dram_tensor("v1", [RANK, DFF], F32, kind="ExternalInput").ap()
    b1 = nc.dram_tensor("b1", [DFF], F32, kind="ExternalInput").ap()
    u2 = nc.dram_tensor("u2", [DFF, RANK], F32, kind="ExternalInput").ap()
    v2 = nc.dram_tensor("v2", [RANK, D], F32, kind="ExternalInput").ap()
    b2 = nc.dram_tensor("b2", [D], F32, kind="ExternalInput").ap()
    cst = nc.dram_tensor("cst", [128, 8], F32, kind="ExternalInput").ap()
    bkd = nc.dram_tensor("bk", [128, NPAIR], F32, kind="ExternalInput").ap()
    atl = nc.dram_tensor("atl", [128, 7 * 128], BF16, kind="ExternalInput").ap()
    idf_d = nc.dram_tensor("idf", [128, 128], F32, kind="ExternalInput").ap()
    idb_d = nc.dram_tensor("idb", [128, 128], BF16, kind="ExternalInput").ap()
    out_d = nc.dram_tensor("out", [NQ * 128, D], F32, kind="ExternalOutput").ap()
    if DBG:
        dbga = nc.dram_tensor("dbga", [128, 103 * 1024], BF16, kind="ExternalOutput").ap()
        dbgc = nc.dram_tensor("dbgc", [128, 103 * 1024], BF16, kind="ExternalOutput").ap()
        dbgl = nc.dram_tensor("dbgl", [128, 2 * 3 * 1024 + 32], F32, kind="ExternalOutput").ap()

    AR = 205 * KB
    arena = nc.alloc_sbuf_tensor("arena", [128, AR // 2], BF16)
    arA = arena.ap()

    def bfv(off_b, cols):
        assert off_b % 2 == 0 and off_b + 2 * cols <= AR, (off_b, cols)
        return arA[:, off_b // 2: off_b // 2 + cols]

    PSTEP = AR // 2

    def sbap(off_el, dims):
        return bass.AP(arena, off_el, [[PSTEP, 128]] + dims)

    def f32v(off_b, cols):
        assert off_b % 4 == 0 and off_b + 4 * cols <= AR, (off_b, cols)
        return arA[:, off_b // 2: off_b // 2 + 2 * cols].bitcast(F32)

    o_cst, o_bk, o_atl = 0, 64, 256
    o_idb, o_idf = 2048, 2304
    o_b1c, o_b2c, o_rec, o_ln = 2816, 2944, 2976, 3072
    oA = 3200
    oB = oA + 64 * KB
    oC = oB + 65 * KB
    oD = oC + 32 * KB
    TOT = oD + 40 * KB
    assert TOT <= AR, (TOT, AR)

    cstS = f32v(o_cst, 8)
    bkS = f32v(o_bk, NPAIR)
    atlS = bfv(o_atl, 7 * 128)
    idbS = bfv(o_idb, 128)
    idfS = f32v(o_idf, 128)
    b1cS = f32v(o_b1c, 32)
    b2cS = f32v(o_b2c, 8)
    recS = f32v(o_rec, 8)
    lnS = f32v(o_ln, 16)

    KTv = bfv(oA, 32 * KB)
    Vsb = bfv(oB, 33280)
    QTv = bfv(oC, 16 * KB)
    ESB = bfv(oD, 4 * KB)
    ATOK = bfv(oD + 8 * KB, 16 * KB)
    XTOK = [f32v(oD, KB), f32v(oD + 4 * KB, KB)]
    YTOK = [f32v(oD + 8 * KB, KB), f32v(oD + 12 * KB, KB)]
    SQ = f32v(oD + 16 * KB, KB)
    WOSCR = [f32v(oD + 20 * KB, KB), f32v(oD + 24 * KB, KB)]
    OTOK = [f32v(oD + 28 * KB, KB), f32v(oD + 32 * KB, KB)]
    PT = bfv(oD + 36 * KB, KB)
    Q2T = bfv(oD + 38 * KB, KB)
    ATT = bfv(oA, 16 * KB)
    WoT = bfv(oA + 32 * KB, 8 * KB)
    YAT = f32v(oA + 48 * KB, 4 * KB)
    U1B = bfv(oA, 2 * KB)
    V1B = bfv(oA + 4 * KB, 8 * KB)
    V2B = bfv(oA + 20 * KB, 2 * KB)
    U2Bb = bfv(oA + 24 * KB, 4 * KB)
    FFT = f32v(oA + 32 * KB, 4 * KB)
    HTr = bfv(oA + 48 * KB, 2 * KB)
    U2Ba = bfv(oD, 4 * KB)
    XTG = bfv(oD + 4 * KB, 8 * KB)
    WT = bfv(oD + 20 * KB, 8 * KB)

    Y1 = f32v(oB, 16 * KB)
    Y1T = bfv(oC, 16 * KB)

    ps = nc.alloc_psum_tensor("ps", [128, 4096], F32)
    psA = ps.ap()
    psB = psA.bitcast(BF16)

    sch = Sched()
    add, now = sch.add, sch.now

    for (dst, src) in ((cstS, cst[:, :]), (bkS, bkd[:, :]), (atlS, atl[:, :]),
                       (idbS, idb_d[:, :]), (idfS, idf_d[:, :]),
                       (b1cS[:, 0:32], b1.rearrange("(c p) -> p c", p=128)),
                       (b2cS[:, 0:8], b2.rearrange("(c p) -> p c", p=128))):
        add("sync", "dma", inc="c", incv=16, out=dst, in_=src, slow=True)
    t_c = now("c")
    t_ones = add("vector", "memset", inc="ones", ap=Vsb, value=1.0)

    SCRF = [f32v(oD + i * 4 * KB, KB) for i in range(2)]
    XTGn = bfv(oD + 8 * KB, 4 * KB)
    WKT = bfv(oD + 16 * KB, 8 * KB)
    WVT = bfv(oC, 8 * KB)

    scrf_rel = [None, None]
    trset_rel = [None, None]
    mmset_rel = [None, None]
    st_tr = [0]
    st_mm = [0]

    def tile_load_transpose(xsrc_row_ap, slot, gates_ld=(), gates_tr=(),
                            evac_dst=None, evac_gates=()):
        w = [scrf_rel[slot]] if scrf_rel[slot] else []
        w += list(gates_ld)
        ld = add("sync", "dma", waits=w, inc="xld", incv=16,
                 out=SCRF[slot][:, :], in_=xsrc_row_ap)
        par = st_tr[0]; st_tr[0] ^= 1
        base = par * 1024
        trl = None
        for i in range(8):
            w2 = [ld] + (list(gates_tr) if i == 0 else [])
            if i == 0 and trset_rel[par]:
                w2.append(trset_rel[par])
            trl = add("tensor", "tr", waits=w2, inc="xtr",
                      out=psA[:, base + i * 128: base + (i + 1) * 128],
                      in_=SCRF[slot][:, i * 128:(i + 1) * 128],
                      identity=idfS[:, :])
        scrf_rel[slot] = trl
        ev = add("scalar", "copy", waits=[trl] + list(evac_gates), inc="xev",
                 out=evac_dst,
                 in_=psA[:, base: base + 1024].rearrange("p (i c) -> p i c", i=8))
        trset_rel[par] = ev
        return ev

    def wt_prep(wsrc, dst_off_el, tag, gates_ld=(), gates_tr=(), evac_gates=()):
        ev = None
        for j in range(8):
            ev = tile_load_transpose(
                wsrc[j * 128:(j + 1) * 128, :], j % 2,
                gates_ld=gates_ld if j < 2 else (),
                gates_tr=gates_tr if j == 0 else (),
                evac_dst=sbap(dst_off_el + j * 128, [[1024, 8], [1, 128]]),
                evac_gates=evac_gates if j == 0 else ())
        return ev

    t_wk = wt_prep(wts["wk"], (oD + 16 * KB) // 2, "wk", gates_ld=[t_c])
    t_wv = wt_prep(wts["wv"], oC // 2, "wv")

    kbank = [None, None]
    xtg_rel = None
    for g in range(8):
        evs = []
        for t in range(4):
            gt = g * 4 + t
            dst = sbap((oD + 8 * KB) // 2 + t * 128, [[512, 8], [1, 128]])
            egates = [xtg_rel] if (t == 0 and xtg_rel) else []
            evs.append(tile_load_transpose(
                xkv[gt * 128:(gt + 1) * 128, :], gt % 2,
                gates_tr=[t_wk, t_wv] if gt == 0 else (),
                evac_dst=dst, evac_gates=egates))
        t_xt = evs[-1]
        last = None
        for j in range(8):
            par = st_mm[0]; st_mm[0] ^= 1
            base = 2048 + par * 1024
            w = [t_xt]
            if kbank[par]:
                w.append(kbank[par])
            for i in range(8):
                last = add("tensor", "mm", waits=w if i == 0 else [],
                           inc="kmm" if i == 7 else None,
                           out=psA[:, base: base + 512],
                           lhsT=WKT[:, i * 1024 + j * 128: i * 1024 + (j + 1) * 128],
                           rhs=XTGn[:, i * 512: (i + 1) * 512],
                           start=(i == 0), stop=(i == 7))
            ev = add("scalar", "copy", waits=[now("kmm")], inc="kev",
                     out=KTv[:, j * 4096 + g * 512: j * 4096 + (g + 1) * 512],
                     in_=psA[:, base: base + 512])
            kbank[par] = ev
        for t in range(4):
            gt = g * 4 + t
            par = st_mm[0]; st_mm[0] ^= 1
            base = 2048 + par * 1024
            w = [t_xt, t_ones] if gt == 0 else [t_xt]
            if kbank[par]:
                w.append(kbank[par])
            for i in range(8):
                for hf in range(2):
                    last = add("tensor", "mm",
                               waits=w if (i == 0 and hf == 0) else [],
                               inc="vmm" if (i == 7 and hf == 1) else None,
                               out=psA[:, base + hf * 512: base + hf * 512 + 512],
                               lhsT=XTGn[:, i * 512 + t * 128: i * 512 + (t + 1) * 128],
                               rhs=WVT[:, i * 1024 + hf * 512: i * 1024 + hf * 512 + 512],
                               start=(i == 0), stop=(i == 7))
            dstv = sbap(oB // 2 + gt * 1040, [[65, 16], [1, 64]])
            srcv = psA[:, base: base + 1024].rearrange("p (h d) -> p h d", h=16)
            ev = add("vector", "ts", waits=[now("vmm")], inc="vev",
                     out=dstv, in0=srcv, s1=cstS[:, 3:4], s2=None,
                     op0=ALU.mult, op1=None)
            kbank[par] = ev
        xtg_rel = last
    t_kmm_all = now("kmm")
    t_kev_all = now("kev")
    t_vmm_all = now("vmm")
    t_vev_all = now("vev")

    t_wq = wt_prep(wts["wq"], (oD + 16 * KB) // 2, "wq",
                   gates_tr=[t_kmm_all])
    qbank = kbank
    for g in range(4):
        evs = []
        for t in range(4):
            gt = g * 4 + t
            dst = sbap((oD + 8 * KB) // 2 + t * 128, [[512, 8], [1, 128]])
            egates = [xtg_rel] if (t == 0 and xtg_rel) else []
            evs.append(tile_load_transpose(
                xq[gt * 128:(gt + 1) * 128, :], gt % 2,
                gates_tr=[t_wq] if gt == 0 else (),
                evac_dst=dst, evac_gates=egates))
        t_xt = evs[-1]
        last = None
        for j in range(8):
            par = st_mm[0]; st_mm[0] ^= 1
            base = 2048 + par * 1024
            w = [t_xt]
            if qbank[par]:
                w.append(qbank[par])
            for i in range(8):
                last = add("tensor", "mm", waits=w if i == 0 else [],
                           inc="qmm" if i == 7 else None,
                           out=psA[:, base: base + 512],
                           lhsT=WKT[:, i * 1024 + j * 128: i * 1024 + (j + 1) * 128],
                           rhs=XTGn[:, i * 512: (i + 1) * 512],
                           start=(i == 0), stop=(i == 7))
            ev = add("scalar", "copy",
                     waits=[now("qmm")] + ([t_vmm_all] if (g == 0 and j == 0) else []),
                     inc="qev",
                     out=QTv[:, j * 2048 + g * 512: j * 2048 + (g + 1) * 512],
                     in_=psA[:, base: base + 512])
            qbank[par] = ev
        xtg_rel = last
    t_qev_all = now("qev")
    qbank_rel = qbank


    exp_tok = [None] * NSTEP
    pvd_tok = [None] * NSTEP
    evac_rel = None
    gstep = 0
    first_S = True
    for m in range(4):
        for h in range(16):
            j = h // 2
            hp = 64 * (h % 2)
            pend = None
            for s in range(SM[m]):
                par = gstep % 2
                for li in range(2):
                    l = 2 * s + li
                    pk = slot2phys(m, l)
                    w = []
                    if first_S:
                        w = [t_qev_all, t_kev_all, qbank_rel[0], qbank_rel[1],
                             trset_rel[0], trset_rel[1]]
                        first_S = False
                    if gstep >= 2 and li == 0:
                        w.append(("e", gstep - 1))
                    add("tensor", "mm", waits=w, inc="smm" if li == 1 else None,
                        out=psA[:, par * 1024 + li * 512: par * 1024 + li * 512 + 512],
                        lhsT=KTv[hp:hp + 64, j * 4096 + pk * 128: j * 4096 + (pk + 1) * 128],
                        rhs=QTv[hp:hp + 64, j * 2048 + m * 512: j * 2048 + (m + 1) * 512],
                        start=True, stop=True)
                par_e = gstep % 4
                wexp = [now("smm")]
                if gstep >= 4:
                    wexp.append(("pvd", gstep - 3))
                pr = sum(SM[:m]) + s
                exp_tok[gstep] = add("scalar", "act", waits=wexp, inc="e",
                                     out=ESB[:, par_e * 1024: par_e * 1024 + 1024],
                                     in_=psA[:, par * 1024: par * 1024 + 1024],
                                     func=AF.Exp, bias=bkS[:, pr:pr + 1],
                                     scale=cstS[:, 0:1])
                if s < 2:
                    for li in range(2):
                        tau = 2 * s + li
                        add("vector", "tt",
                            waits=[exp_tok[gstep]] if li == 0 else [],
                            inc="mk" if li == 1 else None,
                            out=ESB[:, par_e * 1024 + li * 512: par_e * 1024 + li * 512 + 512],
                            in0=ESB[:, par_e * 1024 + li * 512: par_e * 1024 + li * 512 + 512],
                            in1=atlS[:, (3 - tau) * 128: (3 - tau) * 128 + 512],
                            op=ALU.mult)
                    exp_tok[gstep] = now("mk")
                if pend is not None:
                    emit_pv(m, h, pend, exp_tok, add, now, psA, ESB, Vsb,
                            evac_rel if pend[2] == 0 else None,
                            (t_vev_all, t_ones) if (m == 0 and h == 0 and pend[2] == 0) else None)
                    pvd_tok[pend[3]] = now("pvd")
                pend = (m, h, s, gstep)
                gstep += 1
            emit_pv(m, h, pend, exp_tok, add, now, psA, ESB, Vsb,
                    evac_rel if pend[2] == 0 else None,
                    (t_vev_all, t_ones) if (m == 0 and h == 0 and pend[2] == 0) else None)
            pvd_tok[pend[3]] = now("pvd")
            sums_ap = psA[:, 2048:4096].rearrange("p (t q) -> p t q", t=4)[:, :, 64:65]
            rec_ap = recS[:, 0:4].rearrange("p (t o) -> p t o", o=1)
            rc = add("vector", "recip", waits=[pvd_tok[pend[3]]], inc="dve",
                     out=rec_ap, in_=sums_ap)
            add("vector", "wait", waits=[rc])
            for tau in range(4):
                add("vector", "ts", inc="aev" if tau == 3 else None,
                    out=ATOK[:, (4 * m + tau) * 1024 + 64 * h:
                             (4 * m + tau) * 1024 + 64 * h + 64],
                    in0=psA[:, 2048 + tau * 512: 2048 + tau * 512 + 64],
                    s1=recS[:, tau:tau + 1], s2=None, op0=ALU.mult, op1=None)
            evac_rel = now("aev")
    t_exp_all = now("e")
    t_aev_all = now("aev")
    t_smm_all = now("smm")
    t_dbga = None
    if DBG:
        t_dbga = add("sync", "dma", waits=[t_aev_all], inc="dba", incv=16,
                     out=dbga[:, :], in_=arA[:, 0:103 * 1024])

    tr4_rel = [None, None]
    for t in range(NQ):
        st_ = t % 2
        w = [t_aev_all, t_exp_all] if t == 0 else []
        if tr4_rel[st_]:
            w = w + [tr4_rel[st_]]
        trl = None
        for cch in range(8):
            trl = add("tensor", "tr", waits=w if cch == 0 else [], inc="tr4",
                      out=psB[:, st_ * 2048 + cch * 128: st_ * 2048 + (cch + 1) * 128],
                      in_=ATOK[:, t * 1024 + cch * 128: t * 1024 + (cch + 1) * 128],
                      identity=idbS[:, :])
        dst = sbap(oA // 2 + t * 128, [[2048, 8], [1, 128]])
        srcap = psB[:, st_ * 2048: st_ * 2048 + 1024].rearrange(
            "p (c q) -> p c q", c=8)
        ev = add("scalar", "copy",
                 waits=[trl, t_smm_all if t == 0 else None,
                        t_dbga if (DBG and t == 0) else None],
                 inc="atev", out=dst, in_=srcap)
        tr4_rel[st_] = ev
    t_atev_all = now("atev")

    WT_PH4 = WoT
    woscr_rel = [None, None]
    for j in range(8):
        w = [woscr_rel[j % 2]] if woscr_rel[j % 2] else [t_atev_all]
        ld = add("sync", "dma", waits=w, inc="wol", incv=16,
                 out=WOSCR[j % 2][:, :], in_=wts["wo"][j * 128:(j + 1) * 128, :])
        par = st_tr[0]; st_tr[0] ^= 1
        base = par * 1024
        trl = None
        for i in range(8):
            w2 = [ld]
            if i == 0 and j < 2:
                w2 += [tr4_rel[0], tr4_rel[1]]
            if i == 0 and trset_rel[par]:
                w2.append(trset_rel[par])
            trl = add("tensor", "tr", waits=w2 if i == 0 else [ld], inc="wotr",
                      out=psA[:, base + i * 128: base + (i + 1) * 128],
                      in_=WOSCR[j % 2][:, i * 128:(i + 1) * 128],
                      identity=idfS[:, :])
        woscr_rel[j % 2] = trl
        ev = add("scalar", "copy", waits=[trl], inc="wot",
                 out=sbap((oA + 32 * KB) // 2 + j * 128, [[1024, 8], [1, 128]]),
                 in_=psA[:, base: base + 1024].rearrange("p (i c) -> p i c", i=8))
        trset_rel[par] = ev
    t_wo = now("wot")

    ffn_gate = None
    ffn_loads = []

    ln_ysum, ln_qsum = lnS[:, 0:1], lnS[:, 1:2]
    ln_nm, ln_ex2 = lnS[:, 2:3], lnS[:, 3:4]
    ln_m2, ln_vpe = lnS[:, 4:5], lnS[:, 5:6]
    ln_rv, ln_inv = lnS[:, 6:7], lnS[:, 7:8]
    ln_nmi = lnS[:, 8:9]

    def ln_chain(par, resid_in1, apply_out, extra_w=()):
        t1 = add("vector", "stt", waits=list(extra_w), inc="dve",
                 out=YTOK[par][:, :],
                 in0=psA[:, 2048 + par * 1024: 2048 + par * 1024 + 1024],
                 s1=cstS[:, 3:4], in1=resid_in1, op0=ALU.mult, op1=ALU.add,
                 accum_out=ln_ysum)
        t2 = add("scalar", "act", waits=[t1], inc="sq", out=SQ[:, :],
                 in_=YTOK[par][:, :], func=AF.Square, accum_out=ln_qsum)
        e = add("vector", "ts", waits=[t2], inc="dve", out=ln_nm, in0=ln_ysum,
                s1=cstS[:, 4:5], s2=None, op0=ALU.mult, op1=None)
        e2 = add("vector", "ts", inc="dve", out=ln_ex2, in0=ln_qsum,
                 s1=cstS[:, 1:2], s2=None, op0=ALU.mult, op1=None)
        add("vector", "wait", waits=[e2])
        e = add("vector", "tt", inc="dve", out=ln_m2, in0=ln_nm, in1=ln_nm,
                op=ALU.mult)
        add("vector", "wait", waits=[e])
        e = add("vector", "tt", inc="dve", out=ln_vpe, in0=ln_ex2, in1=ln_m2,
                op=ALU.subtract)
        add("vector", "wait", waits=[e])
        e = add("vector", "ts", inc="dve", out=ln_vpe, in0=ln_vpe,
                s1=cstS[:, 2:3], s2=None, op0=ALU.add, op1=None)
        add("vector", "wait", waits=[e])
        t3 = add("vector", "recip", inc="dve", out=ln_rv, in_=ln_vpe)
        t4 = add("scalar", "act", waits=[t3], inc="sq", out=ln_inv, in_=ln_rv,
                 func=AF.Sqrt)
        t5 = add("vector", "tt", waits=[t4], inc="dve", out=ln_nmi, in0=ln_nm,
                 in1=ln_inv, op=ALU.mult)
        add("vector", "wait", waits=[t5])
        ap1 = add("vector", "ts", inc="dve", out=apply_out, in0=YTOK[par][:, :],
                  s1=ln_inv, s2=ln_nmi, op0=ALU.mult, op1=ALU.add)
        return t1, ap1

    yev_rel = [None, None]
    t2_rel = [None, None]
    xtok_rel = [None, None]
    ytok_rel = [None, None]
    y1ev_rel = None
    xr_tok = [None] * NQ
    last_womm = None
    for g4 in range(4):
        for jj in range(8):
            wset = jj % 2
            w = [t_wo] if (g4 == 0 and jj == 0) else []
            if g4 == 0 and jj == 0:
                w.append(tr4_rel[0])
                w.append(tr4_rel[1])
            if yev_rel[wset]:
                w.append(yev_rel[wset])
            for i in range(8):
                last_womm = add("tensor", "mm", waits=w if i == 0 else [],
                                inc="womm" if i == 7 else None,
                                out=psA[:, wset * 512: wset * 512 + 512],
                                lhsT=WT_PH4[:, i * 1024 + jj * 128: i * 1024 + (jj + 1) * 128],
                                rhs=ATT[:, i * 2048 + g4 * 512: i * 2048 + (g4 + 1) * 512],
                                start=(i == 0), stop=(i == 7))
            ev = add("scalar", "copy",
                     waits=[now("womm")] + ([("t2", 32 * g4)] if g4 > 0 else []),
                     inc="yev",
                     out=YAT[:, jj * 512:(jj + 1) * 512],
                     in_=psA[:, wset * 512: wset * 512 + 512])
            yev_rel[wset] = ev
        t_ya = now("yev")
        for tau in range(4):
            tloc = g4 * 4 + tau
            par = tloc % 2
            w = [t_atev_all] if tloc < 2 else []
            if xtok_rel[par]:
                w.append(xtok_rel[par])
            xr_tok[tloc] = add("sync", "dma", waits=w, inc="xr", incv=16,
                               out=XTOK[par][:, :],
                               in_=xq[tloc * 128:(tloc + 1) * 128, :])
            w = [t_ya]
            if t2_rel[par]:
                w.append(t2_rel[par])
            if tloc == 0:
                w.append(evac_rel)
            trl = None
            for cch in range(8):
                trl = add("tensor", "tr", waits=w if cch == 0 else [], inc="t2",
                          out=psA[:, 2048 + par * 1024 + cch * 128:
                                  2048 + par * 1024 + (cch + 1) * 128],
                          in_=YAT[:, cch * 512 + tau * 128: cch * 512 + (tau + 1) * 128],
                          identity=idfS[:, :])
            extra = [trl, xr_tok[tloc]]
            if ytok_rel[par]:
                extra.append(ytok_rel[par])
            t1, ap1 = ln_chain(par, XTOK[par][:, :],
                               Y1[:, tloc * 1024:(tloc + 1) * 1024], extra)
            t2_rel[par] = t1
            xtok_rel[par] = t1
            if DBG and tloc < 2:
                off = tloc * (3 * 1024 + 16)
                dsn = add("sync", "dma", waits=[ap1], inc="dbl", incv=16,
                          out=dbgl[:, off:off + 16], in_=lnS[:, 0:16])
                dsn = add("sync", "dma", inc="dbl", incv=16,
                          out=dbgl[:, off + 16:off + 1040], in_=XTOK[par][:, :])
                dsn = add("sync", "dma", inc="dbl", incv=16,
                          out=dbgl[:, off + 1040:off + 2064], in_=YTOK[par][:, :])
                dsn = add("sync", "dma", inc="dbl", incv=16,
                          out=dbgl[:, off + 2064:off + 3088],
                          in_=Y1[:, tloc * 1024:(tloc + 1) * 1024])
                sch.dbg_gate = dsn
            ytok_rel[par] = ap1
            trl2 = None
            for cch in range(8):
                w2 = []
                if cch == 0:
                    w2 = [ap1]
                    if DBG and tloc < 2:
                        w2.append(sch.dbg_gate)
                    if y1ev_rel:
                        w2.append(y1ev_rel)
                    elif tloc == 0:
                        w2.append(tr4_rel[1])
                trl2 = add("tensor", "tr", waits=w2, inc="y1tr",
                           out=psA[:, 1024 + cch * 128: 1024 + (cch + 1) * 128],
                           in_=Y1[:, tloc * 1024 + cch * 128: tloc * 1024 + (cch + 1) * 128],
                           identity=idfS[:, :])
            y1ev_rel = add("scalar", "copy", waits=[trl2], inc="y1t",
                           out=sbap(oC // 2 + tloc * 128, [[2048, 8], [1, 128]]),
                           in_=psA[:, 1024:2048].rearrange("p (c q) -> p c q", c=8))
    t_y1t = now("y1t")
    t_womm_all = now("womm")
    t_t2_all = now("t2")

    t_dbgc = None
    if DBG:
        t_dbgc = add("sync", "dma", waits=[t_womm_all, t_t2_all,
                                           now("y1t")],
                     inc="dbc", incv=16, out=dbgc[:, :], in_=arA[:, 0:103 * 1024])
    w = [t_womm_all, t_t2_all, t_y1t]
    if DBG:
        w.append(t_dbgc)
    f_tok = None
    for i in range(8):
        f_tok = add("gpsimd", "dma", waits=w if i == 0 else [], inc="fw",
                    incv=16, out=U1B[:, i * 256:(i + 1) * 256],
                    in_=u1[i * 128:(i + 1) * 128, :])
    for r in range(2):
        f_tok = add("gpsimd", "dma", inc="fw", incv=16,
                    out=V1B[:, r * 4096:(r + 1) * 4096],
                    in_=v1[r * 128:(r + 1) * 128, :])
    for r in range(2):
        f_tok = add("gpsimd", "dma", inc="fw", incv=16,
                    out=V2B[:, r * 1024:(r + 1) * 1024],
                    in_=v2[r * 128:(r + 1) * 128, :])
    for i in range(32):
        dstw = U2Ba if i < 16 else U2Bb
        f_tok = add("gpsimd", "dma", inc="fw", incv=16,
                    out=dstw[:, (i % 16) * 256:(i % 16 + 1) * 256],
                    in_=u2[i * 128:(i + 1) * 128, :])
    t_fw = now("fw")

    def u2slice(fc, r):
        t_ = U2Ba if fc < 16 else U2Bb
        return t_[:, (fc % 16) * 256 + r * 128: (fc % 16) * 256 + (r + 1) * 128]

    pbank_rel = [None, None]
    hbank_rel = [None, None]
    t3_rel = [None, None]
    str_rel = [None, None]
    for g4 in range(4):
        for r in range(2):
            pset = r % 2
            w = [t_fw, t_y1t] if (g4 == 0 and r == 0) else []
            if pbank_rel[pset]:
                w.append(pbank_rel[pset])
            elif g4 == 0:
                w.append(yev_rel[pset])
            for i in range(8):
                add("tensor", "mm", waits=w if i == 0 else [],
                    inc="pmm" if i == 7 else None,
                    out=psA[:, pset * 512: pset * 512 + 512],
                    lhsT=U1B[:, i * 256 + r * 128: i * 256 + (r + 1) * 128],
                    rhs=Y1T[:, i * 2048 + g4 * 512: i * 2048 + (g4 + 1) * 512],
                    start=(i == 0), stop=(i == 7))
            ev = add("scalar", "copy", waits=[now("pmm")], inc="pev",
                     out=PT[:, r * 512:(r + 1) * 512],
                     in_=psA[:, pset * 512: pset * 512 + 512])
            pbank_rel[pset] = ev
        t_pt = now("pev")
        ht_rel = [None] * 4
        for fc in range(32):
            hset = fc % 2
            w = [t_pt] if fc < 2 else []
            if hbank_rel[hset]:
                w.append(hbank_rel[hset])
            for r in range(2):
                add("tensor", "mm", waits=w if r == 0 else [],
                    inc="hmm" if r == 1 else None,
                    out=psA[:, (2 + hset) * 512: (2 + hset) * 512 + 512],
                    lhsT=V1B[:, r * 4096 + fc * 128: r * 4096 + (fc + 1) * 128],
                    rhs=PT[:, r * 512:(r + 1) * 512],
                    start=(r == 0), stop=(r == 1))
            wg = [now("hmm")]
            if ht_rel[fc % 4]:
                wg.append(ht_rel[fc % 4])
            ev = add("scalar", "act", waits=wg, inc="hev",
                     out=HTr[:, (fc % 4) * 512: (fc % 4 + 1) * 512],
                     in_=psA[:, (2 + hset) * 512: (2 + hset) * 512 + 512],
                     func=AF.Gelu_apprx_tanh, bias=b1cS[:, fc:fc + 1],
                     scale=cstS[:, 3:4])
            hbank_rel[hset] = ev
            wq2 = [ev]
            if fc == 0:
                if pbank_rel[0]:
                    wq2.append(pbank_rel[0])
                if pbank_rel[1]:
                    wq2.append(pbank_rel[1])
            last_q2 = None
            for r in range(2):
                last_q2 = add("tensor", "mm", waits=wq2 if r == 0 else [],
                              inc="q2p" if r == 1 else None,
                              out=psA[:, r * 512: r * 512 + 512],
                              lhsT=u2slice(fc, r),
                              rhs=HTr[:, (fc % 4) * 512: (fc % 4 + 1) * 512],
                              start=(fc == 0), stop=(fc == 31))
            ht_rel[fc % 4] = last_q2
        for r in range(2):
            ev = add("scalar", "copy", waits=[now("q2p")], inc="q2ev",
                     out=Q2T[:, r * 512:(r + 1) * 512],
                     in_=psA[:, r * 512: r * 512 + 512])
            pbank_rel[r] = ev
        t_q2 = now("q2ev")
        for jj in range(8):
            oset = jj % 2
            w = [t_q2]
            if hbank_rel[oset]:
                w.append(hbank_rel[oset])
            for r in range(2):
                add("tensor", "mm", waits=w if r == 0 else [],
                    inc="omm" if r == 1 else None,
                    out=psA[:, (2 + oset) * 512: (2 + oset) * 512 + 512],
                    lhsT=V2B[:, r * 1024 + jj * 128: r * 1024 + (jj + 1) * 128],
                    rhs=Q2T[:, r * 512:(r + 1) * 512],
                    start=(r == 0), stop=(r == 1))
            ev = add("vector", "ts",
                     waits=[now("omm")] + ([("t3", 32 * g4)] if g4 > 0 else []),
                     inc="oev",
                     out=FFT[:, jj * 512:(jj + 1) * 512],
                     in0=psA[:, (2 + oset) * 512: (2 + oset) * 512 + 512],
                     s1=b2cS[:, jj:jj + 1], s2=None, op0=ALU.add, op1=None)
            hbank_rel[oset] = ev
        t_ff = now("oev")
        for tau in range(4):
            tloc = g4 * 4 + tau
            par = tloc % 2
            w = [t_ff]
            if t3_rel[par]:
                w.append(t3_rel[par])
            elif g4 == 0:
                w.append(t2_rel[par])
            trl = None
            for cch in range(8):
                trl = add("tensor", "tr", waits=w if cch == 0 else [], inc="t3",
                          out=psA[:, 2048 + par * 1024 + cch * 128:
                                  2048 + par * 1024 + (cch + 1) * 128],
                          in_=FFT[:, cch * 512 + tau * 128: cch * 512 + (tau + 1) * 128],
                          identity=idfS[:, :])
            extra = [trl]
            if ytok_rel[par]:
                extra.append(ytok_rel[par])
            t1, ap1 = ln_chain(par, Y1[:, tloc * 1024:(tloc + 1) * 1024],
                               OTOK[par][:, :],
                               extra + ([str_rel[par]] if str_rel[par] else []))
            t3_rel[par] = t1
            ytok_rel[par] = ap1
            st_tok = add("sync", "dma", waits=[ap1], inc="st", incv=16,
                         out=out_d[tloc * 128:(tloc + 1) * 128, :],
                         in_=OTOK[par][:, :])
            str_rel[par] = st_tok
    add("sync", "wait", waits=[now("st")])

    semnames = sorted(sch.cnt.keys())
    sems = {}
    for n in semnames:
        g = nc.semaphore(n)
        sems[n] = g.__enter__()
        nc._sem_guards = getattr(nc, "_sem_guards", [])
        nc._sem_guards.append(g)

    def run_ops(eng, name):
        waited = {}
        for op in sch.ops[name]:
            for (sname, v) in op["waits"]:
                if v > waited.get(sname, 0):
                    eng.wait_ge(sems[sname], v)
                    waited[sname] = v
            k, kw = op["kind"], op["kw"]
            if k == "wait":
                continue
            if k == "dma":
                ins = eng.dma_start(out=kw["out"], in_=kw["in_"],
                                    allow_slow_non_contiguous=kw.get("slow", False))
            elif k == "dmaT":
                ins = eng.dma_start(out=kw["out"], in_=kw["in_"], transpose=True)
            elif k == "mm":
                ins = eng.matmul(kw["out"], kw["lhsT"], kw["rhs"],
                                 start=kw["start"], stop=kw["stop"])
            elif k == "tr":
                ins = eng.transpose(kw["out"], kw["in_"], kw["identity"])
            elif k == "act":
                a = {}
                for f in ("bias", "scale", "accum_out"):
                    if f in kw and kw[f] is not None:
                        a[f] = kw[f]
                ins = eng.activation(kw["out"], kw["in_"], kw["func"], **a)
            elif k == "copy":
                ins = (eng.copy(kw["out"], kw["in_"]) if name == "scalar"
                       else eng.tensor_copy(kw["out"], kw["in_"]))
            elif k == "tt":
                ins = eng.tensor_tensor(kw["out"], kw["in0"], kw["in1"], kw["op"])
            elif k == "ts":
                if kw.get("s2") is None:
                    ins = eng.tensor_scalar(kw["out"], kw["in0"], kw["s1"],
                                            None, kw["op0"])
                else:
                    ins = eng.tensor_scalar(kw["out"], kw["in0"], kw["s1"],
                                            kw["s2"], kw["op0"], kw["op1"])
            elif k == "stt":
                ins = eng.scalar_tensor_tensor(kw["out"], kw["in0"], kw["s1"],
                                               kw["in1"], kw["op0"], kw["op1"],
                                               accum_out=kw.get("accum_out"))
            elif k == "recip":
                ins = eng.reciprocal(kw["out"], kw["in_"])
            elif k == "memset":
                ins = eng.memset(kw["ap"], kw["value"])
            else:
                raise ValueError(k)
            if op["inc"] is not None:
                ins.then_inc(sems[op["inc"]], op["incv"])

    with nc.Block() as block:
        @block.sync
        def _(e):
            run_ops(e, "sync")

        @block.gpsimd
        def _(e):
            run_ops(e, "gpsimd")

        @block.tensor
        def _(e):
            run_ops(e, "tensor")

        @block.vector
        def _(e):
            run_ops(e, "vector")

        @block.scalar
        def _(e):
            run_ops(e, "scalar")

    return nc


def emit_pv(m, h, pend, exp_tok, add, now, psA, ESB, Vsb, evac_rel, v_gate):
    _, _, s, gs = pend
    par_e = gs % 4
    gate = exp_tok[gs]
    for li in range(2):
        l = 2 * s + li
        pk = slot2phys(m, l)
        for tau in range(4):
            w = []
            if li == 0 and tau == 0:
                w.append(gate)
                if evac_rel:
                    w.append(evac_rel)
                if v_gate:
                    w.extend(v_gate)
            add("tensor", "mm", waits=w,
                inc="pvd" if (li == 1 and tau == 3) else None,
                out=psA[:, 2048 + tau * 512: 2048 + tau * 512 + 65],
                lhsT=ESB[:, par_e * 1024 + li * 512 + tau * 128:
                         par_e * 1024 + li * 512 + (tau + 1) * 128],
                rhs=Vsb[:, pk * 1040 + 65 * h: pk * 1040 + 65 * h + 65],
                start=(l == 0), stop=(l == P[m] - 1))


_CACHE = {}


def kernel(L0_attn_in_features, L0_attn_Wq, L0_attn_Wk, L0_attn_Wv, L0_attn_Wo,
           L0_ffn_U1, L0_ffn_V1, L0_ffn_b1, L0_ffn_U2, L0_ffn_V2, L0_ffn_b2):
    x = np.ascontiguousarray(np.asarray(L0_attn_in_features, dtype=np.float32))
    Ws = {k: np.ascontiguousarray(np.asarray(v, dtype=np.float32))
          for k, v in (("wq", L0_attn_Wq), ("wk", L0_attn_Wk),
                       ("wv", L0_attn_Wv), ("wo", L0_attn_Wo),
                       ("u1", L0_ffn_U1), ("v1", L0_ffn_V1),
                       ("u2", L0_ffn_U2), ("v2", L0_ffn_V2))}
    b1v = np.ascontiguousarray(np.asarray(L0_ffn_b1, dtype=np.float32))
    b2v = np.ascontiguousarray(np.asarray(L0_ffn_b2, dtype=np.float32))

    cstv = np.zeros((128, 8), np.float32)
    cstv[:, 0] = 0.125
    cstv[:, 1] = 1.0 / 1024.0
    cstv[:, 2] = 1.0e-5
    cstv[:, 3] = 1.0
    cstv[:, 4] = -1.0 / 1024.0
    atlv = np.zeros((128, 7 * 128), np.float32)
    kk, qq = np.meshgrid(np.arange(128), np.arange(128), indexing="ij")
    atlv[:, 3 * 128:4 * 128] = (kk <= qq).astype(np.float32)
    atlv[:, 4 * 128:] = 1.0
    atlv = atlv.astype(bf16)
    idfv = np.eye(128, dtype=np.float32)
    idbv = idfv.astype(bf16)

    in_maps, meta = [], []
    for c in range(8):
        seq, role = c // 2, c % 2
        phys, _N = phys_order(role)
        qtiles = [g0 + t for g0 in GQ_ROLE[role] for t in range(4)]
        xs = x[seq]
        xkv_rows = np.concatenate(
            [xs[max(g, 0) * 128: max(g, 0) * 128 + 128] for g in phys], axis=0)
        xq_rows = np.concatenate(
            [xs[g * 128:(g + 1) * 128] for g in qtiles], axis=0)
        live = kill_pairs(role)
        bkv = np.full((128, NPAIR), EXPBIAS, np.float32)
        bkv[:, ~live] = KILL
        im = {"xkv": np.ascontiguousarray(xkv_rows),
              "xq": np.ascontiguousarray(xq_rows),
              "b1": b1v, "b2": b2v, "cst": cstv, "bk": bkv, "atl": atlv,
              "idf": idfv, "idb": idbv}
        im.update(Ws)
        in_maps.append(im)
        meta.append((seq, qtiles))

    if "nc" not in _CACHE:
        _CACHE["nc"] = build_program()
    res = run_bass_kernel_spmd(_CACHE["nc"], in_maps, core_ids=list(range(8)))

    out = np.empty((B, S, D), np.float32)
    for c in range(8):
        seq, qtiles = meta[c]
        oc = np.asarray(res.results[c]["out"])
        for li, g in enumerate(qtiles):
            out[seq, g * 128:(g + 1) * 128] = oc[li * 128:(li + 1) * 128]
    return out


if __name__ == "__main__":
    nc = build_program()
    n = sum(len(v) for v in [])
    print("build ok")


# version 15
# speedup vs baseline: 1.8097x; 1.2757x over previous
import numpy as np
import ml_dtypes
import concourse.bass as bass
import concourse.mybir as mybir
from concourse.bass_utils import run_bass_kernel_spmd

F32 = mybir.dt.float32
BF16 = mybir.dt.bfloat16
AF = mybir.ActivationFunctionType
ALU = mybir.AluOpType
bf16 = ml_dtypes.bfloat16

B, S, D, H, DH = 4, 4096, 1024, 16, 64
DFF, RANK = 4096, 256
NT = S // 128
NQ = 16
P = [8, 16, 24, 32]
SM = [p // 2 for p in P]
NPAIR = sum(SM)
SMQ = [p // 4 for p in P]
NQUAD = sum(SMQ)
NSTEP = 16 * NPAIR
EXPBIAS = -20.0
KILL = -1.0e30
GQ_ROLE = [[0, 8, 20, 28], [4, 12, 16, 24]]
KB = 1024


def slot2phys(m, l):
    return P[m] - 4 + l if l < 4 else l - 4


def phys_order(role):
    gq = GQ_ROLE[role]
    N = [g + 4 for g in gq]
    phys = [-1] * NT
    placed = set()
    for m in range(4):
        for t in range(4):
            phys[P[m] - 4 + t] = N[m] - 4 + t
            placed.add(N[m] - 4 + t)
    rest = [g for g in range(max(N)) if g not in placed]
    for i in range(NT):
        if phys[i] < 0:
            phys[i] = rest.pop(0) if rest else -1
    return phys, N


def kill_quads(role):
    phys, N = phys_order(role)
    live = np.zeros(NQUAD, dtype=bool)
    idx = 0
    for m in range(4):
        for qd in range(SMQ[m]):
            l = 4 * qd
            if l < 4:
                live[idx] = True
            else:
                g = phys[slot2phys(m, l)]
                live[idx] = (g >= 0) and (g <= N[m] - 5)
            idx += 1
    return live


DBG = False
ENGINES = ("sync", "gpsimd", "tensor", "vector", "scalar")


class Sched:
    def __init__(self):
        self.ops = {e: [] for e in ENGINES}
        self.cnt = {}

    def add(self, eng, kind, waits=(), inc=None, incv=1, **kw):
        self.ops[eng].append(dict(kind=kind, waits=[w for w in waits if w],
                                  inc=inc, incv=incv, kw=kw))
        if inc is not None:
            self.cnt[inc] = self.cnt.get(inc, 0) + incv
            return (inc, self.cnt[inc])
        return None

    def now(self, name):
        return (name, self.cnt.get(name, 0))


def build_program():
    nc = bass.Bass("TRN2", target_bir_lowering=False,
                   detect_race_conditions=False)

    xkv = nc.dram_tensor("xkv", [NT * 128, D], F32, kind="ExternalInput").ap()
    xq = nc.dram_tensor("xq", [NQ * 128, D], F32, kind="ExternalInput").ap()
    wts = {n: nc.dram_tensor(n, [D, D], F32, kind="ExternalInput").ap()
           for n in ("wq", "wk", "wv", "wo")}
    u1 = nc.dram_tensor("u1", [D, RANK], F32, kind="ExternalInput").ap()
    v1 = nc.dram_tensor("v1", [RANK, DFF], F32, kind="ExternalInput").ap()
    b1 = nc.dram_tensor("b1", [DFF], F32, kind="ExternalInput").ap()
    u2 = nc.dram_tensor("u2", [DFF, RANK], F32, kind="ExternalInput").ap()
    v2 = nc.dram_tensor("v2", [RANK, D], F32, kind="ExternalInput").ap()
    b2 = nc.dram_tensor("b2", [D], F32, kind="ExternalInput").ap()
    cst = nc.dram_tensor("cst", [128, 8], F32, kind="ExternalInput").ap()
    bkd = nc.dram_tensor("bk", [128, NQUAD], F32, kind="ExternalInput").ap()
    atl = nc.dram_tensor("atl", [128, 7 * 128], BF16, kind="ExternalInput").ap()
    idf_d = nc.dram_tensor("idf", [128, 128], F32, kind="ExternalInput").ap()
    idb_d = nc.dram_tensor("idb", [128, 128], BF16, kind="ExternalInput").ap()
    out_d = nc.dram_tensor("out", [NQ * 128, D], F32, kind="ExternalOutput").ap()
    if DBG:
        dbga = nc.dram_tensor("dbga", [128, 103 * 1024], BF16, kind="ExternalOutput").ap()
        dbgc = nc.dram_tensor("dbgc", [128, 103 * 1024], BF16, kind="ExternalOutput").ap()
        dbgl = nc.dram_tensor("dbgl", [128, 2 * 3 * 1024 + 32], F32, kind="ExternalOutput").ap()

    AR = 207 * KB
    arena = nc.alloc_sbuf_tensor("arena", [128, AR // 2], BF16)
    arA = arena.ap()

    def bfv(off_b, cols):
        assert off_b % 2 == 0 and off_b + 2 * cols <= AR, (off_b, cols)
        return arA[:, off_b // 2: off_b // 2 + cols]

    PSTEP = AR // 2

    def sbap(off_el, dims):
        return bass.AP(arena, off_el, [[PSTEP, 128]] + dims)

    def f32v(off_b, cols):
        assert off_b % 4 == 0 and off_b + 4 * cols <= AR, (off_b, cols)
        return arA[:, off_b // 2: off_b // 2 + 2 * cols].bitcast(F32)

    o_cst, o_bk, o_atl = 0, 64, 256
    o_idb, o_idf = 2048, 2304
    o_b1c, o_b2c, o_rec, o_ln = 2816, 2944, 2976, 3072
    oA = 3200
    oB = oA + 64 * KB
    oC = oB + 65 * KB
    oD = oC + 32 * KB
    TOT = oD + 42 * KB
    assert TOT <= AR, (TOT, AR)

    cstS = f32v(o_cst, 8)
    bkS = f32v(o_bk, NQUAD)
    atlS = bfv(o_atl, 7 * 128)
    idbS = bfv(o_idb, 128)
    idfS = f32v(o_idf, 128)
    b1cS = f32v(o_b1c, 32)
    b2cS = f32v(o_b2c, 8)
    recS = f32v(o_rec, 8)
    lnS = f32v(o_ln, 32)

    KTv = bfv(oA, 32 * KB)
    Vsb = bfv(oB, 33280)
    QTv = bfv(oC, 16 * KB)
    ESB = bfv(oD, 3 * KB)
    PVS = f32v(oD + 6 * KB, KB)
    ATOK = bfv(oD + 10 * KB, 16 * KB)
    XTOK = [f32v(oD, KB), f32v(oD + 4 * KB, KB)]
    YTOK = [f32v(oD + 10 * KB, KB), f32v(oD + 14 * KB, KB)]
    SQ = f32v(oD + 18 * KB, KB)
    SQ = [SQ, SQ]
    WOSCR = [f32v(oD + 22 * KB, KB), f32v(oD + 26 * KB, KB)]
    OTOK = [f32v(oD + 30 * KB, KB), f32v(oD + 34 * KB, KB)]
    PT = bfv(oD + 38 * KB, KB)
    Q2T = bfv(oD + 40 * KB, KB)
    ATT = bfv(oA, 16 * KB)
    WoT = bfv(oA + 32 * KB, 8 * KB)
    YAT = f32v(oA + 48 * KB, 4 * KB)
    U1B = bfv(oA, 2 * KB)
    V1B = bfv(oA + 4 * KB, 8 * KB)
    V2B = bfv(oA + 20 * KB, 2 * KB)
    U2Bb = bfv(oA + 24 * KB, 4 * KB)
    FFT = f32v(oA + 32 * KB, 4 * KB)
    HTr = bfv(oA + 48 * KB, 2 * KB)
    U2Ba = bfv(oD, 4 * KB)
    PSP = 4096
    XTG = bfv(oD + 4 * KB, 8 * KB)
    WT = bfv(oD + 20 * KB, 8 * KB)

    Y1 = f32v(oB, 16 * KB)
    Y1T = bfv(oC, 16 * KB)

    ps = nc.alloc_psum_tensor("ps", [128, 4096], F32)
    psA = ps.ap()
    psB = psA.bitcast(BF16)

    sch = Sched()
    add, now = sch.add, sch.now

    for (dst, src) in ((cstS, cst[:, :]), (bkS, bkd[:, :]), (atlS, atl[:, :]),
                       (idbS, idb_d[:, :]), (idfS, idf_d[:, :]),
                       (b1cS[:, 0:32], b1.rearrange("(c p) -> p c", p=128)),
                       (b2cS[:, 0:8], b2.rearrange("(c p) -> p c", p=128))):
        add("sync", "dma", inc="c", incv=16, out=dst, in_=src, slow=True)
    t_c = now("c")
    t_ones = add("vector", "memset", inc="ones", ap=Vsb, value=1.0)

    SCRF = [f32v(oD + i * 4 * KB, KB) for i in range(2)]
    XTGn = bfv(oD + 8 * KB, 8 * KB)
    WKT = bfv(oD + 24 * KB, 8 * KB)
    WVT = bfv(oC, 8 * KB)

    scrf_rel = [None, None]
    trset_rel = [None, None]
    mmset_rel = [None, None]
    st_tr = [0]
    st_mm = [0]

    def tile_load_transpose(xsrc_row_ap, slot, gates_ld=(), gates_tr=(),
                            evac_dst=None, evac_gates=()):
        w = [scrf_rel[slot]] if scrf_rel[slot] else []
        w += list(gates_ld)
        ld = add("sync", "dma", waits=w, inc="xld", incv=16,
                 out=SCRF[slot][:, :], in_=xsrc_row_ap)
        par = st_tr[0]; st_tr[0] ^= 1
        base = par * 1024
        trl = None
        for i in range(8):
            w2 = [ld] + (list(gates_tr) if i == 0 else [])
            if i == 0 and trset_rel[par]:
                w2.append(trset_rel[par])
            trl = add("tensor", "tr", waits=w2, inc="xtr",
                      out=psA[:, base + i * 128: base + (i + 1) * 128],
                      in_=SCRF[slot][:, i * 128:(i + 1) * 128],
                      identity=idfS[:, :])
        scrf_rel[slot] = trl
        ev = add("scalar", "copy", waits=[trl] + list(evac_gates), inc="xev",
                 out=evac_dst,
                 in_=psA[:, base: base + 1024].rearrange("p (i c) -> p i c", i=8))
        trset_rel[par] = ev
        return ev

    def wt_prep(wsrc, dst_off_el, tag, gates_ld=(), gates_tr=(), evac_gates=()):
        ev = None
        for j in range(8):
            ev = tile_load_transpose(
                wsrc[j * 128:(j + 1) * 128, :], j % 2,
                gates_ld=gates_ld if j < 2 else (),
                gates_tr=gates_tr if j == 0 else (),
                evac_dst=sbap(dst_off_el + j * 128, [[1024, 8], [1, 128]]),
                evac_gates=evac_gates if j == 0 else ())
        return ev

    t_wk = wt_prep(wts["wk"], (oD + 24 * KB) // 2, "wk", gates_ld=[t_c])
    t_wv = wt_prep(wts["wv"], oC // 2, "wv")

    kbank = [None, None]
    xtg_rel = None
    for g in range(4):
        evs = []
        for t in range(8):
            gt = g * 8 + t
            dst = sbap((oD + 8 * KB) // 2 + t * 128, [[1024, 8], [1, 128]])
            egates = [xtg_rel] if (t == 0 and xtg_rel) else []
            evs.append(tile_load_transpose(
                xkv[gt * 128:(gt + 1) * 128, :], gt % 2,
                gates_tr=[t_wk, t_wv] if gt == 0 else (),
                evac_dst=dst, evac_gates=egates))
        t_xt = evs[-1]
        last = None
        for j in range(8):
            par = st_mm[0]; st_mm[0] ^= 1
            base = 2048 + par * 1024
            w = [t_xt]
            if kbank[par]:
                w.append(kbank[par])
            for i in range(8):
                for hf in range(2):
                    last = add("tensor", "mm",
                               waits=w if (i == 0 and hf == 0) else [],
                               inc="kmm" if (i == 7 and hf == 1) else None,
                               out=psA[:, base + hf * 512: base + hf * 512 + 512],
                               lhsT=WKT[:, i * 1024 + j * 128: i * 1024 + (j + 1) * 128],
                               rhs=XTGn[:, i * 1024 + hf * 512: i * 1024 + hf * 512 + 512],
                               start=(i == 0), stop=(i == 7))
            ev = add("scalar", "copy", waits=[now("kmm")], inc="kev",
                     out=KTv[:, j * 4096 + g * 1024: j * 4096 + (g + 1) * 1024],
                     in_=psA[:, base: base + 1024])
            kbank[par] = ev
        for t in range(8):
            gt = g * 8 + t
            par = st_mm[0]; st_mm[0] ^= 1
            base = 2048 + par * 1024
            w = [t_xt, t_ones] if gt == 0 else [t_xt]
            if kbank[par]:
                w.append(kbank[par])
            for i in range(8):
                for hf in range(2):
                    last = add("tensor", "mm",
                               waits=w if (i == 0 and hf == 0) else [],
                               inc="vmm" if (i == 7 and hf == 1) else None,
                               out=psA[:, base + hf * 512: base + hf * 512 + 512],
                               lhsT=XTGn[:, i * 1024 + t * 128: i * 1024 + (t + 1) * 128],
                               rhs=WVT[:, i * 1024 + hf * 512: i * 1024 + hf * 512 + 512],
                               start=(i == 0), stop=(i == 7))
            dstv = sbap(oB // 2 + gt * 1040, [[65, 16], [1, 64]])
            srcv = psA[:, base: base + 1024].rearrange("p (h d) -> p h d", h=16)
            ev = add("vector", "ts", waits=[now("vmm")], inc="vev",
                     out=dstv, in0=srcv, s1=cstS[:, 3:4], s2=None,
                     op0=ALU.mult, op1=None)
            kbank[par] = ev
        xtg_rel = last
    t_kmm_all = now("kmm")
    t_kev_all = now("kev")
    t_vmm_all = now("vmm")
    t_vev_all = now("vev")

    t_wq = wt_prep(wts["wq"], (oD + 24 * KB) // 2, "wq",
                   gates_tr=[t_kmm_all])
    qbank = kbank
    for g in range(2):
        evs = []
        for t in range(8):
            gt = g * 8 + t
            dst = sbap((oD + 8 * KB) // 2 + t * 128, [[1024, 8], [1, 128]])
            egates = [xtg_rel] if (t == 0 and xtg_rel) else []
            evs.append(tile_load_transpose(
                xq[gt * 128:(gt + 1) * 128, :], gt % 2,
                gates_tr=[t_wq] if gt == 0 else (),
                evac_dst=dst, evac_gates=egates))
        t_xt = evs[-1]
        last = None
        for j in range(8):
            par = st_mm[0]; st_mm[0] ^= 1
            base = 2048 + par * 1024
            w = [t_xt]
            if qbank[par]:
                w.append(qbank[par])
            for i in range(8):
                for hf in range(2):
                    last = add("tensor", "mm",
                               waits=w if (i == 0 and hf == 0) else [],
                               inc="qmm" if (i == 7 and hf == 1) else None,
                               out=psA[:, base + hf * 512: base + hf * 512 + 512],
                               lhsT=WKT[:, i * 1024 + j * 128: i * 1024 + (j + 1) * 128],
                               rhs=XTGn[:, i * 1024 + hf * 512: i * 1024 + hf * 512 + 512],
                               start=(i == 0), stop=(i == 7))
            ev = add("scalar", "act",
                     waits=[now("qmm")] + ([t_vmm_all] if (g == 0 and j == 0) else []),
                     inc="qev",
                     out=QTv[:, j * 2048 + g * 1024: j * 2048 + (g + 1) * 1024],
                     in_=psA[:, base: base + 1024],
                     func=AF.Copy, scale=cstS[:, 0:1])
            qbank[par] = ev
        xtg_rel = last
    t_qev_all = now("qev")
    qbank_rel = qbank

    exp_tok = [None] * NSTEP
    pvd_tok = [None] * NSTEP
    evac_rel = None
    gstep = 0
    first_S = True
    for m in range(4):
        for hp in range(8):
            h0, h1 = 2 * hp, 2 * hp + 1
            j = hp
            pend = []
            for kc in range(P[m]):
                t = gstep
                par = t % 2
                pk = slot2phys(m, kc)
                w = []
                if first_S:
                    w = [t_qev_all, t_kev_all, qbank_rel[0], qbank_rel[1],
                         trset_rel[0], trset_rel[1]]
                    first_S = False
                if t >= 2:
                    w.append(("e", t - 1))
                add("tensor", "mm", waits=w, inc=None,
                    out=psA[:, par * 1024: par * 1024 + 512],
                    lhsT=KTv[0:64, j * 4096 + pk * 128: j * 4096 + (pk + 1) * 128],
                    rhs=QTv[0:64, j * 2048 + m * 512: j * 2048 + (m + 1) * 512],
                    start=True, stop=True, tile_position=(0, 0))
                add("tensor", "mm", waits=[], inc="smm",
                    out=psA[:, par * 1024 + 512: par * 1024 + 1024],
                    lhsT=KTv[64:128, j * 4096 + pk * 128: j * 4096 + (pk + 1) * 128],
                    rhs=QTv[64:128, j * 2048 + m * 512: j * 2048 + (m + 1) * 512],
                    start=True, stop=True, tile_position=(64, 0))
                par_e = t % 3
                wexp = [now("smm")]
                if t >= 3:
                    wexp.append(("pvd", t - 2))
                prq = sum(SMQ[:m]) + kc // 4
                exp_tok[t] = add("scalar", "act", waits=wexp, inc="e",
                                 out=ESB[:, par_e * 1024: par_e * 1024 + 1024],
                                 in_=psA[:, par * 1024: par * 1024 + 1024],
                                 func=AF.Exp, bias=bkS[:, prq:prq + 1])
                if kc < 4:
                    for hh in range(2):
                        add("vector", "tt",
                            waits=[exp_tok[t]] if hh == 0 else [],
                            inc="mk" if hh == 1 else None,
                            out=ESB[:, par_e * 1024 + hh * 512: par_e * 1024 + hh * 512 + 512],
                            in0=ESB[:, par_e * 1024 + hh * 512: par_e * 1024 + hh * 512 + 512],
                            in1=atlS[:, (3 - kc) * 128: (3 - kc) * 128 + 512],
                            op=ALU.mult)
                    exp_tok[t] = now("mk")
                pend.append((kc, t))
                gstep += 1
                while len(pend) > 2:
                    pkc, pt = pend.pop(0)
                    ppk = slot2phys(m, pkc)
                    ppe = pt % 3
                    for hh in range(2):
                        ww = [exp_tok[pt]] if hh == 0 else []
                        if hh == 0 and pkc == 0:
                            if evac_rel:
                                ww.append(evac_rel)
                            if m == 0 and hp == 0:
                                ww += [t_vev_all, t_ones]
                        add("tensor", "mm", waits=ww,
                            inc="pvd" if hh == 1 else None,
                            out=psA[0:65, 3072 + hh * 512: 3072 + hh * 512 + 512],
                            lhsT=Vsb[:, ppk * 1040 + 65 * (2 * hp + hh):
                                     ppk * 1040 + 65 * (2 * hp + hh) + 65],
                            rhs=ESB[:, ppe * 1024 + hh * 512: ppe * 1024 + hh * 512 + 512],
                            start=(pkc == 0), stop=(pkc == P[m] - 1))
                    pvd_tok[pt] = now("pvd")
            for (pkc, pt) in pend:
                ppk = slot2phys(m, pkc)
                ppe = pt % 3
                for hh in range(2):
                    ww = [exp_tok[pt]] if hh == 0 else []
                    if hh == 0 and pkc == 0:
                        if evac_rel:
                            ww.append(evac_rel)
                        if m == 0 and hp == 0:
                            ww += [t_vev_all, t_ones]
                    add("tensor", "mm", waits=ww,
                        inc="pvd" if hh == 1 else None,
                        out=psA[0:65, 3072 + hh * 512: 3072 + hh * 512 + 512],
                        lhsT=Vsb[:, ppk * 1040 + 65 * (2 * hp + hh):
                                 ppk * 1040 + 65 * (2 * hp + hh) + 65],
                        rhs=ESB[:, ppe * 1024 + hh * 512: ppe * 1024 + hh * 512 + 512],
                        start=(pkc == 0), stop=(pkc == P[m] - 1))
                pvd_tok[pt] = now("pvd")
            cp0 = add("vector", "copy", waits=[pvd_tok[gstep - 1]], inc="dve",
                      out=PVS[0:65, 0:512], in_=psA[0:65, 3072:3584])
            cp1 = add("vector", "copy", inc="dve",
                      out=PVS[0:65, 512:1024], in_=psA[0:65, 3584:4096])
            trn = None
            for hh in range(2):
                for tau in range(4):
                    ww = [cp1] if (hh == 0 and tau == 0) else []
                    trn = add("tensor", "tr", waits=ww, inc="ntr",
                              out=psA[:, 3072 + hh * 512 + tau * 65:
                                      3072 + hh * 512 + tau * 65 + 65],
                              in_=PVS[0:65, hh * 512 + tau * 128:
                                      hh * 512 + (tau + 1) * 128],
                              identity=idfS[0:65, 0:65])
            sums_ap = bass.AP(ps, 3072 + 64, [[PSP, 128], [512, 2], [65, 4], [1, 1]])
            rec_ap = recS[:, 0:8].rearrange("p (a r o) -> p a r o", a=2, o=1)
            rc = add("vector", "recip", waits=[trn], inc="dve",
                     out=rec_ap, in_=sums_ap)
            add("vector", "wait", waits=[rc])
            for hh in range(2):
                for tau in range(4):
                    add("vector", "ts",
                        inc="aev" if (hh == 1 and tau == 3) else None,
                        out=ATOK[:, (4 * m + tau) * 1024 + 64 * (2 * hp + hh):
                                 (4 * m + tau) * 1024 + 64 * (2 * hp + hh) + 64],
                        in0=psA[:, 3072 + hh * 512 + tau * 65:
                                3072 + hh * 512 + tau * 65 + 64],
                        s1=recS[:, hh * 4 + tau: hh * 4 + tau + 1], s2=None,
                        op0=ALU.mult, op1=None)
            evac_rel = now("aev")
    t_exp_all = now("e")
    t_aev_all = now("aev")
    t_smm_all = now("smm")
    t_dbga = None
    if DBG:
        t_dbga = add("sync", "dma", waits=[t_aev_all], inc="dba", incv=16,
                     out=dbga[:, :], in_=arA[:, 0:103 * 1024])

    tr4_rel = [None, None]
    for t in range(NQ):
        st_ = t % 2
        w = [t_aev_all, t_exp_all] if t == 0 else []
        if tr4_rel[st_]:
            w = w + [tr4_rel[st_]]
        trl = None
        for cch in range(8):
            trl = add("tensor", "tr", waits=w if cch == 0 else [], inc="tr4",
                      out=psB[:, st_ * 2048 + cch * 128: st_ * 2048 + (cch + 1) * 128],
                      in_=ATOK[:, t * 1024 + cch * 128: t * 1024 + (cch + 1) * 128],
                      identity=idbS[:, :])
        dst = sbap(oA // 2 + t * 128, [[2048, 8], [1, 128]])
        srcap = psB[:, st_ * 2048: st_ * 2048 + 1024].rearrange(
            "p (c q) -> p c q", c=8)
        ev = add("scalar", "copy",
                 waits=[trl, t_smm_all if t == 0 else None,
                        t_dbga if (DBG and t == 0) else None],
                 inc="atev", out=dst, in_=srcap)
        tr4_rel[st_] = ev
    t_atev_all = now("atev")

    WT_PH4 = WoT
    woscr_rel = [None, None]
    for j in range(8):
        w = [woscr_rel[j % 2]] if woscr_rel[j % 2] else [t_atev_all]
        ld = add("sync", "dma", waits=w, inc="wol", incv=16,
                 out=WOSCR[j % 2][:, :], in_=wts["wo"][j * 128:(j + 1) * 128, :])
        par = st_tr[0]; st_tr[0] ^= 1
        base = par * 1024
        trl = None
        for i in range(8):
            w2 = [ld]
            if i == 0 and j < 2:
                w2 += [tr4_rel[0], tr4_rel[1]]
            if i == 0 and trset_rel[par]:
                w2.append(trset_rel[par])
            trl = add("tensor", "tr", waits=w2 if i == 0 else [ld], inc="wotr",
                      out=psA[:, base + i * 128: base + (i + 1) * 128],
                      in_=WOSCR[j % 2][:, i * 128:(i + 1) * 128],
                      identity=idfS[:, :])
        woscr_rel[j % 2] = trl
        ev = add("scalar", "copy", waits=[trl], inc="wot",
                 out=sbap((oA + 32 * KB) // 2 + j * 128, [[1024, 8], [1, 128]]),
                 in_=psA[:, base: base + 1024].rearrange("p (i c) -> p i c", i=8))
        trset_rel[par] = ev
    t_wo = now("wot")

    ffn_gate = None
    ffn_loads = []

    def _lnslots(par):
        b = par * 16
        return [lnS[:, b + i: b + i + 1] for i in range(9)]

    def ln_chain(par, resid_in1, apply_out, extra_w=()):
        (ln_ysum, ln_qsum, ln_nm, ln_ex2, ln_m2, ln_vpe, ln_rv, ln_inv,
         ln_nmi) = _lnslots(par)
        t1 = add("vector", "stt", waits=list(extra_w), inc="dve",
                 out=YTOK[par][:, :],
                 in0=psA[:, 2048 + par * 1024: 2048 + par * 1024 + 1024],
                 s1=cstS[:, 3:4], in1=resid_in1, op0=ALU.mult, op1=ALU.add,
                 accum_out=ln_ysum)
        t2 = add("scalar", "act", waits=[t1], inc="sq", out=SQ[par][:, :],
                 in_=YTOK[par][:, :], func=AF.Square, accum_out=ln_qsum)
        e = add("vector", "ts", waits=[t2], inc="dve", out=ln_nm, in0=ln_ysum,
                s1=cstS[:, 4:5], s2=None, op0=ALU.mult, op1=None)
        e2 = add("vector", "ts", inc="dve", out=ln_ex2, in0=ln_qsum,
                 s1=cstS[:, 1:2], s2=None, op0=ALU.mult, op1=None)
        add("vector", "wait", waits=[e2])
        e = add("vector", "tt", inc="dve", out=ln_m2, in0=ln_nm, in1=ln_nm,
                op=ALU.mult)
        add("vector", "wait", waits=[e])
        e = add("vector", "tt", inc="dve", out=ln_vpe, in0=ln_ex2, in1=ln_m2,
                op=ALU.subtract)
        add("vector", "wait", waits=[e])
        e = add("vector", "ts", inc="dve", out=ln_vpe, in0=ln_vpe,
                s1=cstS[:, 2:3], s2=None, op0=ALU.add, op1=None)
        add("vector", "wait", waits=[e])
        t3 = add("vector", "recip", inc="dve", out=ln_rv, in_=ln_vpe)
        t4 = add("scalar", "act", waits=[t3], inc="sq", out=ln_inv, in_=ln_rv,
                 func=AF.Sqrt)
        t5 = add("vector", "tt", waits=[t4], inc="dve", out=ln_nmi, in0=ln_nm,
                 in1=ln_inv, op=ALU.mult)
        add("vector", "wait", waits=[t5])
        ap1 = add("vector", "ts", inc="dve", out=apply_out, in0=YTOK[par][:, :],
                  s1=ln_inv, s2=ln_nmi, op0=ALU.mult, op1=ALU.add)
        return t1, ap1
        return t1, ap1

    yev_rel = [None, None]
    t2_rel = [None, None]
    xtok_rel = [None, None]
    ytok_rel = [None, None]
    y1ev_rel = None
    xr_tok = [None] * NQ
    last_womm = None
    for g4 in range(4):
        for jj in range(8):
            wset = jj % 2
            w = [t_wo] if (g4 == 0 and jj == 0) else []
            if g4 == 0 and jj == 0:
                w.append(tr4_rel[0])
                w.append(tr4_rel[1])
            if yev_rel[wset]:
                w.append(yev_rel[wset])
            for i in range(8):
                last_womm = add("tensor", "mm", waits=w if i == 0 else [],
                                inc="womm" if i == 7 else None,
                                out=psA[:, wset * 512: wset * 512 + 512],
                                lhsT=WT_PH4[:, i * 1024 + jj * 128: i * 1024 + (jj + 1) * 128],
                                rhs=ATT[:, i * 2048 + g4 * 512: i * 2048 + (g4 + 1) * 512],
                                start=(i == 0), stop=(i == 7))
            ev = add("scalar", "copy",
                     waits=[now("womm")] + ([("t2", 32 * g4)] if g4 > 0 else []),
                     inc="yev",
                     out=YAT[:, jj * 512:(jj + 1) * 512],
                     in_=psA[:, wset * 512: wset * 512 + 512])
            yev_rel[wset] = ev
        t_ya = now("yev")
        for tau in range(4):
            tloc = g4 * 4 + tau
            par = tloc % 2
            w = [t_atev_all] if tloc < 2 else []
            if xtok_rel[par]:
                w.append(xtok_rel[par])
            xr_tok[tloc] = add("sync", "dma", waits=w, inc="xr", incv=16,
                               out=XTOK[par][:, :],
                               in_=xq[tloc * 128:(tloc + 1) * 128, :])
            w = [t_ya]
            if t2_rel[par]:
                w.append(t2_rel[par])
            if tloc == 0:
                w.append(evac_rel)
            trl = None
            for cch in range(8):
                trl = add("tensor", "tr", waits=w if cch == 0 else [], inc="t2",
                          out=psA[:, 2048 + par * 1024 + cch * 128:
                                  2048 + par * 1024 + (cch + 1) * 128],
                          in_=YAT[:, cch * 512 + tau * 128: cch * 512 + (tau + 1) * 128],
                          identity=idfS[:, :])
            extra = [trl, xr_tok[tloc]]
            if ytok_rel[par]:
                extra.append(ytok_rel[par])
            t1, ap1 = ln_chain(par, XTOK[par][:, :],
                               Y1[:, tloc * 1024:(tloc + 1) * 1024], extra)
            t2_rel[par] = t1
            xtok_rel[par] = t1
            if DBG and tloc < 2:
                off = tloc * (3 * 1024 + 16)
                dsn = add("sync", "dma", waits=[ap1], inc="dbl", incv=16,
                          out=dbgl[:, off:off + 16], in_=lnS[:, 0:16])
                dsn = add("sync", "dma", inc="dbl", incv=16,
                          out=dbgl[:, off + 16:off + 1040], in_=XTOK[par][:, :])
                dsn = add("sync", "dma", inc="dbl", incv=16,
                          out=dbgl[:, off + 1040:off + 2064], in_=YTOK[par][:, :])
                dsn = add("sync", "dma", inc="dbl", incv=16,
                          out=dbgl[:, off + 2064:off + 3088],
                          in_=Y1[:, tloc * 1024:(tloc + 1) * 1024])
                sch.dbg_gate = dsn
            ytok_rel[par] = ap1
            trl2 = None
            for cch in range(8):
                w2 = []
                if cch == 0:
                    w2 = [ap1]
                    if DBG and tloc < 2:
                        w2.append(sch.dbg_gate)
                    if y1ev_rel:
                        w2.append(y1ev_rel)
                    elif tloc == 0:
                        w2.append(tr4_rel[1])
                trl2 = add("tensor", "tr", waits=w2, inc="y1tr",
                           out=psA[:, 1024 + cch * 128: 1024 + (cch + 1) * 128],
                           in_=Y1[:, tloc * 1024 + cch * 128: tloc * 1024 + (cch + 1) * 128],
                           identity=idfS[:, :])
            y1ev_rel = add("scalar", "copy", waits=[trl2], inc="y1t",
                           out=sbap(oC // 2 + tloc * 128, [[2048, 8], [1, 128]]),
                           in_=psA[:, 1024:2048].rearrange("p (c q) -> p c q", c=8))
    t_y1t = now("y1t")
    t_womm_all = now("womm")
    t_t2_all = now("t2")

    t_dbgc = None
    if DBG:
        t_dbgc = add("sync", "dma", waits=[t_womm_all, t_t2_all,
                                           now("y1t")],
                     inc="dbc", incv=16, out=dbgc[:, :], in_=arA[:, 0:103 * 1024])
    w = [t_womm_all, t_t2_all, t_y1t]
    if DBG:
        w.append(t_dbgc)
    f_tok = None
    for i in range(8):
        f_tok = add("gpsimd", "dma", waits=w if i == 0 else [], inc="fw",
                    incv=16, out=U1B[:, i * 256:(i + 1) * 256],
                    in_=u1[i * 128:(i + 1) * 128, :])
    for r in range(2):
        f_tok = add("gpsimd", "dma", inc="fw", incv=16,
                    out=V1B[:, r * 4096:(r + 1) * 4096],
                    in_=v1[r * 128:(r + 1) * 128, :])
    for r in range(2):
        f_tok = add("gpsimd", "dma", inc="fw", incv=16,
                    out=V2B[:, r * 1024:(r + 1) * 1024],
                    in_=v2[r * 128:(r + 1) * 128, :])
    for i in range(32):
        dstw = U2Ba if i < 16 else U2Bb
        f_tok = add("gpsimd", "dma", inc="fw", incv=16,
                    out=dstw[:, (i % 16) * 256:(i % 16 + 1) * 256],
                    in_=u2[i * 128:(i + 1) * 128, :])
    t_fw = now("fw")

    def u2slice(fc, r):
        t_ = U2Ba if fc < 16 else U2Bb
        return t_[:, (fc % 16) * 256 + r * 128: (fc % 16) * 256 + (r + 1) * 128]

    pbank_rel = [None, None]
    hbank_rel = [None, None]
    t3_rel = [None, None]
    str_rel = [None, None]
    for g4 in range(4):
        for r in range(2):
            pset = r % 2
            w = [t_fw, t_y1t] if (g4 == 0 and r == 0) else []
            if pbank_rel[pset]:
                w.append(pbank_rel[pset])
            elif g4 == 0:
                w.append(yev_rel[pset])
            for i in range(8):
                add("tensor", "mm", waits=w if i == 0 else [],
                    inc="pmm" if i == 7 else None,
                    out=psA[:, pset * 512: pset * 512 + 512],
                    lhsT=U1B[:, i * 256 + r * 128: i * 256 + (r + 1) * 128],
                    rhs=Y1T[:, i * 2048 + g4 * 512: i * 2048 + (g4 + 1) * 512],
                    start=(i == 0), stop=(i == 7))
            ev = add("scalar", "copy", waits=[now("pmm")], inc="pev",
                     out=PT[:, r * 512:(r + 1) * 512],
                     in_=psA[:, pset * 512: pset * 512 + 512])
            pbank_rel[pset] = ev
        t_pt = now("pev")
        ht_rel = [None] * 4
        pend_q2 = None
        for fc in range(32):
            hset = fc % 2
            w = [t_pt] if fc < 2 else []
            if hbank_rel[hset]:
                w.append(hbank_rel[hset])
            for r in range(2):
                add("tensor", "mm", waits=w if r == 0 else [],
                    inc="hmm" if r == 1 else None,
                    out=psA[:, (2 + hset) * 512: (2 + hset) * 512 + 512],
                    lhsT=V1B[:, r * 4096 + fc * 128: r * 4096 + (fc + 1) * 128],
                    rhs=PT[:, r * 512:(r + 1) * 512],
                    start=(r == 0), stop=(r == 1))
            wg = [now("hmm")]
            if ht_rel[fc % 4]:
                wg.append(ht_rel[fc % 4])
            ev = add("scalar", "act", waits=wg, inc="hev",
                     out=HTr[:, (fc % 4) * 512: (fc % 4 + 1) * 512],
                     in_=psA[:, (2 + hset) * 512: (2 + hset) * 512 + 512],
                     func=AF.Gelu_apprx_tanh, bias=b1cS[:, fc:fc + 1],
                     scale=cstS[:, 3:4])
            hbank_rel[hset] = ev
            def _q2(fcq, evq):
                wq2 = [evq]
                if fcq == 0:
                    if pbank_rel[0]:
                        wq2.append(pbank_rel[0])
                    if pbank_rel[1]:
                        wq2.append(pbank_rel[1])
                lastq = None
                for r in range(2):
                    lastq = add("tensor", "mm", waits=wq2 if r == 0 else [],
                                inc="q2p" if r == 1 else None,
                                out=psA[:, r * 512: r * 512 + 512],
                                lhsT=u2slice(fcq, r),
                                rhs=HTr[:, (fcq % 4) * 512: (fcq % 4 + 1) * 512],
                                start=(fcq == 0), stop=(fcq == 31))
                ht_rel[fcq % 4] = lastq
            if pend_q2 is not None:
                _q2(*pend_q2)
            pend_q2 = (fc, ev)
        _q2(*pend_q2)
        for r in range(2):
            ev = add("scalar", "copy", waits=[now("q2p")], inc="q2ev",
                     out=Q2T[:, r * 512:(r + 1) * 512],
                     in_=psA[:, r * 512: r * 512 + 512])
            pbank_rel[r] = ev
        t_q2 = now("q2ev")
        for jj in range(8):
            oset = jj % 2
            w = [t_q2]
            if hbank_rel[oset]:
                w.append(hbank_rel[oset])
            for r in range(2):
                add("tensor", "mm", waits=w if r == 0 else [],
                    inc="omm" if r == 1 else None,
                    out=psA[:, (2 + oset) * 512: (2 + oset) * 512 + 512],
                    lhsT=V2B[:, r * 1024 + jj * 128: r * 1024 + (jj + 1) * 128],
                    rhs=Q2T[:, r * 512:(r + 1) * 512],
                    start=(r == 0), stop=(r == 1))
            ev = add("vector", "ts",
                     waits=[now("omm")] + ([("t3", 32 * g4)] if g4 > 0 else []),
                     inc="oev",
                     out=FFT[:, jj * 512:(jj + 1) * 512],
                     in0=psA[:, (2 + oset) * 512: (2 + oset) * 512 + 512],
                     s1=b2cS[:, jj:jj + 1], s2=None, op0=ALU.add, op1=None)
            hbank_rel[oset] = ev
        t_ff = now("oev")
        for tau in range(4):
            tloc = g4 * 4 + tau
            par = tloc % 2
            w = [t_ff]
            if t3_rel[par]:
                w.append(t3_rel[par])
            elif g4 == 0:
                w.append(t2_rel[par])
            trl = None
            for cch in range(8):
                trl = add("tensor", "tr", waits=w if cch == 0 else [], inc="t3",
                          out=psA[:, 2048 + par * 1024 + cch * 128:
                                  2048 + par * 1024 + (cch + 1) * 128],
                          in_=FFT[:, cch * 512 + tau * 128: cch * 512 + (tau + 1) * 128],
                          identity=idfS[:, :])
            extra = [trl]
            if ytok_rel[par]:
                extra.append(ytok_rel[par])
            t1, ap1 = ln_chain(par, Y1[:, tloc * 1024:(tloc + 1) * 1024],
                               OTOK[par][:, :],
                               extra + ([str_rel[par]] if str_rel[par] else []))
            t3_rel[par] = t1
            ytok_rel[par] = ap1
            st_tok = add("sync", "dma", waits=[ap1], inc="st", incv=16,
                         out=out_d[tloc * 128:(tloc + 1) * 128, :],
                         in_=OTOK[par][:, :])
            str_rel[par] = st_tok
    add("sync", "wait", waits=[now("st")])

    semnames = sorted(sch.cnt.keys())
    sems = {}
    for n in semnames:
        g = nc.semaphore(n)
        sems[n] = g.__enter__()
        nc._sem_guards = getattr(nc, "_sem_guards", [])
        nc._sem_guards.append(g)

    def run_ops(eng, name):
        waited = {}
        for op in sch.ops[name]:
            for (sname, v) in op["waits"]:
                if v > waited.get(sname, 0):
                    eng.wait_ge(sems[sname], v)
                    waited[sname] = v
            k, kw = op["kind"], op["kw"]
            if k == "wait":
                continue
            if k == "dma":
                ins = eng.dma_start(out=kw["out"], in_=kw["in_"],
                                    allow_slow_non_contiguous=kw.get("slow", False))
            elif k == "dmaT":
                ins = eng.dma_start(out=kw["out"], in_=kw["in_"], transpose=True)
            elif k == "mm":
                ins = eng.matmul(kw["out"], kw["lhsT"], kw["rhs"],
                                 start=kw["start"], stop=kw["stop"])
            elif k == "tr":
                ins = eng.transpose(kw["out"], kw["in_"], kw["identity"])
            elif k == "act":
                a = {}
                for f in ("bias", "scale", "accum_out"):
                    if f in kw and kw[f] is not None:
                        a[f] = kw[f]
                ins = eng.activation(kw["out"], kw["in_"], kw["func"], **a)
            elif k == "copy":
                ins = (eng.copy(kw["out"], kw["in_"]) if name == "scalar"
                       else eng.tensor_copy(kw["out"], kw["in_"]))
            elif k == "tt":
                ins = eng.tensor_tensor(kw["out"], kw["in0"], kw["in1"], kw["op"])
            elif k == "ts":
                if kw.get("s2") is None:
                    ins = eng.tensor_scalar(kw["out"], kw["in0"], kw["s1"],
                                            None, kw["op0"])
                else:
                    ins = eng.tensor_scalar(kw["out"], kw["in0"], kw["s1"],
                                            kw["s2"], kw["op0"], kw["op1"])
            elif k == "stt":
                ins = eng.scalar_tensor_tensor(kw["out"], kw["in0"], kw["s1"],
                                               kw["in1"], kw["op0"], kw["op1"],
                                               accum_out=kw.get("accum_out"))
            elif k == "recip":
                ins = eng.reciprocal(kw["out"], kw["in_"])
            elif k == "memset":
                ins = eng.memset(kw["ap"], kw["value"])
            else:
                raise ValueError(k)
            if op["inc"] is not None:
                ins.then_inc(sems[op["inc"]], op["incv"])

    with nc.Block() as block:
        @block.sync
        def _(e):
            run_ops(e, "sync")

        @block.gpsimd
        def _(e):
            run_ops(e, "gpsimd")

        @block.tensor
        def _(e):
            run_ops(e, "tensor")

        @block.vector
        def _(e):
            run_ops(e, "vector")

        @block.scalar
        def _(e):
            run_ops(e, "scalar")

    return nc


def emit_pv(m, h, pend, exp_tok, add, now, psA, ESB, Vsb, evac_rel, v_gate):
    _, _, s, gs = pend
    par_e = gs % 4
    gate = exp_tok[gs]
    for li in range(2):
        l = 2 * s + li
        pk = slot2phys(m, l)
        for tau in range(4):
            w = []
            if li == 0 and tau == 0:
                w.append(gate)
                if evac_rel:
                    w.append(evac_rel)
                if v_gate:
                    w.extend(v_gate)
            add("tensor", "mm", waits=w,
                inc="pvd" if (li == 1 and tau == 3) else None,
                out=psA[:, 2048 + tau * 512: 2048 + tau * 512 + 65],
                lhsT=ESB[:, par_e * 1024 + li * 512 + tau * 128:
                         par_e * 1024 + li * 512 + (tau + 1) * 128],
                rhs=Vsb[:, pk * 1040 + 65 * h: pk * 1040 + 65 * h + 65],
                start=(l == 0), stop=(l == P[m] - 1))


_CACHE = {}


def kernel(L0_attn_in_features, L0_attn_Wq, L0_attn_Wk, L0_attn_Wv, L0_attn_Wo,
           L0_ffn_U1, L0_ffn_V1, L0_ffn_b1, L0_ffn_U2, L0_ffn_V2, L0_ffn_b2):
    x = np.ascontiguousarray(np.asarray(L0_attn_in_features, dtype=np.float32))
    Ws = {k: np.ascontiguousarray(np.asarray(v, dtype=np.float32))
          for k, v in (("wq", L0_attn_Wq), ("wk", L0_attn_Wk),
                       ("wv", L0_attn_Wv), ("wo", L0_attn_Wo),
                       ("u1", L0_ffn_U1), ("v1", L0_ffn_V1),
                       ("u2", L0_ffn_U2), ("v2", L0_ffn_V2))}
    b1v = np.ascontiguousarray(np.asarray(L0_ffn_b1, dtype=np.float32))
    b2v = np.ascontiguousarray(np.asarray(L0_ffn_b2, dtype=np.float32))

    cstv = np.zeros((128, 8), np.float32)
    cstv[:, 0] = 0.125
    cstv[:, 1] = 1.0 / 1024.0
    cstv[:, 2] = 1.0e-5
    cstv[:, 3] = 1.0
    cstv[:, 4] = -1.0 / 1024.0
    atlv = np.zeros((128, 7 * 128), np.float32)
    kk, qq = np.meshgrid(np.arange(128), np.arange(128), indexing="ij")
    atlv[:, 3 * 128:4 * 128] = (kk <= qq).astype(np.float32)
    atlv[:, 4 * 128:] = 1.0
    atlv = atlv.astype(bf16)
    idfv = np.eye(128, dtype=np.float32)
    idbv = idfv.astype(bf16)

    in_maps, meta = [], []
    for c in range(8):
        seq, role = c // 2, c % 2
        phys, _N = phys_order(role)
        qtiles = [g0 + t for g0 in GQ_ROLE[role] for t in range(4)]
        xs = x[seq]
        xkv_rows = np.concatenate(
            [xs[max(g, 0) * 128: max(g, 0) * 128 + 128] for g in phys], axis=0)
        xq_rows = np.concatenate(
            [xs[g * 128:(g + 1) * 128] for g in qtiles], axis=0)
        live = kill_quads(role)
        bkv = np.full((128, NQUAD), EXPBIAS, np.float32)
        bkv[:, ~live] = KILL
        im = {"xkv": np.ascontiguousarray(xkv_rows),
              "xq": np.ascontiguousarray(xq_rows),
              "b1": b1v, "b2": b2v, "cst": cstv, "bk": bkv, "atl": atlv,
              "idf": idfv, "idb": idbv}
        im.update(Ws)
        in_maps.append(im)
        meta.append((seq, qtiles))

    if "nc" not in _CACHE:
        _CACHE["nc"] = build_program()
    res = run_bass_kernel_spmd(_CACHE["nc"], in_maps, core_ids=list(range(8)))

    out = np.empty((B, S, D), np.float32)
    for c in range(8):
        seq, qtiles = meta[c]
        oc = np.asarray(res.results[c]["out"])
        for li, g in enumerate(qtiles):
            out[seq, g * 128:(g + 1) * 128] = oc[li * 128:(li + 1) * 128]
    return out


if __name__ == "__main__":
    nc = build_program()
    n = sum(len(v) for v in [])
    print("build ok")


# version 21
# speedup vs baseline: 2.2023x; 1.2169x over previous
import numpy as np
import ml_dtypes
import concourse.bass as bass
import concourse.mybir as mybir
from concourse.bass_utils import run_bass_kernel_spmd

F32 = mybir.dt.float32
BF16 = mybir.dt.bfloat16
AF = mybir.ActivationFunctionType
ALU = mybir.AluOpType
bf16 = ml_dtypes.bfloat16

B, S, D, H, DH = 4, 4096, 1024, 16, 64
DFF, RANK = 4096, 256
NT = S // 128
NQ = 16
P = [8, 16, 24, 32]
SM = [p // 2 for p in P]
NPAIR = sum(SM)
SMQ = [p // 4 for p in P]
NQUAD = sum(SMQ)
NSTEP = 16 * NPAIR
EXPBIAS = -20.0
KILL = -1.0e30
GQ_ROLE = [[0, 8, 20, 28], [4, 12, 16, 24]]
KB = 1024


def slot2phys(m, l):
    return P[m] - 4 + l if l < 4 else l - 4


def phys_order(role):
    gq = GQ_ROLE[role]
    N = [g + 4 for g in gq]
    phys = [-1] * NT
    placed = set()
    for m in range(4):
        for t in range(4):
            phys[P[m] - 4 + t] = N[m] - 4 + t
            placed.add(N[m] - 4 + t)
    rest = [g for g in range(max(N)) if g not in placed]
    for i in range(NT):
        if phys[i] < 0:
            phys[i] = rest.pop(0) if rest else -1
    return phys, N


def kill_quads(role):
    phys, N = phys_order(role)
    live = np.zeros(NQUAD, dtype=bool)
    idx = 0
    for m in range(4):
        for qd in range(SMQ[m]):
            l = 4 * qd
            if l < 4:
                live[idx] = True
            else:
                g = phys[slot2phys(m, l)]
                live[idx] = (g >= 0) and (g <= N[m] - 5)
            idx += 1
    return live


DBG = False
ENGINES = ("sync", "gpsimd", "tensor", "vector", "scalar")


class Sched:
    def __init__(self):
        self.ops = {e: [] for e in ENGINES}
        self.cnt = {}

    def add(self, eng, kind, waits=(), inc=None, incv=1, **kw):
        self.ops[eng].append(dict(kind=kind, waits=[w for w in waits if w],
                                  inc=inc, incv=incv, kw=kw))
        if inc is not None:
            self.cnt[inc] = self.cnt.get(inc, 0) + incv
            return (inc, self.cnt[inc])
        return None

    def now(self, name):
        return (name, self.cnt.get(name, 0))


def build_program():
    nc = bass.Bass("TRN2", target_bir_lowering=False,
                   detect_race_conditions=False)

    xkvT = nc.dram_tensor("xkvT", [128, NT * 1024], F32,
                          kind="ExternalInput").ap()
    xqT = nc.dram_tensor("xqT", [128, NQ * 1024], F32,
                         kind="ExternalInput").ap()
    xq = nc.dram_tensor("xq", [NQ * 128, D], F32, kind="ExternalInput").ap()
    wts = {n: nc.dram_tensor(n, [128, 8 * 1024], F32, kind="ExternalInput").ap()
           for n in ("wqT", "wkT", "wvT", "woT")}
    u1 = nc.dram_tensor("u1", [D, RANK], F32, kind="ExternalInput").ap()
    v1 = nc.dram_tensor("v1", [RANK, DFF], F32, kind="ExternalInput").ap()
    b1 = nc.dram_tensor("b1", [DFF], F32, kind="ExternalInput").ap()
    u2 = nc.dram_tensor("u2", [DFF, RANK], F32, kind="ExternalInput").ap()
    v2 = nc.dram_tensor("v2", [RANK, D], F32, kind="ExternalInput").ap()
    b2 = nc.dram_tensor("b2", [D], F32, kind="ExternalInput").ap()
    cst = nc.dram_tensor("cst", [128, 8], F32, kind="ExternalInput").ap()
    bkd = nc.dram_tensor("bk", [128, NQUAD], F32, kind="ExternalInput").ap()
    atl = nc.dram_tensor("atl", [128, 7 * 128], BF16, kind="ExternalInput").ap()
    idf_d = nc.dram_tensor("idf", [128, 128], F32, kind="ExternalInput").ap()
    idb_d = nc.dram_tensor("idb", [128, 128], BF16, kind="ExternalInput").ap()
    out_d = nc.dram_tensor("out", [NQ * 128, D], F32, kind="ExternalOutput").ap()
    if DBG:
        dbga = nc.dram_tensor("dbga", [128, 103 * 1024], BF16, kind="ExternalOutput").ap()
        dbgc = nc.dram_tensor("dbgc", [128, 103 * 1024], BF16, kind="ExternalOutput").ap()
        dbgl = nc.dram_tensor("dbgl", [128, 2 * 3 * 1024 + 32], F32, kind="ExternalOutput").ap()

    AR = 207 * KB
    arena = nc.alloc_sbuf_tensor("arena", [128, AR // 2], BF16)
    arA = arena.ap()

    def bfv(off_b, cols):
        assert off_b % 2 == 0 and off_b + 2 * cols <= AR, (off_b, cols)
        return arA[:, off_b // 2: off_b // 2 + cols]

    PSTEP = AR // 2

    def sbap(off_el, dims):
        return bass.AP(arena, off_el, [[PSTEP, 128]] + dims)

    def f32v(off_b, cols):
        assert off_b % 4 == 0 and off_b + 4 * cols <= AR, (off_b, cols)
        return arA[:, off_b // 2: off_b // 2 + 2 * cols].bitcast(F32)

    o_cst, o_bk, o_atl = 0, 64, 256
    o_idb, o_idf = 2048, 2304
    o_b1c, o_b2c, o_rec, o_ln = 2816, 2944, 2976, 3072
    oA = 3200
    oB = oA + 64 * KB
    oC = oB + 65 * KB
    oD = oC + 32 * KB
    TOT = oD + 42 * KB
    assert TOT <= AR, (TOT, AR)

    cstS = f32v(o_cst, 8)
    bkS = f32v(o_bk, NQUAD)
    atlS = bfv(o_atl, 7 * 128)
    idbS = bfv(o_idb, 128)
    idfS = f32v(o_idf, 128)
    b1cS = f32v(o_b1c, 32)
    b2cS = f32v(o_b2c, 8)
    recS = f32v(o_rec, 8)
    lnS = f32v(o_ln, 32)

    KTv = bfv(oA, 32 * KB)
    Vsb = bfv(oB, 33280)
    QTv = bfv(oC, 16 * KB)
    ESB = bfv(oD, 3 * KB)
    PVS = f32v(oD + 6 * KB, KB)
    ATOK = bfv(oD + 10 * KB, 16 * KB)
    XTOK = [f32v(oD, KB), f32v(oD + 4 * KB, KB)]
    YTOK = [f32v(oD + 10 * KB, KB), f32v(oD + 14 * KB, KB)]
    SQ = f32v(oD + 18 * KB, KB)
    SQ = [SQ, SQ]
    WOSCR = [f32v(oD + 22 * KB, KB), f32v(oD + 26 * KB, KB)]
    OTOK = [f32v(oD + 30 * KB, KB), f32v(oD + 34 * KB, KB)]
    PT = bfv(oD + 38 * KB, KB)
    Q2T = bfv(oD + 40 * KB, KB)
    ATT = bfv(oA, 16 * KB)
    WoT = bfv(oA + 32 * KB, 8 * KB)
    YAT = f32v(oA + 48 * KB, 4 * KB)
    U1B = bfv(oA, 2 * KB)
    V1B = bfv(oA + 4 * KB, 8 * KB)
    V2B = bfv(oA + 20 * KB, 2 * KB)
    U2Bb = bfv(oA + 24 * KB, 4 * KB)
    FFT = f32v(oA + 32 * KB, 4 * KB)
    HTr = bfv(oA + 48 * KB, 2 * KB)
    U2Ba = bfv(oD, 4 * KB)
    PSP = 4096
    XTG = bfv(oD + 4 * KB, 8 * KB)
    WT = bfv(oD + 20 * KB, 8 * KB)

    Y1 = f32v(oB, 16 * KB)
    Y1T = bfv(oC, 16 * KB)

    ps = nc.alloc_psum_tensor("ps", [128, 4096], F32)
    psA = ps.ap()
    psB = psA.bitcast(BF16)

    sch = Sched()
    add, now = sch.add, sch.now

    for (dst, src) in ((cstS, cst[:, :]), (bkS, bkd[:, :]), (atlS, atl[:, :]),
                       (idbS, idb_d[:, :]), (idfS, idf_d[:, :]),
                       (b1cS[:, 0:32], b1.rearrange("(c p) -> p c", p=128)),
                       (b2cS[:, 0:8], b2.rearrange("(c p) -> p c", p=128))):
        add("sync", "dma", inc="c", incv=16, out=dst, in_=src, slow=True)
    t_c = now("c")

    SCRF = [f32v(oD + i * 4 * KB, KB) for i in range(2)]
    XB = [bfv(oD + 8 * KB, 8 * KB),
          bfv(oD + 24 * KB, 8 * KB)]
    WKT = bfv(oC + 16 * KB, 8 * KB)
    WVT = bfv(oC, 8 * KB)

    scrf_rel = [None, None]
    st_ld = [0]
    st_mm = [0]

    def chunk_load_cast(src_ap, dst_bf, gates_ld=(), gates_cast=()):
        slot = st_ld[0]; st_ld[0] ^= 1
        w = ([scrf_rel[slot]] if scrf_rel[slot] else []) + list(gates_ld)
        ld = add("sync", "dma", waits=w, inc="xld", incv=16,
                 out=SCRF[slot][:, :], in_=src_ap)
        ev = add("vector", "ts", waits=[ld] + list(gates_cast), inc="xev",
                 out=dst_bf, in0=SCRF[slot][:, :], s1=cstS[:, 3:4], s2=None,
                 op0=ALU.mult, op1=None)
        scrf_rel[slot] = ev
        return ev

    t_wk = t_wv = None
    for s in range(8):
        t_wk = chunk_load_cast(wts["wkT"][:, s * 1024:(s + 1) * 1024],
                               WKT[:, s * 1024:(s + 1) * 1024],
                               gates_ld=[t_c] if s < 2 else ())
    for s in range(8):
        t_wv = chunk_load_cast(wts["wvT"][:, s * 1024:(s + 1) * 1024],
                               WVT[:, s * 1024:(s + 1) * 1024])

    def xgroup_casts(src, goff, buf, free_tok):
        t = None
        for s in range(8):
            t = chunk_load_cast(
                src[:, goff + s * 1024: goff + (s + 1) * 1024],
                buf[:, s * 1024:(s + 1) * 1024],
                gates_cast=[free_tok] if (s == 0 and free_tok) else ())
        return t

    xt_done = [None, None]
    xt_done[0] = xgroup_casts(xkvT, 0, XB[0], None)
    t_ones = add("vector", "memset", inc="ones",
                 ap=sbap(oB // 2 + 64, [[1040, 32], [65, 16], [1, 1]]),
                 value=1.0)
    kbank = [None, None]
    mm_done = [None] * 4
    t_wq = None
    for g in range(4):
        if g + 1 < 4:
            xt_done[(g + 1) % 2] = xgroup_casts(
                xkvT, (g + 1) * 8192, XB[(g + 1) % 2],
                mm_done[g - 1] if g >= 1 else None)
        else:
            t_wq = xgroup_casts(wts["wqT"], 0, XB[0], mm_done[2])
        t_xt = xt_done[g % 2]
        XG = XB[g % 2]
        last = None
        for j in range(8):
            par = st_mm[0]; st_mm[0] ^= 1
            base = 2048 + par * 1024
            w = [t_xt]
            if g == 0 and j == 0:
                w.append(t_wk)
            if kbank[par]:
                w.append(kbank[par])
            for i in range(8):
                for hf in range(2):
                    last = add("tensor", "mm",
                               waits=w if (i == 0 and hf == 0) else [],
                               inc="kmm" if (i == 7 and hf == 1) else None,
                               out=psA[:, base + hf * 512: base + hf * 512 + 512],
                               lhsT=WKT[:, i * 1024 + j * 128: i * 1024 + (j + 1) * 128],
                               rhs=XG[:, i * 1024 + hf * 512: i * 1024 + hf * 512 + 512],
                               start=(i == 0), stop=(i == 7))
            ev = add("scalar", "copy", waits=[now("kmm")], inc="kev",
                     out=KTv[:, j * 4096 + g * 1024: j * 4096 + (g + 1) * 1024],
                     in_=psA[:, base: base + 1024])
            kbank[par] = ev
        for t in range(8):
            gt = g * 8 + t
            par = st_mm[0]; st_mm[0] ^= 1
            base = 2048 + par * 1024
            w = [t_xt, t_ones, t_wv] if gt == 0 else [t_xt]
            if kbank[par]:
                w.append(kbank[par])
            for i in range(8):
                for hf in range(2):
                    last = add("tensor", "mm",
                               waits=w if (i == 0 and hf == 0) else [],
                               inc="vmm" if (i == 7 and hf == 1) else None,
                               out=psA[:, base + hf * 512: base + hf * 512 + 512],
                               lhsT=XG[:, i * 1024 + t * 128: i * 1024 + (t + 1) * 128],
                               rhs=WVT[:, i * 1024 + hf * 512: i * 1024 + hf * 512 + 512],
                               start=(i == 0), stop=(i == 7))
            dstv = sbap(oB // 2 + gt * 1040, [[65, 16], [1, 64]])
            srcv = psA[:, base: base + 1024].rearrange("p (h d) -> p h d", h=16)
            ev = add("vector", "ts", waits=[now("vmm")], inc="vev",
                     out=dstv, in0=srcv, s1=cstS[:, 3:4], s2=None,
                     op0=ALU.mult, op1=None)
            kbank[par] = ev
        mm_done[g] = now("vmm")
    t_kmm_all = now("kmm")
    t_kev_all = now("kev")
    t_vmm_all = now("vmm")
    t_vev_all = now("vev")

    qx = xgroup_casts(xqT, 0, XB[1], mm_done[3])
    qbank = kbank
    for g in range(2):
        if g == 1:
            qx = xgroup_casts(xqT, 8192, XB[1], now("qmm"))
        t_xt = qx
        last = None
        for j in range(8):
            par = st_mm[0]; st_mm[0] ^= 1
            base = 2048 + par * 1024
            w = [t_xt]
            if g == 0 and j == 0:
                w.append(t_wq)
            if qbank[par]:
                w.append(qbank[par])
            for i in range(8):
                for hf in range(2):
                    last = add("tensor", "mm",
                               waits=w if (i == 0 and hf == 0) else [],
                               inc="qmm" if (i == 7 and hf == 1) else None,
                               out=psA[:, base + hf * 512: base + hf * 512 + 512],
                               lhsT=XB[0][:, i * 1024 + j * 128: i * 1024 + (j + 1) * 128],
                               rhs=XB[1][:, i * 1024 + hf * 512: i * 1024 + hf * 512 + 512],
                               start=(i == 0), stop=(i == 7))
            ev = add("scalar", "act",
                     waits=[now("qmm")] + ([t_vmm_all] if (g == 0 and j == 0) else []),
                     inc="qev",
                     out=QTv[:, j * 2048 + g * 1024: j * 2048 + (g + 1) * 1024],
                     in_=psA[:, base: base + 1024],
                     func=AF.Copy, scale=cstS[:, 0:1])
            qbank[par] = ev
    t_qev_all = now("qev")
    qbank_rel = qbank

    exp_tok = [None] * NSTEP
    pvd_tok = [None] * NSTEP
    evac_rel = None
    gstep = 0
    first_S = True
    pvset = 0
    evrel = [None, None]
    lastaev = [None]
    pend_norm = []
    for m in range(4):
        for hp in range(8):
            h0, h1 = 2 * hp, 2 * hp + 1
            j = hp
            pvbase = 2048 + pvset * 1024
            pend = []
            for kc in range(P[m]):
                t = gstep
                par = t % 2
                pk = slot2phys(m, kc)
                w = []
                if first_S:
                    w = [t_qev_all, t_kev_all, qbank_rel[0], qbank_rel[1]]
                    first_S = False
                if t >= 2:
                    w.append(("e", t - 1))
                add("tensor", "mm", waits=w, inc=None,
                    out=psA[:, par * 1024: par * 1024 + 512],
                    lhsT=KTv[0:64, j * 4096 + pk * 128: j * 4096 + (pk + 1) * 128],
                    rhs=QTv[0:64, j * 2048 + m * 512: j * 2048 + (m + 1) * 512],
                    start=True, stop=True, tile_position=(0, 0))
                add("tensor", "mm", waits=[], inc="smm",
                    out=psA[:, par * 1024 + 512: par * 1024 + 1024],
                    lhsT=KTv[64:128, j * 4096 + pk * 128: j * 4096 + (pk + 1) * 128],
                    rhs=QTv[64:128, j * 2048 + m * 512: j * 2048 + (m + 1) * 512],
                    start=True, stop=True, tile_position=(64, 0))
                par_e = t % 3
                wexp = [now("smm")]
                if t >= 3:
                    wexp.append(("pvd", t - 2))
                prq = sum(SMQ[:m]) + kc // 4
                exp_tok[t] = add("scalar", "act", waits=wexp, inc="e",
                                 out=ESB[:, par_e * 1024: par_e * 1024 + 1024],
                                 in_=psA[:, par * 1024: par * 1024 + 1024],
                                 func=AF.Exp, bias=bkS[:, prq:prq + 1])
                if kc < 4:
                    for hh in range(2):
                        add("vector", "tt",
                            waits=[exp_tok[t]] if hh == 0 else [],
                            inc="mk" if hh == 1 else None,
                            out=ESB[:, par_e * 1024 + hh * 512: par_e * 1024 + hh * 512 + 512],
                            in0=ESB[:, par_e * 1024 + hh * 512: par_e * 1024 + hh * 512 + 512],
                            in1=atlS[:, (3 - kc) * 128: (3 - kc) * 128 + 512],
                            op=ALU.mult)
                    exp_tok[t] = now("mk")
                pend.append((kc, t))
                gstep += 1
                if kc == 1 and pend_norm:
                    pend_norm.pop(0)()
                while len(pend) > 2:
                    pkc, pt = pend.pop(0)
                    ppk = slot2phys(m, pkc)
                    ppe = pt % 3
                    for hh in range(2):
                        ww = [exp_tok[pt]] if hh == 0 else []
                        if hh == 0 and pkc == 0:
                            if evrel[pvset]:
                                ww.append(evrel[pvset])
                            if m == 0 and hp == 0:
                                ww += [t_vev_all, t_ones]
                        add("tensor", "mm", waits=ww,
                            inc="pvd" if hh == 1 else None,
                            out=psA[0:65, pvbase + hh * 512: pvbase + hh * 512 + 512],
                            lhsT=Vsb[:, ppk * 1040 + 65 * (2 * hp + hh):
                                     ppk * 1040 + 65 * (2 * hp + hh) + 65],
                            rhs=ESB[:, ppe * 1024 + hh * 512: ppe * 1024 + hh * 512 + 512],
                            start=(pkc == 0), stop=(pkc == P[m] - 1))
                    pvd_tok[pt] = now("pvd")
            for (pkc, pt) in pend:
                ppk = slot2phys(m, pkc)
                ppe = pt % 3
                for hh in range(2):
                    ww = [exp_tok[pt]] if hh == 0 else []
                    if hh == 0 and pkc == 0:
                        if evrel[pvset]:
                            ww.append(evrel[pvset])
                        if m == 0 and hp == 0:
                            ww += [t_vev_all, t_ones]
                    add("tensor", "mm", waits=ww,
                        inc="pvd" if hh == 1 else None,
                        out=psA[0:65, pvbase + hh * 512: pvbase + hh * 512 + 512],
                        lhsT=Vsb[:, ppk * 1040 + 65 * (2 * hp + hh):
                                 ppk * 1040 + 65 * (2 * hp + hh) + 65],
                        rhs=ESB[:, ppe * 1024 + hh * 512: ppe * 1024 + hh * 512 + 512],
                        start=(pkc == 0), stop=(pkc == P[m] - 1))
                pvd_tok[pt] = now("pvd")
            cp0 = add("vector", "copy", waits=[pvd_tok[gstep - 1]], inc="dve",
                      out=PVS[0:65, 0:512], in_=psA[0:65, pvbase: pvbase + 512])
            cp1 = add("vector", "copy", inc="dve",
                      out=PVS[0:65, 512:1024],
                      in_=psA[0:65, pvbase + 512: pvbase + 1024])
            def _norm(m=m, hp=hp, pvbase=pvbase, cp1=cp1, pvset=pvset):
                trn = None
                for hh in range(2):
                    for tau in range(4):
                        ww = [cp1] if (hh == 0 and tau == 0) else []
                        trn = add("tensor", "tr", waits=ww, inc="ntr",
                                  out=psA[:, pvbase + hh * 512 + tau * 65:
                                          pvbase + hh * 512 + tau * 65 + 65],
                                  in_=PVS[0:65, hh * 512 + tau * 128:
                                          hh * 512 + (tau + 1) * 128],
                                  identity=idfS[0:65, 0:65])
                sums_ap = bass.AP(ps, pvbase + 64,
                                  [[PSP, 128], [512, 2], [65, 4], [1, 1]])
                rec_ap = recS[:, 0:8].rearrange("p (a r o) -> p a r o",
                                                a=2, o=1)
                rc = add("vector", "recip", waits=[trn], inc="dve",
                         out=rec_ap, in_=sums_ap)
                add("vector", "wait", waits=[rc])
                for hh in range(2):
                    for tau in range(4):
                        add("vector", "ts",
                            inc="aev" if (hh == 1 and tau == 3) else None,
                            out=ATOK[:, (4 * m + tau) * 1024
                                     + 64 * (2 * hp + hh):
                                     (4 * m + tau) * 1024
                                     + 64 * (2 * hp + hh) + 64],
                            in0=psA[:, pvbase + hh * 512 + tau * 65:
                                    pvbase + hh * 512 + tau * 65 + 64],
                            s1=recS[:, hh * 4 + tau: hh * 4 + tau + 1],
                            s2=None, op0=ALU.mult, op1=None)
                rel = now("aev")
                evrel[pvset] = rel
                lastaev[0] = rel
            pend_norm.append(_norm)
            pvset ^= 1
    while pend_norm:
        pend_norm.pop(0)()
    evac_rel = lastaev[0]
    t_exp_all = now("e")
    t_aev_all = now("aev")
    t_smm_all = now("smm")
    t_dbga = None
    if DBG:
        t_dbga = add("sync", "dma", waits=[t_aev_all], inc="dba", incv=16,
                     out=dbga[:, :], in_=arA[:, 0:103 * 1024])

    tr4_rel = [None, None]
    atev_tok = []
    for t in range(NQ):
        st_ = t % 2
        w = [t_aev_all, t_exp_all] if t == 0 else []
        if tr4_rel[st_]:
            w = w + [tr4_rel[st_]]
        trl = None
        for cch in range(8):
            trl = add("tensor", "tr", waits=w if cch == 0 else [], inc="tr4",
                      out=psB[:, st_ * 2048 + cch * 128: st_ * 2048 + (cch + 1) * 128],
                      in_=ATOK[:, t * 1024 + cch * 128: t * 1024 + (cch + 1) * 128],
                      identity=idbS[:, :])
        dst = sbap(oA // 2 + t * 128, [[2048, 8], [1, 128]])
        srcap = psB[:, st_ * 2048: st_ * 2048 + 1024].rearrange(
            "p (c q) -> p c q", c=8)
        ev = add("scalar", "copy",
                 waits=[trl, t_smm_all if t == 0 else None,
                        t_dbga if (DBG and t == 0) else None],
                 inc="atev", out=dst, in_=srcap)
        tr4_rel[st_] = ev
        atev_tok.append(ev)
    t_atev_all = now("atev")

    WT_PH4 = WoT
    t_wo = None
    for j in range(8):
        t_wo = add("gpsimd", "dma", waits=[t_smm_all] if j == 0 else [],
                   inc="wol", incv=16,
                   out=WoT[:, j * 1024:(j + 1) * 1024],
                   in_=wts["woT"][:, j * 1024:(j + 1) * 1024])
    t_wo = now("wol")

    ffn_gate = None
    ffn_loads = []

    def _lnslots(par):
        b = par * 16
        return [lnS[:, b + i: b + i + 1] for i in range(9)]

    def ln_chain(par, resid_in1, apply_out, extra_w=()):
        (ln_ysum, ln_qsum, ln_nm, ln_ex2, ln_m2, ln_vpe, ln_rv, ln_inv,
         ln_nmi) = _lnslots(par)
        t1 = add("vector", "stt", waits=list(extra_w), inc="dve",
                 out=YTOK[par][:, :],
                 in0=psA[:, 2048 + par * 1024: 2048 + par * 1024 + 1024],
                 s1=cstS[:, 3:4], in1=resid_in1, op0=ALU.mult, op1=ALU.add,
                 accum_out=ln_ysum)
        t2 = add("vector", "stt", waits=[t1], inc="dve", out=SQ[par][:, :],
                 in0=YTOK[par][:, :], s1=cstS[:, 3:4], in1=YTOK[par][:, :],
                 op0=ALU.mult, op1=ALU.mult, accum_out=ln_qsum)
        e = add("vector", "ts", waits=[t2], inc="dve", out=ln_nm, in0=ln_ysum,
                s1=cstS[:, 4:5], s2=None, op0=ALU.mult, op1=None)
        e2 = add("vector", "ts", inc="dve", out=ln_ex2, in0=ln_qsum,
                 s1=cstS[:, 1:2], s2=None, op0=ALU.mult, op1=None)
        add("vector", "wait", waits=[e2])
        e = add("vector", "tt", inc="dve", out=ln_m2, in0=ln_nm, in1=ln_nm,
                op=ALU.mult)
        add("vector", "wait", waits=[e])
        e = add("vector", "tt", inc="dve", out=ln_vpe, in0=ln_ex2, in1=ln_m2,
                op=ALU.subtract)
        add("vector", "wait", waits=[e])
        e = add("vector", "ts", inc="dve", out=ln_vpe, in0=ln_vpe,
                s1=cstS[:, 2:3], s2=None, op0=ALU.add, op1=None)
        add("vector", "wait", waits=[e])
        t3 = add("vector", "recip", inc="dve", out=ln_rv, in_=ln_vpe)
        t4 = add("scalar", "act", waits=[t3], inc="sq", out=ln_inv, in_=ln_rv,
                 func=AF.Sqrt)
        t5 = add("vector", "tt", waits=[t4], inc="dve", out=ln_nmi, in0=ln_nm,
                 in1=ln_inv, op=ALU.mult)
        add("vector", "wait", waits=[t5])
        ap1 = add("vector", "ts", inc="dve", out=apply_out, in0=YTOK[par][:, :],
                  s1=ln_inv, s2=ln_nmi, op0=ALU.mult, op1=ALU.add)
        return t1, ap1

    mm_rel = [None, None]
    xtok_rel = [None, None]
    ytok_rel = [None, None]
    y1_state = {"rel": None}
    pend_y1 = []
    ap1_tok = [None] * NQ

    def emit_y1t(tloc):
        trl2 = None
        for cch in range(8):
            w2 = []
            if cch == 0:
                w2 = [ap1_tok[tloc]]
                if DBG and tloc < 2:
                    w2.append(sch.dbg_gate)
                if y1_state["rel"]:
                    w2.append(y1_state["rel"])
                elif tloc == 0:
                    w2.append(tr4_rel[1])
            trl2 = add("tensor", "tr", waits=w2, inc="y1tr",
                       out=psA[:, 1024 + cch * 128: 1024 + (cch + 1) * 128],
                       in_=Y1[:, tloc * 1024 + cch * 128:
                              tloc * 1024 + (cch + 1) * 128],
                       identity=idfS[:, :])
        y1_state["rel"] = add(
            "scalar", "copy", waits=[trl2], inc="y1t",
            out=sbap(oC // 2 + tloc * 128, [[2048, 8], [1, 128]]),
            in_=psA[:, 1024:2048].rearrange("p (c q) -> p c q", c=8))

    xr_tok = [None] * NQ
    last_womm = None
    for tloc in range(NQ):
        par = tloc % 2
        base = 2048 + par * 1024
        w = [t_atev_all] if tloc < 2 else []
        if xtok_rel[par]:
            w.append(xtok_rel[par])
        xr_tok[tloc] = add("sync", "dma", waits=w, inc="xr", incv=16,
                           out=XTOK[par][:, :],
                           in_=xq[tloc * 128:(tloc + 1) * 128, :])
        w0 = [atev_tok[tloc]]
        if tloc == 0:
            w0 += [t_wo, evac_rel]
        if mm_rel[par]:
            w0.append(mm_rel[par])
        for i in range(8):
            for hf in range(2):
                last_womm = add(
                    "tensor", "mm", waits=w0 if (i == 0 and hf == 0) else [],
                    inc="womm" if (i == 7 and hf == 1) else None,
                    out=psA[:, base + hf * 512: base + hf * 512 + 512],
                    lhsT=ATT[:, i * 2048 + tloc * 128:
                             i * 2048 + (tloc + 1) * 128],
                    rhs=WoT[:, i * 1024 + hf * 512: i * 1024 + hf * 512 + 512],
                    start=(i == 0), stop=(i == 7))
        t_womm = now("womm")
        while len(pend_y1) >= 2:
            emit_y1t(pend_y1.pop(0))
        extra = [t_womm, xr_tok[tloc]]
        if ytok_rel[par]:
            extra.append(ytok_rel[par])
        t1, ap1 = ln_chain(par, XTOK[par][:, :],
                           Y1[:, tloc * 1024:(tloc + 1) * 1024], extra)
        mm_rel[par] = t1
        xtok_rel[par] = t1
        if DBG and tloc < 2:
            off = tloc * (3 * 1024 + 16)
            dsn = add("sync", "dma", waits=[ap1], inc="dbl", incv=16,
                      out=dbgl[:, off:off + 16], in_=lnS[:, 0:16])
            dsn = add("sync", "dma", inc="dbl", incv=16,
                      out=dbgl[:, off + 16:off + 1040], in_=XTOK[par][:, :])
            dsn = add("sync", "dma", inc="dbl", incv=16,
                      out=dbgl[:, off + 1040:off + 2064], in_=YTOK[par][:, :])
            dsn = add("sync", "dma", inc="dbl", incv=16,
                      out=dbgl[:, off + 2064:off + 3088],
                      in_=Y1[:, tloc * 1024:(tloc + 1) * 1024])
            sch.dbg_gate = dsn
        ytok_rel[par] = ap1
        ap1_tok[tloc] = ap1
        pend_y1.append(tloc)
    while pend_y1:
        emit_y1t(pend_y1.pop(0))
    t_y1t = now("y1t")
    t_womm_all = now("womm")

    t_dbgc = None
    if DBG:
        t_dbgc = add("sync", "dma", waits=[t_womm_all,
                                           now("y1t")],
                     inc="dbc", incv=16, out=dbgc[:, :], in_=arA[:, 0:103 * 1024])
    w = [t_womm_all, t_y1t]
    if DBG:
        w.append(t_dbgc)
    f_tok = None
    for i in range(8):
        f_tok = add("gpsimd", "dma", waits=w if i == 0 else [], inc="fw",
                    incv=16, out=U1B[:, i * 256:(i + 1) * 256],
                    in_=u1[i * 128:(i + 1) * 128, :])
    for r in range(2):
        f_tok = add("gpsimd", "dma", inc="fw", incv=16,
                    out=V1B[:, r * 4096:(r + 1) * 4096],
                    in_=v1[r * 128:(r + 1) * 128, :])
    for r in range(2):
        f_tok = add("gpsimd", "dma", inc="fw", incv=16,
                    out=V2B[:, r * 1024:(r + 1) * 1024],
                    in_=v2[r * 128:(r + 1) * 128, :])
    for i in range(32):
        dstw = U2Ba if i < 16 else U2Bb
        f_tok = add("gpsimd", "dma", inc="fw", incv=16,
                    out=dstw[:, (i % 16) * 256:(i % 16 + 1) * 256],
                    in_=u2[i * 128:(i + 1) * 128, :])
    t_fw = now("fw")

    def u2slice(fc, r):
        t_ = U2Ba if fc < 16 else U2Bb
        return t_[:, (fc % 16) * 256 + r * 128: (fc % 16) * 256 + (r + 1) * 128]

    pbank_rel = [None, None]
    hbank_rel = [None, None]
    t3_rel = [None, None]
    str_rel = [None, None]
    for g4 in range(4):
        for r in range(2):
            pset = r % 2
            w = [t_fw, t_y1t] if (g4 == 0 and r == 0) else []
            if pbank_rel[pset]:
                w.append(pbank_rel[pset])
            for i in range(8):
                add("tensor", "mm", waits=w if i == 0 else [],
                    inc="pmm" if i == 7 else None,
                    out=psA[:, pset * 512: pset * 512 + 512],
                    lhsT=U1B[:, i * 256 + r * 128: i * 256 + (r + 1) * 128],
                    rhs=Y1T[:, i * 2048 + g4 * 512: i * 2048 + (g4 + 1) * 512],
                    start=(i == 0), stop=(i == 7))
            ev = add("scalar", "copy", waits=[now("pmm")], inc="pev",
                     out=PT[:, r * 512:(r + 1) * 512],
                     in_=psA[:, pset * 512: pset * 512 + 512])
            pbank_rel[pset] = ev
        t_pt = now("pev")
        ht_rel = [None] * 4
        pend_q2 = None
        for fc in range(32):
            hset = fc % 2
            w = [t_pt] if fc < 2 else []
            if hbank_rel[hset]:
                w.append(hbank_rel[hset])
            for r in range(2):
                add("tensor", "mm", waits=w if r == 0 else [],
                    inc="hmm" if r == 1 else None,
                    out=psA[:, (2 + hset) * 512: (2 + hset) * 512 + 512],
                    lhsT=V1B[:, r * 4096 + fc * 128: r * 4096 + (fc + 1) * 128],
                    rhs=PT[:, r * 512:(r + 1) * 512],
                    start=(r == 0), stop=(r == 1))
            wg = [now("hmm")]
            if ht_rel[fc % 4]:
                wg.append(ht_rel[fc % 4])
            ev = add("scalar", "act", waits=wg, inc="hev",
                     out=HTr[:, (fc % 4) * 512: (fc % 4 + 1) * 512],
                     in_=psA[:, (2 + hset) * 512: (2 + hset) * 512 + 512],
                     func=AF.Gelu_apprx_tanh, bias=b1cS[:, fc:fc + 1],
                     scale=cstS[:, 3:4])
            hbank_rel[hset] = ev
            def _q2(fcq, evq):
                wq2 = [evq]
                if fcq == 0:
                    if pbank_rel[0]:
                        wq2.append(pbank_rel[0])
                    if pbank_rel[1]:
                        wq2.append(pbank_rel[1])
                lastq = None
                for r in range(2):
                    lastq = add("tensor", "mm", waits=wq2 if r == 0 else [],
                                inc="q2p" if r == 1 else None,
                                out=psA[:, r * 512: r * 512 + 512],
                                lhsT=u2slice(fcq, r),
                                rhs=HTr[:, (fcq % 4) * 512: (fcq % 4 + 1) * 512],
                                start=(fcq == 0), stop=(fcq == 31))
                ht_rel[fcq % 4] = lastq
            if pend_q2 is not None:
                _q2(*pend_q2)
            pend_q2 = (fc, ev)
        _q2(*pend_q2)
        for r in range(2):
            ev = add("scalar", "copy", waits=[now("q2p")], inc="q2ev",
                     out=Q2T[:, r * 512:(r + 1) * 512],
                     in_=psA[:, r * 512: r * 512 + 512])
            pbank_rel[r] = ev
        t_q2 = now("q2ev")
        for jj in range(8):
            oset = jj % 2
            w = [t_q2]
            if hbank_rel[oset]:
                w.append(hbank_rel[oset])
            for r in range(2):
                add("tensor", "mm", waits=w if r == 0 else [],
                    inc="omm" if r == 1 else None,
                    out=psA[:, (2 + oset) * 512: (2 + oset) * 512 + 512],
                    lhsT=V2B[:, r * 1024 + jj * 128: r * 1024 + (jj + 1) * 128],
                    rhs=Q2T[:, r * 512:(r + 1) * 512],
                    start=(r == 0), stop=(r == 1))
            ev = add("vector", "ts",
                     waits=[now("omm")] + ([("t3", 32 * g4)] if g4 > 0 else []),
                     inc="oev",
                     out=FFT[:, jj * 512:(jj + 1) * 512],
                     in0=psA[:, (2 + oset) * 512: (2 + oset) * 512 + 512],
                     s1=b2cS[:, jj:jj + 1], s2=None, op0=ALU.add, op1=None)
            hbank_rel[oset] = ev
        t_ff = now("oev")
        for tau in range(4):
            tloc = g4 * 4 + tau
            par = tloc % 2
            w = [t_ff]
            if t3_rel[par]:
                w.append(t3_rel[par])
            elif g4 == 0:
                w.append(mm_rel[par])
            trl = None
            for cch in range(8):
                trl = add("tensor", "tr", waits=w if cch == 0 else [], inc="t3",
                          out=psA[:, 2048 + par * 1024 + cch * 128:
                                  2048 + par * 1024 + (cch + 1) * 128],
                          in_=FFT[:, cch * 512 + tau * 128: cch * 512 + (tau + 1) * 128],
                          identity=idfS[:, :])
            extra = [trl]
            if ytok_rel[par]:
                extra.append(ytok_rel[par])
            t1, ap1 = ln_chain(par, Y1[:, tloc * 1024:(tloc + 1) * 1024],
                               OTOK[par][:, :],
                               extra + ([str_rel[par]] if str_rel[par] else []))
            t3_rel[par] = t1
            ytok_rel[par] = ap1
            st_tok = add("sync", "dma", waits=[ap1], inc="st", incv=16,
                         out=out_d[tloc * 128:(tloc + 1) * 128, :],
                         in_=OTOK[par][:, :])
            str_rel[par] = st_tok
    add("sync", "wait", waits=[now("st")])

    semnames = sorted(sch.cnt.keys())
    sems = {}
    for n in semnames:
        g = nc.semaphore(n)
        sems[n] = g.__enter__()
        nc._sem_guards = getattr(nc, "_sem_guards", [])
        nc._sem_guards.append(g)

    def run_ops(eng, name):
        waited = {}
        for op in sch.ops[name]:
            for (sname, v) in op["waits"]:
                if v > waited.get(sname, 0):
                    eng.wait_ge(sems[sname], v)
                    waited[sname] = v
            k, kw = op["kind"], op["kw"]
            if k == "wait":
                continue
            if k == "dma":
                ins = eng.dma_start(out=kw["out"], in_=kw["in_"],
                                    allow_slow_non_contiguous=kw.get("slow", False))
            elif k == "dmaT":
                ins = eng.dma_start(out=kw["out"], in_=kw["in_"], transpose=True)
            elif k == "mm":
                ins = eng.matmul(kw["out"], kw["lhsT"], kw["rhs"],
                                 start=kw["start"], stop=kw["stop"])
            elif k == "tr":
                ins = eng.transpose(kw["out"], kw["in_"], kw["identity"])
            elif k == "act":
                a = {}
                for f in ("bias", "scale", "accum_out"):
                    if f in kw and kw[f] is not None:
                        a[f] = kw[f]
                ins = eng.activation(kw["out"], kw["in_"], kw["func"], **a)
            elif k == "copy":
                ins = (eng.copy(kw["out"], kw["in_"]) if name == "scalar"
                       else eng.tensor_copy(kw["out"], kw["in_"]))
            elif k == "tt":
                ins = eng.tensor_tensor(kw["out"], kw["in0"], kw["in1"], kw["op"])
            elif k == "ts":
                if kw.get("s2") is None:
                    ins = eng.tensor_scalar(kw["out"], kw["in0"], kw["s1"],
                                            None, kw["op0"])
                else:
                    ins = eng.tensor_scalar(kw["out"], kw["in0"], kw["s1"],
                                            kw["s2"], kw["op0"], kw["op1"])
            elif k == "stt":
                ins = eng.scalar_tensor_tensor(kw["out"], kw["in0"], kw["s1"],
                                               kw["in1"], kw["op0"], kw["op1"],
                                               accum_out=kw.get("accum_out"))
            elif k == "recip":
                ins = eng.reciprocal(kw["out"], kw["in_"])
            elif k == "memset":
                ins = eng.memset(kw["ap"], kw["value"])
            else:
                raise ValueError(k)
            if op["inc"] is not None:
                ins.then_inc(sems[op["inc"]], op["incv"])

    with nc.Block() as block:
        @block.sync
        def _(e):
            run_ops(e, "sync")

        @block.gpsimd
        def _(e):
            run_ops(e, "gpsimd")

        @block.tensor
        def _(e):
            run_ops(e, "tensor")

        @block.vector
        def _(e):
            run_ops(e, "vector")

        @block.scalar
        def _(e):
            run_ops(e, "scalar")

    return nc


def emit_pv(m, h, pend, exp_tok, add, now, psA, ESB, Vsb, evac_rel, v_gate):
    _, _, s, gs = pend
    par_e = gs % 4
    gate = exp_tok[gs]
    for li in range(2):
        l = 2 * s + li
        pk = slot2phys(m, l)
        for tau in range(4):
            w = []
            if li == 0 and tau == 0:
                w.append(gate)
                if evac_rel:
                    w.append(evac_rel)
                if v_gate:
                    w.extend(v_gate)
            add("tensor", "mm", waits=w,
                inc="pvd" if (li == 1 and tau == 3) else None,
                out=psA[:, 2048 + tau * 512: 2048 + tau * 512 + 65],
                lhsT=ESB[:, par_e * 1024 + li * 512 + tau * 128:
                         par_e * 1024 + li * 512 + (tau + 1) * 128],
                rhs=Vsb[:, pk * 1040 + 65 * h: pk * 1040 + 65 * h + 65],
                start=(l == 0), stop=(l == P[m] - 1))


_CACHE = {}


def kernel(L0_attn_in_features, L0_attn_Wq, L0_attn_Wk, L0_attn_Wv, L0_attn_Wo,
           L0_ffn_U1, L0_ffn_V1, L0_ffn_b1, L0_ffn_U2, L0_ffn_V2, L0_ffn_b2):
    x = np.ascontiguousarray(np.asarray(L0_attn_in_features, dtype=np.float32))
    Ws = {k: np.ascontiguousarray(np.asarray(v, dtype=np.float32))
          for k, v in (("u1", L0_ffn_U1), ("v1", L0_ffn_V1),
                       ("u2", L0_ffn_U2), ("v2", L0_ffn_V2))}

    def _wT(w):
        w = np.asarray(w, dtype=np.float32)
        return np.ascontiguousarray(
            w.reshape(8, 128, 8, 128).transpose(3, 2, 0, 1).reshape(128, 8192))

    def _xT(rows):
        n = rows.shape[0] // 1024
        return np.ascontiguousarray(np.concatenate(
            [_wT(rows[g * 1024:(g + 1) * 1024]) for g in range(n)], axis=1))

    WsT = {"wqT": _wT(L0_attn_Wq), "wkT": _wT(L0_attn_Wk),
           "wvT": _wT(L0_attn_Wv), "woT": _wT(L0_attn_Wo)}
    b1v = np.ascontiguousarray(np.asarray(L0_ffn_b1, dtype=np.float32))
    b2v = np.ascontiguousarray(np.asarray(L0_ffn_b2, dtype=np.float32))

    cstv = np.zeros((128, 8), np.float32)
    cstv[:, 0] = 0.125
    cstv[:, 1] = 1.0 / 1024.0
    cstv[:, 2] = 1.0e-5
    cstv[:, 3] = 1.0
    cstv[:, 4] = -1.0 / 1024.0
    atlv = np.zeros((128, 7 * 128), np.float32)
    kk, qq = np.meshgrid(np.arange(128), np.arange(128), indexing="ij")
    atlv[:, 3 * 128:4 * 128] = (kk <= qq).astype(np.float32)
    atlv[:, 4 * 128:] = 1.0
    atlv = atlv.astype(bf16)
    idfv = np.eye(128, dtype=np.float32)
    idbv = idfv.astype(bf16)

    in_maps, meta = [], []
    for c in range(8):
        seq, role = c // 2, c % 2
        phys, _N = phys_order(role)
        qtiles = [g0 + t for g0 in GQ_ROLE[role] for t in range(4)]
        xs = x[seq]
        xkv_rows = np.concatenate(
            [xs[max(g, 0) * 128: max(g, 0) * 128 + 128] for g in phys], axis=0)
        xq_rows = np.concatenate(
            [xs[g * 128:(g + 1) * 128] for g in qtiles], axis=0)
        live = kill_quads(role)
        bkv = np.full((128, NQUAD), EXPBIAS, np.float32)
        bkv[:, ~live] = KILL
        im = {"xkvT": _xT(xkv_rows), "xqT": _xT(xq_rows),
              "xq": np.ascontiguousarray(xq_rows),
              "b1": b1v, "b2": b2v, "cst": cstv, "bk": bkv, "atl": atlv,
              "idf": idfv, "idb": idbv}
        im.update(Ws)
        im.update(WsT)
        in_maps.append(im)
        meta.append((seq, qtiles))

    if "nc" not in _CACHE:
        _CACHE["nc"] = build_program()
    res = run_bass_kernel_spmd(_CACHE["nc"], in_maps, core_ids=list(range(8)))

    out = np.empty((B, S, D), np.float32)
    for c in range(8):
        seq, qtiles = meta[c]
        oc = np.asarray(res.results[c]["out"])
        for li, g in enumerate(qtiles):
            out[seq, g * 128:(g + 1) * 128] = oc[li * 128:(li + 1) * 128]
    return out


if __name__ == "__main__":
    nc = build_program()
    n = sum(len(v) for v in [])
    print("build ok")


# version 22
# speedup vs baseline: 2.2086x; 1.0029x over previous
import numpy as np
import ml_dtypes
import concourse.bass as bass
import concourse.mybir as mybir
from concourse.bass_utils import run_bass_kernel_spmd

F32 = mybir.dt.float32
BF16 = mybir.dt.bfloat16
AF = mybir.ActivationFunctionType
ALU = mybir.AluOpType
bf16 = ml_dtypes.bfloat16

B, S, D, H, DH = 4, 4096, 1024, 16, 64
DFF, RANK = 4096, 256
NT = S // 128
NQ = 16
P = [8, 16, 24, 32]
SM = [p // 2 for p in P]
NPAIR = sum(SM)
SMQ = [p // 4 for p in P]
NQUAD = sum(SMQ)
NSTEP = 16 * NPAIR
EXPBIAS = -20.0
KILL = -1.0e30
GQ_ROLE = [[0, 8, 20, 28], [4, 12, 16, 24]]
KB = 1024


def slot2phys(m, l):
    return P[m] - 4 + l if l < 4 else l - 4


def phys_order(role):
    gq = GQ_ROLE[role]
    N = [g + 4 for g in gq]
    phys = [-1] * NT
    placed = set()
    for m in range(4):
        for t in range(4):
            phys[P[m] - 4 + t] = N[m] - 4 + t
            placed.add(N[m] - 4 + t)
    rest = [g for g in range(max(N)) if g not in placed]
    for i in range(NT):
        if phys[i] < 0:
            phys[i] = rest.pop(0) if rest else -1
    return phys, N


def kill_quads(role):
    phys, N = phys_order(role)
    live = np.zeros(NQUAD, dtype=bool)
    idx = 0
    for m in range(4):
        for qd in range(SMQ[m]):
            l = 4 * qd
            if l < 4:
                live[idx] = True
            else:
                g = phys[slot2phys(m, l)]
                live[idx] = (g >= 0) and (g <= N[m] - 5)
            idx += 1
    return live


DBG = False
ENGINES = ("sync", "gpsimd", "tensor", "vector", "scalar")


class Sched:
    def __init__(self):
        self.ops = {e: [] for e in ENGINES}
        self.cnt = {}

    def add(self, eng, kind, waits=(), inc=None, incv=1, **kw):
        self.ops[eng].append(dict(kind=kind, waits=[w for w in waits if w],
                                  inc=inc, incv=incv, kw=kw))
        if inc is not None:
            self.cnt[inc] = self.cnt.get(inc, 0) + incv
            return (inc, self.cnt[inc])
        return None

    def now(self, name):
        return (name, self.cnt.get(name, 0))


def build_program():
    nc = bass.Bass("TRN2", target_bir_lowering=False,
                   detect_race_conditions=False)

    xkvT = nc.dram_tensor("xkvT", [128, NT * 1024], F32,
                          kind="ExternalInput").ap()
    xqT = nc.dram_tensor("xqT", [128, NQ * 1024], F32,
                         kind="ExternalInput").ap()
    xq = nc.dram_tensor("xq", [NQ * 128, D], F32, kind="ExternalInput").ap()
    wts = {n: nc.dram_tensor(n, [128, 8 * 1024], F32, kind="ExternalInput").ap()
           for n in ("wqT", "wkT", "wvT", "woT")}
    u1 = nc.dram_tensor("u1", [D, RANK], F32, kind="ExternalInput").ap()
    v1 = nc.dram_tensor("v1", [RANK, DFF], F32, kind="ExternalInput").ap()
    b1 = nc.dram_tensor("b1", [DFF], F32, kind="ExternalInput").ap()
    u2 = nc.dram_tensor("u2", [DFF, RANK], F32, kind="ExternalInput").ap()
    v2 = nc.dram_tensor("v2", [RANK, D], F32, kind="ExternalInput").ap()
    b2 = nc.dram_tensor("b2", [D], F32, kind="ExternalInput").ap()
    cst = nc.dram_tensor("cst", [128, 8], F32, kind="ExternalInput").ap()
    bkd = nc.dram_tensor("bk", [128, NQUAD], F32, kind="ExternalInput").ap()
    atl = nc.dram_tensor("atl", [128, 7 * 128], BF16, kind="ExternalInput").ap()
    idf_d = nc.dram_tensor("idf", [128, 128], F32, kind="ExternalInput").ap()
    idb_d = nc.dram_tensor("idb", [128, 128], BF16, kind="ExternalInput").ap()
    out_d = nc.dram_tensor("out", [NQ * 128, D], F32, kind="ExternalOutput").ap()
    if DBG:
        dbga = nc.dram_tensor("dbga", [128, 103 * 1024], BF16, kind="ExternalOutput").ap()
        dbgc = nc.dram_tensor("dbgc", [128, 103 * 1024], BF16, kind="ExternalOutput").ap()
        dbgl = nc.dram_tensor("dbgl", [128, 2 * 3 * 1024 + 32], F32, kind="ExternalOutput").ap()

    AR = 207 * KB
    arena = nc.alloc_sbuf_tensor("arena", [128, AR // 2], BF16)
    arA = arena.ap()

    def bfv(off_b, cols):
        assert off_b % 2 == 0 and off_b + 2 * cols <= AR, (off_b, cols)
        return arA[:, off_b // 2: off_b // 2 + cols]

    PSTEP = AR // 2

    def sbap(off_el, dims):
        return bass.AP(arena, off_el, [[PSTEP, 128]] + dims)

    def f32v(off_b, cols):
        assert off_b % 4 == 0 and off_b + 4 * cols <= AR, (off_b, cols)
        return arA[:, off_b // 2: off_b // 2 + 2 * cols].bitcast(F32)

    o_cst, o_bk, o_atl = 0, 64, 256
    o_idb, o_idf = 2048, 2304
    o_b1c, o_b2c, o_rec, o_ln = 2816, 2944, 2976, 3072
    oA = 3200
    oB = oA + 64 * KB
    oC = oB + 65 * KB
    oD = oC + 32 * KB
    TOT = oD + 42 * KB
    assert TOT <= AR, (TOT, AR)

    cstS = f32v(o_cst, 8)
    bkS = f32v(o_bk, NQUAD)
    atlS = bfv(o_atl, 7 * 128)
    idbS = bfv(o_idb, 128)
    idfS = f32v(o_idf, 128)
    b1cS = f32v(o_b1c, 32)
    b2cS = f32v(o_b2c, 8)
    recS = f32v(o_rec, 8)
    lnS = f32v(o_ln, 32)

    KTv = bfv(oA, 32 * KB)
    Vsb = bfv(oB, 33280)
    QTv = bfv(oC, 16 * KB)
    ESB = bfv(oD, 3 * KB)
    PVS = f32v(oD + 6 * KB, KB)
    ATOK = bfv(oD + 10 * KB, 16 * KB)
    XTOK = [f32v(oD, KB), f32v(oD + 4 * KB, KB)]
    YTOK = [f32v(oD + 10 * KB, KB), f32v(oD + 14 * KB, KB)]
    SQ = f32v(oD + 18 * KB, KB)
    SQ = [SQ, SQ]
    WOSCR = [f32v(oD + 22 * KB, KB), f32v(oD + 26 * KB, KB)]
    OTOK = [f32v(oD + 30 * KB, KB), f32v(oD + 34 * KB, KB)]
    PT = bfv(oD + 38 * KB, KB)
    Q2T = bfv(oD + 40 * KB, KB)
    ATT = bfv(oA, 16 * KB)
    WoT = bfv(oA + 32 * KB, 8 * KB)
    YAT = f32v(oA + 48 * KB, 4 * KB)
    U1B = bfv(oA, 2 * KB)
    V1B = bfv(oA + 48 * KB, 8 * KB)
    V2B = bfv(oA + 20 * KB, 2 * KB)
    U2Bb = bfv(oA + 24 * KB, 4 * KB)
    FFT = f32v(oA + 32 * KB, 4 * KB)
    HTr = bfv(oA + 4 * KB, 2 * KB)
    U2Ba = bfv(oD + 22 * KB, 4 * KB)
    PSP = 4096
    XTG = bfv(oD + 4 * KB, 8 * KB)
    WT = bfv(oD + 20 * KB, 8 * KB)

    Y1 = f32v(oB, 16 * KB)
    Y1T = bfv(oC, 16 * KB)

    ps = nc.alloc_psum_tensor("ps", [128, 4096], F32)
    psA = ps.ap()
    psB = psA.bitcast(BF16)

    sch = Sched()
    add, now = sch.add, sch.now

    for (dst, src) in ((cstS, cst[:, :]), (bkS, bkd[:, :]), (atlS, atl[:, :]),
                       (idbS, idb_d[:, :]), (idfS, idf_d[:, :]),
                       (b1cS[:, 0:32], b1.rearrange("(c p) -> p c", p=128)),
                       (b2cS[:, 0:8], b2.rearrange("(c p) -> p c", p=128))):
        add("sync", "dma", inc="c", incv=16, out=dst, in_=src, slow=True)
    t_c = now("c")

    SCRF = [f32v(oD + i * 4 * KB, KB) for i in range(2)]
    XB = [bfv(oD + 8 * KB, 8 * KB),
          bfv(oD + 24 * KB, 8 * KB)]
    WKT = bfv(oC + 16 * KB, 8 * KB)
    WVT = bfv(oC, 8 * KB)

    scrf_rel = [None, None]
    st_ld = [0]
    st_mm = [0]

    def chunk_load_cast(src_ap, dst_bf, gates_ld=(), gates_cast=()):
        slot = st_ld[0]; st_ld[0] ^= 1
        w = ([scrf_rel[slot]] if scrf_rel[slot] else []) + list(gates_ld)
        ld = add("sync", "dma", waits=w, inc="xld", incv=16,
                 out=SCRF[slot][:, :], in_=src_ap)
        ev = add("vector", "ts", waits=[ld] + list(gates_cast), inc="xev",
                 out=dst_bf, in0=SCRF[slot][:, :], s1=cstS[:, 3:4], s2=None,
                 op0=ALU.mult, op1=None)
        scrf_rel[slot] = ev
        return ev

    t_wk = t_wv = None
    for s in range(8):
        t_wk = chunk_load_cast(wts["wkT"][:, s * 1024:(s + 1) * 1024],
                               WKT[:, s * 1024:(s + 1) * 1024],
                               gates_ld=[t_c] if s < 2 else ())
    for s in range(8):
        t_wv = chunk_load_cast(wts["wvT"][:, s * 1024:(s + 1) * 1024],
                               WVT[:, s * 1024:(s + 1) * 1024])

    def xgroup_casts(src, goff, buf, free_tok):
        t = None
        for s in range(8):
            t = chunk_load_cast(
                src[:, goff + s * 1024: goff + (s + 1) * 1024],
                buf[:, s * 1024:(s + 1) * 1024],
                gates_cast=[free_tok] if (s == 0 and free_tok) else ())
        return t

    xt_done = [None, None]
    xt_done[0] = xgroup_casts(xkvT, 0, XB[0], None)
    t_ones = add("vector", "memset", inc="ones",
                 ap=sbap(oB // 2 + 64, [[1040, 32], [65, 16], [1, 1]]),
                 value=1.0)
    kbank = [None, None]
    mm_done = [None] * 4
    t_wq = None
    for g in range(4):
        if g + 1 < 4:
            xt_done[(g + 1) % 2] = xgroup_casts(
                xkvT, (g + 1) * 8192, XB[(g + 1) % 2],
                mm_done[g - 1] if g >= 1 else None)
        else:
            t_wq = xgroup_casts(wts["wqT"], 0, XB[0], mm_done[2])
        t_xt = xt_done[g % 2]
        XG = XB[g % 2]
        last = None
        for j in range(8):
            par = st_mm[0]; st_mm[0] ^= 1
            base = 2048 + par * 1024
            w = [t_xt]
            if g == 0 and j == 0:
                w.append(t_wk)
            if kbank[par]:
                w.append(kbank[par])
            for i in range(8):
                for hf in range(2):
                    last = add("tensor", "mm",
                               waits=w if (i == 0 and hf == 0) else [],
                               inc="kmm" if (i == 7 and hf == 1) else None,
                               out=psA[:, base + hf * 512: base + hf * 512 + 512],
                               lhsT=WKT[:, i * 1024 + j * 128: i * 1024 + (j + 1) * 128],
                               rhs=XG[:, i * 1024 + hf * 512: i * 1024 + hf * 512 + 512],
                               start=(i == 0), stop=(i == 7))
            ev = add("scalar", "copy", waits=[now("kmm")], inc="kev",
                     out=KTv[:, j * 4096 + g * 1024: j * 4096 + (g + 1) * 1024],
                     in_=psA[:, base: base + 1024])
            kbank[par] = ev
        for t in range(8):
            gt = g * 8 + t
            par = st_mm[0]; st_mm[0] ^= 1
            base = 2048 + par * 1024
            w = [t_xt, t_ones, t_wv] if gt == 0 else [t_xt]
            if kbank[par]:
                w.append(kbank[par])
            for i in range(8):
                for hf in range(2):
                    last = add("tensor", "mm",
                               waits=w if (i == 0 and hf == 0) else [],
                               inc="vmm" if (i == 7 and hf == 1) else None,
                               out=psA[:, base + hf * 512: base + hf * 512 + 512],
                               lhsT=XG[:, i * 1024 + t * 128: i * 1024 + (t + 1) * 128],
                               rhs=WVT[:, i * 1024 + hf * 512: i * 1024 + hf * 512 + 512],
                               start=(i == 0), stop=(i == 7))
            dstv = sbap(oB // 2 + gt * 1040, [[65, 16], [1, 64]])
            srcv = psA[:, base: base + 1024].rearrange("p (h d) -> p h d", h=16)
            ev = add("vector", "ts", waits=[now("vmm")], inc="vev",
                     out=dstv, in0=srcv, s1=cstS[:, 3:4], s2=None,
                     op0=ALU.mult, op1=None)
            kbank[par] = ev
        mm_done[g] = now("vmm")
    t_kmm_all = now("kmm")
    t_kev_all = now("kev")
    t_vmm_all = now("vmm")
    t_vev_all = now("vev")

    qx = xgroup_casts(xqT, 0, XB[1], mm_done[3])
    qbank = kbank
    for g in range(2):
        if g == 1:
            qx = xgroup_casts(xqT, 8192, XB[1], now("qmm"))
        t_xt = qx
        last = None
        for j in range(8):
            par = st_mm[0]; st_mm[0] ^= 1
            base = 2048 + par * 1024
            w = [t_xt]
            if g == 0 and j == 0:
                w.append(t_wq)
            if qbank[par]:
                w.append(qbank[par])
            for i in range(8):
                for hf in range(2):
                    last = add("tensor", "mm",
                               waits=w if (i == 0 and hf == 0) else [],
                               inc="qmm" if (i == 7 and hf == 1) else None,
                               out=psA[:, base + hf * 512: base + hf * 512 + 512],
                               lhsT=XB[0][:, i * 1024 + j * 128: i * 1024 + (j + 1) * 128],
                               rhs=XB[1][:, i * 1024 + hf * 512: i * 1024 + hf * 512 + 512],
                               start=(i == 0), stop=(i == 7))
            ev = add("scalar", "act",
                     waits=[now("qmm")] + ([t_vmm_all] if (g == 0 and j == 0) else []),
                     inc="qev",
                     out=QTv[:, j * 2048 + g * 1024: j * 2048 + (g + 1) * 1024],
                     in_=psA[:, base: base + 1024],
                     func=AF.Copy, scale=cstS[:, 0:1])
            qbank[par] = ev
    t_qev_all = now("qev")
    qbank_rel = qbank

    exp_tok = [None] * NSTEP
    pvd_tok = [None] * NSTEP
    evac_rel = None
    gstep = 0
    first_S = True
    pvset = 0
    evrel = [None, None]
    lastaev = [None]
    pend_norm = []
    for m in range(4):
        for hp in range(8):
            h0, h1 = 2 * hp, 2 * hp + 1
            j = hp
            pvbase = 2048 + pvset * 1024
            pend = []
            for kc in range(P[m]):
                t = gstep
                par = t % 2
                pk = slot2phys(m, kc)
                w = []
                if first_S:
                    w = [t_qev_all, t_kev_all, qbank_rel[0], qbank_rel[1]]
                    first_S = False
                if t >= 2:
                    w.append(("e", t - 1))
                add("tensor", "mm", waits=w, inc=None,
                    out=psA[:, par * 1024: par * 1024 + 512],
                    lhsT=KTv[0:64, j * 4096 + pk * 128: j * 4096 + (pk + 1) * 128],
                    rhs=QTv[0:64, j * 2048 + m * 512: j * 2048 + (m + 1) * 512],
                    start=True, stop=True, tile_position=(0, 0))
                add("tensor", "mm", waits=[], inc="smm",
                    out=psA[:, par * 1024 + 512: par * 1024 + 1024],
                    lhsT=KTv[64:128, j * 4096 + pk * 128: j * 4096 + (pk + 1) * 128],
                    rhs=QTv[64:128, j * 2048 + m * 512: j * 2048 + (m + 1) * 512],
                    start=True, stop=True, tile_position=(64, 0))
                par_e = t % 3
                wexp = [now("smm")]
                if t >= 3:
                    wexp.append(("pvd", t - 2))
                prq = sum(SMQ[:m]) + kc // 4
                exp_tok[t] = add("scalar", "act", waits=wexp, inc="e",
                                 out=ESB[:, par_e * 1024: par_e * 1024 + 1024],
                                 in_=psA[:, par * 1024: par * 1024 + 1024],
                                 func=AF.Exp, bias=bkS[:, prq:prq + 1])
                if kc < 4:
                    for hh in range(2):
                        add("vector", "tt",
                            waits=[exp_tok[t]] if hh == 0 else [],
                            inc="mk" if hh == 1 else None,
                            out=ESB[:, par_e * 1024 + hh * 512: par_e * 1024 + hh * 512 + 512],
                            in0=ESB[:, par_e * 1024 + hh * 512: par_e * 1024 + hh * 512 + 512],
                            in1=atlS[:, (3 - kc) * 128: (3 - kc) * 128 + 512],
                            op=ALU.mult)
                    exp_tok[t] = now("mk")
                pend.append((kc, t))
                gstep += 1
                if kc == 1 and pend_norm:
                    pend_norm.pop(0)()
                while len(pend) > 2:
                    pkc, pt = pend.pop(0)
                    ppk = slot2phys(m, pkc)
                    ppe = pt % 3
                    for hh in range(2):
                        ww = [exp_tok[pt]] if hh == 0 else []
                        if hh == 0 and pkc == 0:
                            if evrel[pvset]:
                                ww.append(evrel[pvset])
                            if m == 0 and hp == 0:
                                ww += [t_vev_all, t_ones]
                        add("tensor", "mm", waits=ww,
                            inc="pvd" if hh == 1 else None,
                            out=psA[0:65, pvbase + hh * 512: pvbase + hh * 512 + 512],
                            lhsT=Vsb[:, ppk * 1040 + 65 * (2 * hp + hh):
                                     ppk * 1040 + 65 * (2 * hp + hh) + 65],
                            rhs=ESB[:, ppe * 1024 + hh * 512: ppe * 1024 + hh * 512 + 512],
                            start=(pkc == 0), stop=(pkc == P[m] - 1))
                    pvd_tok[pt] = now("pvd")
            for (pkc, pt) in pend:
                ppk = slot2phys(m, pkc)
                ppe = pt % 3
                for hh in range(2):
                    ww = [exp_tok[pt]] if hh == 0 else []
                    if hh == 0 and pkc == 0:
                        if evrel[pvset]:
                            ww.append(evrel[pvset])
                        if m == 0 and hp == 0:
                            ww += [t_vev_all, t_ones]
                    add("tensor", "mm", waits=ww,
                        inc="pvd" if hh == 1 else None,
                        out=psA[0:65, pvbase + hh * 512: pvbase + hh * 512 + 512],
                        lhsT=Vsb[:, ppk * 1040 + 65 * (2 * hp + hh):
                                 ppk * 1040 + 65 * (2 * hp + hh) + 65],
                        rhs=ESB[:, ppe * 1024 + hh * 512: ppe * 1024 + hh * 512 + 512],
                        start=(pkc == 0), stop=(pkc == P[m] - 1))
                pvd_tok[pt] = now("pvd")
            cp0 = add("vector", "copy", waits=[pvd_tok[gstep - 1]], inc="dve",
                      out=PVS[0:65, 0:512], in_=psA[0:65, pvbase: pvbase + 512])
            cp1 = add("vector", "copy", inc="dve",
                      out=PVS[0:65, 512:1024],
                      in_=psA[0:65, pvbase + 512: pvbase + 1024])
            def _norm(m=m, hp=hp, pvbase=pvbase, cp1=cp1, pvset=pvset):
                trn = None
                for hh in range(2):
                    for tau in range(4):
                        ww = [cp1] if (hh == 0 and tau == 0) else []
                        trn = add("tensor", "tr", waits=ww, inc="ntr",
                                  out=psA[:, pvbase + hh * 512 + tau * 65:
                                          pvbase + hh * 512 + tau * 65 + 65],
                                  in_=PVS[0:65, hh * 512 + tau * 128:
                                          hh * 512 + (tau + 1) * 128],
                                  identity=idfS[0:65, 0:65])
                sums_ap = bass.AP(ps, pvbase + 64,
                                  [[PSP, 128], [512, 2], [65, 4], [1, 1]])
                rec_ap = recS[:, 0:8].rearrange("p (a r o) -> p a r o",
                                                a=2, o=1)
                rc = add("vector", "recip", waits=[trn], inc="dve",
                         out=rec_ap, in_=sums_ap)
                add("vector", "wait", waits=[rc])
                for hh in range(2):
                    for tau in range(4):
                        add("vector", "ts",
                            inc="aev" if (hh == 1 and tau == 3) else None,
                            out=ATOK[:, (4 * m + tau) * 1024
                                     + 64 * (2 * hp + hh):
                                     (4 * m + tau) * 1024
                                     + 64 * (2 * hp + hh) + 64],
                            in0=psA[:, pvbase + hh * 512 + tau * 65:
                                    pvbase + hh * 512 + tau * 65 + 64],
                            s1=recS[:, hh * 4 + tau: hh * 4 + tau + 1],
                            s2=None, op0=ALU.mult, op1=None)
                rel = now("aev")
                evrel[pvset] = rel
                lastaev[0] = rel
            pend_norm.append(_norm)
            pvset ^= 1
    while pend_norm:
        pend_norm.pop(0)()
    evac_rel = lastaev[0]
    t_exp_all = now("e")
    t_aev_all = now("aev")
    t_smm_all = now("smm")
    t_dbga = None
    if DBG:
        t_dbga = add("sync", "dma", waits=[t_aev_all], inc="dba", incv=16,
                     out=dbga[:, :], in_=arA[:, 0:103 * 1024])

    tr4_rel = [None, None]
    atev_tok = []
    for t in range(NQ):
        st_ = t % 2
        w = [t_aev_all, t_exp_all] if t == 0 else []
        if tr4_rel[st_]:
            w = w + [tr4_rel[st_]]
        trl = None
        for cch in range(8):
            trl = add("tensor", "tr", waits=w if cch == 0 else [], inc="tr4",
                      out=psB[:, st_ * 2048 + cch * 128: st_ * 2048 + (cch + 1) * 128],
                      in_=ATOK[:, t * 1024 + cch * 128: t * 1024 + (cch + 1) * 128],
                      identity=idbS[:, :])
        dst = sbap(oA // 2 + t * 128, [[2048, 8], [1, 128]])
        srcap = psB[:, st_ * 2048: st_ * 2048 + 1024].rearrange(
            "p (c q) -> p c q", c=8)
        ev = add("scalar", "copy",
                 waits=[trl, t_smm_all if t == 0 else None,
                        t_dbga if (DBG and t == 0) else None],
                 inc="atev", out=dst, in_=srcap)
        tr4_rel[st_] = ev
        atev_tok.append(ev)
    t_atev_all = now("atev")
    for i in range(16):
        add("gpsimd", "dma", waits=[t_atev_all] if i == 0 else [],
            inc="fw", incv=16,
            out=U2Ba[:, i * 256:(i + 1) * 256],
            in_=u2[i * 128:(i + 1) * 128, :])
    t_u2a = now("fw")

    WT_PH4 = WoT
    t_wo = None
    for j in range(8):
        t_wo = add("gpsimd", "dma", waits=[t_smm_all] if j == 0 else [],
                   inc="wol", incv=16,
                   out=WoT[:, j * 1024:(j + 1) * 1024],
                   in_=wts["woT"][:, j * 1024:(j + 1) * 1024])
    t_wo = now("wol")
    for r in range(2):
        add("gpsimd", "dma", inc="fw", incv=16,
            out=V1B[:, r * 4096:(r + 1) * 4096],
            in_=v1[r * 128:(r + 1) * 128, :])
    t_v1 = now("fw")

    ffn_gate = None
    ffn_loads = []

    def _lnslots(par):
        b = par * 16
        return [lnS[:, b + i: b + i + 1] for i in range(9)]

    def ln_chain(par, resid_in1, apply_out, extra_w=()):
        (ln_ysum, ln_qsum, ln_nm, ln_ex2, ln_m2, ln_vpe, ln_rv, ln_inv,
         ln_nmi) = _lnslots(par)
        t1 = add("vector", "stt", waits=list(extra_w), inc="dve",
                 out=YTOK[par][:, :],
                 in0=psA[:, 2048 + par * 1024: 2048 + par * 1024 + 1024],
                 s1=cstS[:, 3:4], in1=resid_in1, op0=ALU.mult, op1=ALU.add,
                 accum_out=ln_ysum)
        t2 = add("vector", "stt", waits=[t1], inc="dve", out=SQ[par][:, :],
                 in0=YTOK[par][:, :], s1=cstS[:, 3:4], in1=YTOK[par][:, :],
                 op0=ALU.mult, op1=ALU.mult, accum_out=ln_qsum)
        e = add("vector", "ts", waits=[t2], inc="dve", out=ln_nm, in0=ln_ysum,
                s1=cstS[:, 4:5], s2=None, op0=ALU.mult, op1=None)
        e2 = add("vector", "ts", inc="dve", out=ln_ex2, in0=ln_qsum,
                 s1=cstS[:, 1:2], s2=None, op0=ALU.mult, op1=None)
        add("vector", "wait", waits=[e2])
        e = add("vector", "tt", inc="dve", out=ln_m2, in0=ln_nm, in1=ln_nm,
                op=ALU.mult)
        add("vector", "wait", waits=[e])
        e = add("vector", "tt", inc="dve", out=ln_vpe, in0=ln_ex2, in1=ln_m2,
                op=ALU.subtract)
        add("vector", "wait", waits=[e])
        e = add("vector", "ts", inc="dve", out=ln_vpe, in0=ln_vpe,
                s1=cstS[:, 2:3], s2=None, op0=ALU.add, op1=None)
        add("vector", "wait", waits=[e])
        t3 = add("vector", "recip", inc="dve", out=ln_rv, in_=ln_vpe)
        t4 = add("scalar", "act", waits=[t3], inc="sq", out=ln_inv, in_=ln_rv,
                 func=AF.Sqrt)
        t5 = add("vector", "tt", waits=[t4], inc="dve", out=ln_nmi, in0=ln_nm,
                 in1=ln_inv, op=ALU.mult)
        add("vector", "wait", waits=[t5])
        ap1 = add("vector", "ts", inc="dve", out=apply_out, in0=YTOK[par][:, :],
                  s1=ln_inv, s2=ln_nmi, op0=ALU.mult, op1=ALU.add)
        return t1, ap1

    mm_rel = [None, None]
    xtok_rel = [None, None]
    ytok_rel = [None, None]
    y1_state = {"rel": None}
    pend_y1 = []
    ap1_tok = [None] * NQ

    def emit_y1t(tloc):
        trl2 = None
        for cch in range(8):
            w2 = []
            if cch == 0:
                w2 = [ap1_tok[tloc]]
                if DBG and tloc < 2:
                    w2.append(sch.dbg_gate)
                if y1_state["rel"]:
                    w2.append(y1_state["rel"])
                elif tloc == 0:
                    w2.append(tr4_rel[1])
            trl2 = add("tensor", "tr", waits=w2, inc="y1tr",
                       out=psA[:, 1024 + cch * 128: 1024 + (cch + 1) * 128],
                       in_=Y1[:, tloc * 1024 + cch * 128:
                              tloc * 1024 + (cch + 1) * 128],
                       identity=idfS[:, :])
        y1_state["rel"] = add(
            "scalar", "copy", waits=[trl2], inc="y1t",
            out=sbap(oC // 2 + tloc * 128, [[2048, 8], [1, 128]]),
            in_=psA[:, 1024:2048].rearrange("p (c q) -> p c q", c=8))

    xr_tok = [None] * NQ
    last_womm = None
    for tloc in range(NQ):
        par = tloc % 2
        base = 2048 + par * 1024
        w = [t_atev_all] if tloc < 2 else []
        if xtok_rel[par]:
            w.append(xtok_rel[par])
        xr_tok[tloc] = add("sync", "dma", waits=w, inc="xr", incv=16,
                           out=XTOK[par][:, :],
                           in_=xq[tloc * 128:(tloc + 1) * 128, :])
        w0 = [atev_tok[tloc]]
        if tloc == 0:
            w0 += [t_wo, evac_rel]
        if mm_rel[par]:
            w0.append(mm_rel[par])
        for i in range(8):
            for hf in range(2):
                last_womm = add(
                    "tensor", "mm", waits=w0 if (i == 0 and hf == 0) else [],
                    inc="womm" if (i == 7 and hf == 1) else None,
                    out=psA[:, base + hf * 512: base + hf * 512 + 512],
                    lhsT=ATT[:, i * 2048 + tloc * 128:
                             i * 2048 + (tloc + 1) * 128],
                    rhs=WoT[:, i * 1024 + hf * 512: i * 1024 + hf * 512 + 512],
                    start=(i == 0), stop=(i == 7))
        t_womm = now("womm")
        while len(pend_y1) >= 2:
            emit_y1t(pend_y1.pop(0))
        extra = [t_womm, xr_tok[tloc]]
        if ytok_rel[par]:
            extra.append(ytok_rel[par])
        t1, ap1 = ln_chain(par, XTOK[par][:, :],
                           Y1[:, tloc * 1024:(tloc + 1) * 1024], extra)
        mm_rel[par] = t1
        xtok_rel[par] = t1
        if DBG and tloc < 2:
            off = tloc * (3 * 1024 + 16)
            dsn = add("sync", "dma", waits=[ap1], inc="dbl", incv=16,
                      out=dbgl[:, off:off + 16], in_=lnS[:, 0:16])
            dsn = add("sync", "dma", inc="dbl", incv=16,
                      out=dbgl[:, off + 16:off + 1040], in_=XTOK[par][:, :])
            dsn = add("sync", "dma", inc="dbl", incv=16,
                      out=dbgl[:, off + 1040:off + 2064], in_=YTOK[par][:, :])
            dsn = add("sync", "dma", inc="dbl", incv=16,
                      out=dbgl[:, off + 2064:off + 3088],
                      in_=Y1[:, tloc * 1024:(tloc + 1) * 1024])
            sch.dbg_gate = dsn
        ytok_rel[par] = ap1
        ap1_tok[tloc] = ap1
        pend_y1.append(tloc)
    while pend_y1:
        emit_y1t(pend_y1.pop(0))
    t_y1t = now("y1t")
    t_womm_all = now("womm")

    t_dbgc = None
    if DBG:
        t_dbgc = add("sync", "dma", waits=[t_womm_all,
                                           now("y1t")],
                     inc="dbc", incv=16, out=dbgc[:, :], in_=arA[:, 0:103 * 1024])
    w = [t_womm_all]
    if DBG:
        w.append(t_dbgc)
    for i in range(8):
        add("gpsimd", "dma", waits=w if i == 0 else [], inc="fw",
            incv=16, out=U1B[:, i * 256:(i + 1) * 256],
            in_=u1[i * 128:(i + 1) * 128, :])
    t_u1 = now("fw")
    for r in range(2):
        add("gpsimd", "dma", inc="fw", incv=16,
            out=V2B[:, r * 1024:(r + 1) * 1024],
            in_=v2[r * 128:(r + 1) * 128, :])
    for i in range(16, 32):
        add("gpsimd", "dma", inc="fw", incv=16,
            out=U2Bb[:, (i % 16) * 256:(i % 16 + 1) * 256],
            in_=u2[i * 128:(i + 1) * 128, :])
    t_fw = now("fw")

    def u2slice(fc, r):
        t_ = U2Ba if fc < 16 else U2Bb
        return t_[:, (fc % 16) * 256 + r * 128: (fc % 16) * 256 + (r + 1) * 128]

    pbank_rel = [None, None]
    hbank_rel = [None, None]
    t3_rel = [None, None]
    str_rel = [None, None]
    pend_e = []

    def emit_tau_block(g4, t_ff):
        for tau in range(4):
            tloc = g4 * 4 + tau
            par = tloc % 2
            w = [t_ff]
            if t3_rel[par]:
                w.append(t3_rel[par])
            elif g4 == 0:
                w.append(mm_rel[par])
            trl = None
            for cch in range(8):
                trl = add("tensor", "tr", waits=w if cch == 0 else [],
                          inc="t3",
                          out=psA[:, 2048 + par * 1024 + cch * 128:
                                  2048 + par * 1024 + (cch + 1) * 128],
                          in_=FFT[:, cch * 512 + tau * 128:
                                  cch * 512 + (tau + 1) * 128],
                          identity=idfS[:, :])
            extra = [trl]
            if ytok_rel[par]:
                extra.append(ytok_rel[par])
            t1, ap1 = ln_chain(par, Y1[:, tloc * 1024:(tloc + 1) * 1024],
                               OTOK[par][:, :],
                               extra + ([str_rel[par]] if str_rel[par]
                                        else []))
            t3_rel[par] = t1
            ytok_rel[par] = ap1
            st_tok = add("sync", "dma", waits=[ap1], inc="st", incv=16,
                         out=out_d[tloc * 128:(tloc + 1) * 128, :],
                         in_=OTOK[par][:, :])
            str_rel[par] = st_tok

    for g4 in range(4):
        for r in range(2):
            pset = r % 2
            w = [t_u1, t_y1t] if (g4 == 0 and r == 0) else []
            if pbank_rel[pset]:
                w.append(pbank_rel[pset])
            for i in range(8):
                add("tensor", "mm", waits=w if i == 0 else [],
                    inc="pmm" if i == 7 else None,
                    out=psA[:, pset * 512: pset * 512 + 512],
                    lhsT=U1B[:, i * 256 + r * 128: i * 256 + (r + 1) * 128],
                    rhs=Y1T[:, i * 2048 + g4 * 512: i * 2048 + (g4 + 1) * 512],
                    start=(i == 0), stop=(i == 7))
            ev = add("scalar", "copy", waits=[now("pmm")], inc="pev",
                     out=PT[:, r * 512:(r + 1) * 512],
                     in_=psA[:, pset * 512: pset * 512 + 512])
            pbank_rel[pset] = ev
        t_pt = now("pev")
        while pend_e:
            emit_tau_block(*pend_e.pop(0))
        ht_rel = [None] * 4
        pend_q2 = None
        for fc in range(32):
            hset = fc % 2
            w = [t_pt] if fc < 2 else []
            if g4 == 0 and fc == 0:
                w.append(t_v1)
            if hbank_rel[hset]:
                w.append(hbank_rel[hset])
            for r in range(2):
                add("tensor", "mm", waits=w if r == 0 else [],
                    inc="hmm" if r == 1 else None,
                    out=psA[:, (2 + hset) * 512: (2 + hset) * 512 + 512],
                    lhsT=V1B[:, r * 4096 + fc * 128: r * 4096 + (fc + 1) * 128],
                    rhs=PT[:, r * 512:(r + 1) * 512],
                    start=(r == 0), stop=(r == 1))
            wg = [now("hmm")]
            if ht_rel[fc % 4]:
                wg.append(ht_rel[fc % 4])
            ev = add("scalar", "act", waits=wg, inc="hev",
                     out=HTr[:, (fc % 4) * 512: (fc % 4 + 1) * 512],
                     in_=psA[:, (2 + hset) * 512: (2 + hset) * 512 + 512],
                     func=AF.Gelu_apprx_tanh, bias=b1cS[:, fc:fc + 1],
                     scale=cstS[:, 3:4])
            hbank_rel[hset] = ev
            def _q2(fcq, evq):
                wq2 = [evq]
                if fcq == 0:
                    wq2.append(t_u2a)
                    if pbank_rel[0]:
                        wq2.append(pbank_rel[0])
                    if pbank_rel[1]:
                        wq2.append(pbank_rel[1])
                elif fcq == 16 and g4 == 0:
                    wq2.append(t_fw)
                lastq = None
                for r in range(2):
                    lastq = add("tensor", "mm", waits=wq2 if r == 0 else [],
                                inc="q2p" if r == 1 else None,
                                out=psA[:, r * 512: r * 512 + 512],
                                lhsT=u2slice(fcq, r),
                                rhs=HTr[:, (fcq % 4) * 512: (fcq % 4 + 1) * 512],
                                start=(fcq == 0), stop=(fcq == 31))
                ht_rel[fcq % 4] = lastq
            if pend_q2 is not None:
                _q2(*pend_q2)
            pend_q2 = (fc, ev)
        _q2(*pend_q2)
        for r in range(2):
            ev = add("scalar", "copy", waits=[now("q2p")], inc="q2ev",
                     out=Q2T[:, r * 512:(r + 1) * 512],
                     in_=psA[:, r * 512: r * 512 + 512])
            pbank_rel[r] = ev
        t_q2 = now("q2ev")
        for jj in range(8):
            oset = jj % 2
            w = [t_q2]
            if hbank_rel[oset]:
                w.append(hbank_rel[oset])
            for r in range(2):
                add("tensor", "mm", waits=w if r == 0 else [],
                    inc="omm" if r == 1 else None,
                    out=psA[:, (2 + oset) * 512: (2 + oset) * 512 + 512],
                    lhsT=V2B[:, r * 1024 + jj * 128: r * 1024 + (jj + 1) * 128],
                    rhs=Q2T[:, r * 512:(r + 1) * 512],
                    start=(r == 0), stop=(r == 1))
            ev = add("vector", "ts",
                     waits=[now("omm")] + ([("t3", 32 * g4)] if g4 > 0 else []),
                     inc="oev",
                     out=FFT[:, jj * 512:(jj + 1) * 512],
                     in0=psA[:, (2 + oset) * 512: (2 + oset) * 512 + 512],
                     s1=b2cS[:, jj:jj + 1], s2=None, op0=ALU.add, op1=None)
            hbank_rel[oset] = ev
        t_ff = now("oev")
        pend_e.append((g4, t_ff))
    while pend_e:
        emit_tau_block(*pend_e.pop(0))
    add("sync", "wait", waits=[now("st")])

    semnames = sorted(sch.cnt.keys())
    sems = {}
    for n in semnames:
        g = nc.semaphore(n)
        sems[n] = g.__enter__()
        nc._sem_guards = getattr(nc, "_sem_guards", [])
        nc._sem_guards.append(g)

    def run_ops(eng, name):
        waited = {}
        for op in sch.ops[name]:
            for (sname, v) in op["waits"]:
                if v > waited.get(sname, 0):
                    eng.wait_ge(sems[sname], v)
                    waited[sname] = v
            k, kw = op["kind"], op["kw"]
            if k == "wait":
                continue
            if k == "dma":
                ins = eng.dma_start(out=kw["out"], in_=kw["in_"],
                                    allow_slow_non_contiguous=kw.get("slow", False))
            elif k == "dmaT":
                ins = eng.dma_start(out=kw["out"], in_=kw["in_"], transpose=True)
            elif k == "mm":
                ins = eng.matmul(kw["out"], kw["lhsT"], kw["rhs"],
                                 start=kw["start"], stop=kw["stop"])
            elif k == "tr":
                ins = eng.transpose(kw["out"], kw["in_"], kw["identity"])
            elif k == "act":
                a = {}
                for f in ("bias", "scale", "accum_out"):
                    if f in kw and kw[f] is not None:
                        a[f] = kw[f]
                ins = eng.activation(kw["out"], kw["in_"], kw["func"], **a)
            elif k == "copy":
                ins = (eng.copy(kw["out"], kw["in_"]) if name == "scalar"
                       else eng.tensor_copy(kw["out"], kw["in_"]))
            elif k == "tt":
                ins = eng.tensor_tensor(kw["out"], kw["in0"], kw["in1"], kw["op"])
            elif k == "ts":
                if kw.get("s2") is None:
                    ins = eng.tensor_scalar(kw["out"], kw["in0"], kw["s1"],
                                            None, kw["op0"])
                else:
                    ins = eng.tensor_scalar(kw["out"], kw["in0"], kw["s1"],
                                            kw["s2"], kw["op0"], kw["op1"])
            elif k == "stt":
                ins = eng.scalar_tensor_tensor(kw["out"], kw["in0"], kw["s1"],
                                               kw["in1"], kw["op0"], kw["op1"],
                                               accum_out=kw.get("accum_out"))
            elif k == "recip":
                ins = eng.reciprocal(kw["out"], kw["in_"])
            elif k == "memset":
                ins = eng.memset(kw["ap"], kw["value"])
            else:
                raise ValueError(k)
            if op["inc"] is not None:
                ins.then_inc(sems[op["inc"]], op["incv"])

    with nc.Block() as block:
        @block.sync
        def _(e):
            run_ops(e, "sync")

        @block.gpsimd
        def _(e):
            run_ops(e, "gpsimd")

        @block.tensor
        def _(e):
            run_ops(e, "tensor")

        @block.vector
        def _(e):
            run_ops(e, "vector")

        @block.scalar
        def _(e):
            run_ops(e, "scalar")

    return nc


def emit_pv(m, h, pend, exp_tok, add, now, psA, ESB, Vsb, evac_rel, v_gate):
    _, _, s, gs = pend
    par_e = gs % 4
    gate = exp_tok[gs]
    for li in range(2):
        l = 2 * s + li
        pk = slot2phys(m, l)
        for tau in range(4):
            w = []
            if li == 0 and tau == 0:
                w.append(gate)
                if evac_rel:
                    w.append(evac_rel)
                if v_gate:
                    w.extend(v_gate)
            add("tensor", "mm", waits=w,
                inc="pvd" if (li == 1 and tau == 3) else None,
                out=psA[:, 2048 + tau * 512: 2048 + tau * 512 + 65],
                lhsT=ESB[:, par_e * 1024 + li * 512 + tau * 128:
                         par_e * 1024 + li * 512 + (tau + 1) * 128],
                rhs=Vsb[:, pk * 1040 + 65 * h: pk * 1040 + 65 * h + 65],
                start=(l == 0), stop=(l == P[m] - 1))


_CACHE = {}


def kernel(L0_attn_in_features, L0_attn_Wq, L0_attn_Wk, L0_attn_Wv, L0_attn_Wo,
           L0_ffn_U1, L0_ffn_V1, L0_ffn_b1, L0_ffn_U2, L0_ffn_V2, L0_ffn_b2):
    x = np.ascontiguousarray(np.asarray(L0_attn_in_features, dtype=np.float32))
    Ws = {k: np.ascontiguousarray(np.asarray(v, dtype=np.float32))
          for k, v in (("u1", L0_ffn_U1), ("v1", L0_ffn_V1),
                       ("u2", L0_ffn_U2), ("v2", L0_ffn_V2))}

    def _wT(w):
        w = np.asarray(w, dtype=np.float32)
        return np.ascontiguousarray(
            w.reshape(8, 128, 8, 128).transpose(3, 2, 0, 1).reshape(128, 8192))

    def _xT(rows):
        n = rows.shape[0] // 1024
        return np.ascontiguousarray(np.concatenate(
            [_wT(rows[g * 1024:(g + 1) * 1024]) for g in range(n)], axis=1))

    WsT = {"wqT": _wT(L0_attn_Wq), "wkT": _wT(L0_attn_Wk),
           "wvT": _wT(L0_attn_Wv), "woT": _wT(L0_attn_Wo)}
    b1v = np.ascontiguousarray(np.asarray(L0_ffn_b1, dtype=np.float32))
    b2v = np.ascontiguousarray(np.asarray(L0_ffn_b2, dtype=np.float32))

    cstv = np.zeros((128, 8), np.float32)
    cstv[:, 0] = 0.125
    cstv[:, 1] = 1.0 / 1024.0
    cstv[:, 2] = 1.0e-5
    cstv[:, 3] = 1.0
    cstv[:, 4] = -1.0 / 1024.0
    atlv = np.zeros((128, 7 * 128), np.float32)
    kk, qq = np.meshgrid(np.arange(128), np.arange(128), indexing="ij")
    atlv[:, 3 * 128:4 * 128] = (kk <= qq).astype(np.float32)
    atlv[:, 4 * 128:] = 1.0
    atlv = atlv.astype(bf16)
    idfv = np.eye(128, dtype=np.float32)
    idbv = idfv.astype(bf16)

    in_maps, meta = [], []
    for c in range(8):
        seq, role = c // 2, c % 2
        phys, _N = phys_order(role)
        qtiles = [g0 + t for g0 in GQ_ROLE[role] for t in range(4)]
        xs = x[seq]
        xkv_rows = np.concatenate(
            [xs[max(g, 0) * 128: max(g, 0) * 128 + 128] for g in phys], axis=0)
        xq_rows = np.concatenate(
            [xs[g * 128:(g + 1) * 128] for g in qtiles], axis=0)
        live = kill_quads(role)
        bkv = np.full((128, NQUAD), EXPBIAS, np.float32)
        bkv[:, ~live] = KILL
        im = {"xkvT": _xT(xkv_rows), "xqT": _xT(xq_rows),
              "xq": np.ascontiguousarray(xq_rows),
              "b1": b1v, "b2": b2v, "cst": cstv, "bk": bkv, "atl": atlv,
              "idf": idfv, "idb": idbv}
        im.update(Ws)
        im.update(WsT)
        in_maps.append(im)
        meta.append((seq, qtiles))

    if "nc" not in _CACHE:
        _CACHE["nc"] = build_program()
    res = run_bass_kernel_spmd(_CACHE["nc"], in_maps, core_ids=list(range(8)))

    out = np.empty((B, S, D), np.float32)
    for c in range(8):
        seq, qtiles = meta[c]
        oc = np.asarray(res.results[c]["out"])
        for li, g in enumerate(qtiles):
            out[seq, g * 128:(g + 1) * 128] = oc[li * 128:(li + 1) * 128]
    return out


if __name__ == "__main__":
    nc = build_program()
    n = sum(len(v) for v in [])
    print("build ok")
